# version 20 (final)
# baseline (speedup 1.0000x reference)
import numpy as np
from contextlib import ExitStack

import concourse.bass as bass
import concourse.mybir as mybir
from concourse.bass_utils import run_bass_kernel_spmd

F32 = mybir.dt.float32
BF16 = mybir.dt.bfloat16
I32 = mybir.dt.int32
AF = mybir.ActivationFunctionType
AX = mybir.AxisListType
OP = mybir.AluOpType

ENGS = ("sync", "gpsimd", "scalar", "vector", "tensor")


class Plan:

    def __init__(self, nc, es):
        self.nc = nc
        self.es = es
        self.ops = {e: [] for e in ENGS}
        self.cnt = {}
        self.sems = {}

    def sem(self, name):
        if name not in self.sems:
            self.sems[name] = self.es.enter_context(self.nc.semaphore(name))
            self.cnt[name] = 0
        return self.sems[name]

    def op(self, eng, fn, waits=(), inc=None, dma=False):
        tok = None
        extra = ()
        if inc is not None:
            self.sem(inc)
            if dma and self.cnt[inc] > 0:
                extra = ((inc, self.cnt[inc]),)
            self.cnt[inc] += 16 if dma else 1
            tok = (inc, self.cnt[inc])
        ws = tuple(t for t in waits if t is not None) + extra
        self.ops[eng].append((fn, ws, inc, 16 if dma else 1))
        return tok

    def now(self, *names):
        return [(n, self.cnt[n]) for n in names if self.cnt.get(n, 0) > 0]

    def play(self, engname, eng):
        seen = {}
        for fn, waits, incname, n in self.ops[engname]:
            for (s, v) in waits:
                if v > 0 and seen.get(s, 0) < v:
                    eng.wait_ge(self.sems[s], v)
                    seen[s] = v
            ins = fn(eng)
            if incname is not None:
                ins.then_inc(self.sems[incname], n)


def build_program(B=2, S=2048, D=1024, H=3072):
    nc = bass.Bass("TRN2", target_bir_lowering=False)
    DC = D // 128
    HC = H // 128
    NQT = S // 128
    NG = B * NQT
    TA = min(512, S)
    NSUB = TA // 128
    NST = (B * S) // TA
    NKB = max(S // 512, 1)
    KW = min(512, S)
    NVC = max(S // 512, 1)
    NH = D // 512
    NR = max(S // 1024, 1)
    NTR = min(S // 128, 8)
    eps = 1e-5
    isqd = float(1.0 / np.sqrt(np.float32(D)))

    x_h = nc.dram_tensor("x", [B, S, D], F32, kind="ExternalInput").ap()
    wqkv_h = nc.dram_tensor("wqkv", [D, 3 * D], F32, kind="ExternalInput").ap()
    bqkv_h = nc.dram_tensor("bqkv", [3 * D], F32, kind="ExternalInput").ap()
    wproj_h = nc.dram_tensor("wproj", [D, D], F32, kind="ExternalInput").ap()
    bproj_h = nc.dram_tensor("bproj", [D], F32, kind="ExternalInput").ap()
    gamma_h = nc.dram_tensor("gamma", [1, 1], F32, kind="ExternalInput").ap()
    wgT_h = nc.dram_tensor("wgT", [D, H], I32, kind="ExternalInput").ap()
    wuT_h = nc.dram_tensor("wuT", [D, H], I32, kind="ExternalInput").ap()
    wdT_h = nc.dram_tensor("wdT", [H, D], I32, kind="ExternalInput").ap()
    y_h = nc.dram_tensor("y", [B, S, D], F32, kind="ExternalOutput").ap()

    q_scr = nc.dram_tensor("q_scr", [B, 2, DC, 128, S], BF16, kind="Internal").ap()
    k_scr = nc.dram_tensor("k_scr", [B, 2, DC, 128, S], BF16, kind="Internal").ap()
    v_scr = nc.dram_tensor("v_scr", [B, 2, NQT, 128, D], BF16, kind="Internal").ap()
    x1_scr = nc.dram_tensor("x1_scr", [B * S, D], F32, kind="Internal").ap()
    wg_scr = nc.dram_tensor("wg_scr", [HC, 128, DC, 128], BF16, kind="Internal").ap()
    wu_scr = nc.dram_tensor("wu_scr", [HC, 128, DC, 128], BF16, kind="Internal").ap()
    wp_scr = nc.dram_tensor("wp_scr", [2, DC, 128, D], BF16, kind="Internal").ap()

    es = ExitStack()
    with es:
        def sb(name, shape, dtype):
            return nc.alloc_sbuf_tensor(name, shape, dtype).ap()

        ident = sb("ident", [128, 128], F32)
        maskt = sb("maskt", [128, 128], F32)
        bv_b = sb("bv_b", [128, D], F32)
        bp_b = sb("bp_b", [128, D], F32)
        gam = sb("gam", [1, 1], F32)
        lg1 = sb("lg1", [1, 1], F32)
        lgb = sb("lgb", [128, 1], F32)
        sumA_v = sb("sumA_v", [128, NSUB], F32)
        sumA_s = sb("sumA_s", [128, NSUB], F32)
        stA = sb("stA", [128, 4 * NSUB], F32)
        stA_s = sb("stA_s", [128, NSUB], F32)
        sum2_v = sb("sum2_v", [128, NG], F32)
        sum2_s = sb("sum2_s", [128, NG], F32)
        st2 = sb("st2", [128, 3 * NG], F32)
        st2_s = sb("st2_s", [128, NG], F32)
        inv2 = sb("inv2", [128, NG], F32)
        ngm2 = sb("ngm2", [128, NG], F32)
        mx_b = sb("mx_b", [128, 2], F32)
        ell = sb("ell", [128, 1], F32)
        invl = sb("invl", [128, 1], F32)

        rem = nc.sbuf_bytes_remaining
        got = nc.bump_sbuf(rem - 64)
        assert got is not None
        arena, arena_hi = got

        starts = {}

        def at(name, cols, dtype, off):
            a = (off + 31) // 32 * 32
            t = nc.alloc_sbuf_tensor_at(name, [128, cols], dtype,
                                        offset=arena + a)
            starts[name] = a
            return t.ap(), a + cols * mybir.dt.size(dtype)

        o = 0
        wq_hi, o = at("wq_hi", DC * 3 * D, BF16, o)
        wq_lo, o = at("wq_lo", DC * 3 * D, BF16, o)
        xsA, o = at("xsA", NSUB * D, F32, o)
        xnA, o = at("xnA", D, F32, o)
        xnf_hi, o = at("xnf_hi", 2 * DC * TA, BF16, o)
        xnf_lo, o = at("xnf_lo", 2 * DC * TA, BF16, o)
        qf_hi, o = at("qf_hi", DC * TA, BF16, o)
        qf_lo, o = at("qf_lo", DC * TA, BF16, o)
        kf_hi, o = at("kf_hi", DC * TA, BF16, o)
        kf_lo, o = at("kf_lo", DC * TA, BF16, o)
        vt_hi, o = at("vt_hi", 2 * D, BF16, o)
        vt_lo, o = at("vt_lo", 2 * D, BF16, o)
        brow, o = at("brow", D, F32, o)
        brow = brow[0:1, :]
        ones1, o = at("ones1", 128, F32, o)
        ones1 = ones1[0:1, :]
        mask0, o = at("mask0", 128, F32, o)
        bq_sb, o = at("bq_sb", 16, F32, o)
        if 2 * DC * TA * 2 >= 3 * D * 4:
            wtmp, _ = at("wtmp", 3 * D, F32, starts["qf_hi"])
            wtmp2, _ = at("wtmp2", 3 * D, F32, starts["kf_hi"])
            a_end = o
        else:
            wtmp, o = at("wtmp", 3 * D, F32, o)
            wtmp2, o = at("wtmp2", 3 * D, F32, o)
            a_end = o
        o = 0
        qt_hl, o = at("qt_hl", 2 * D, BF16, o)
        qt_hi = qt_hl[:, 0:D]
        qt_lo = qt_hl[:, D:2 * D]
        k_hi, o = at("k_hi", DC * S, BF16, o)
        k_lo, o = at("k_lo", DC * S, BF16, o)
        v_hi, o = at("v_hi", NQT * D, BF16, o)
        v_lo, o = at("v_lo", NQT * D, BF16, o)
        wp_hi, o = at("wp_hi", DC * D, BF16, o)
        wp_lo, o = at("wp_lo", DC * D, BF16, o)
        a32, o = at("a32", S, F32, o)
        at_hi, o = at("at_hi", S, BF16, o)
        at_lo, o = at("at_lo", S, BF16, o)
        if S >= D:
            ao32, _ = at("ao32", D, F32, starts["a32"])
        else:
            ao32, o = at("ao32", D, F32, o)
        aof_hi, o = at("aof_hi", D, BF16, o)
        aof_lo, o = at("aof_lo", D, BF16, o)
        xres, o = at("xres", D, F32, o)
        x1t, o = at("x1t", D, F32, o)
        b_end = o
        o = 0
        wd_sb, o = at("wd_sb", HC * D, BF16, o)
        wgbuf, o = at("wgbuf", 2 * D, BF16, o)
        wubuf, o = at("wubuf", 2 * D, BF16, o)
        x1A, o = at("x1A", 2 * NSUB * D, F32, o)
        xn2, o = at("xn2", D, F32, o)
        xn2f_hi, o = at("xn2f_hi", 2 * DC * TA, BF16, o)
        xn2f_lo, o = at("xn2f_lo", 2 * DC * TA, BF16, o)
        sgb, o = at("sgb", TA, F32, o)
        h32, o = at("h32", TA, F32, o)
        hf_hi, o = at("hf_hi", HC * TA, BF16, o)
        hf_lo, o = at("hf_lo", HC * TA, BF16, o)
        outA, o = at("outA", 2 * D, F32, o)
        c_end = o
        sqA, sq_end = at("sqA", 1024, F32, max(a_end, b_end, c_end))
        assert sq_end <= arena_hi - arena, (
            a_end, b_end, c_end, sq_end, arena_hi - arena)

        ps0 = es.enter_context(nc.psum_tensor("ps0", [128, 2048], F32)).ap()
        ps1 = es.enter_context(nc.psum_tensor("ps1", [128, 1024], F32)).ap()
        ps2 = es.enter_context(nc.psum_tensor("ps2", [128, 1024], F32)).ap()

        plan = Plan(nc, es)
        P = plan.op

        def mm_group(psum_ap, pairs, waits, inc="te"):
            n = len(pairs)
            tok = None
            for i, (l, r) in enumerate(pairs):
                tok = P("tensor",
                        lambda e, l=l, r=r, st=(i == 0), sp=(i == n - 1):
                        e.matmul(psum_ap, l, r, start=st, stop=sp),
                        waits=waits if i == 0 else (),
                        inc=inc if i == n - 1 else None)
            return tok

        t_bq = P("sync", lambda e: e.dma_start(
            out=bq_sb[:, 0:16],
            in_=bqkv_h[0:2048].rearrange("(j p) -> p j", p=128),
            allow_slow_non_contiguous=True),
            inc="s_ld", dma=True)
        t_brow = P("sync", lambda e: e.dma_start(
            out=brow[0:1, 0:D],
            in_=bqkv_h[2048:3072].rearrange("(o d) -> o d", o=1)),
            inc="s_ld", dma=True)
        t_gam = P("sync", lambda e: e.dma_start(out=gam[:, :], in_=gamma_h[:, :]),
                  inc="s_ld", dma=True)

        t_id0 = P("gpsimd", lambda e: e.memset(ident[:], 0.0), inc="gp")
        t_id = P("gpsimd", lambda e: e.affine_select(
            out=ident[:], in_=ident[:], compare_op=OP.not_equal, fill=1.0,
            base=0, pattern=[[-1, 128]], channel_multiplier=1),
            waits=[t_id0], inc="gp")
        t_e0 = P("gpsimd", lambda e: e.iota(
            mask0[:], pattern=[[-1, 128]], base=0, channel_multiplier=1,
            allow_small_or_imprecise_dtypes=True), inc="gp")
        t_e = P("gpsimd", lambda e: e.tensor_scalar_max(mask0[:], mask0[:], 0.0),
                waits=[t_e0], inc="gp")
        t_on = P("vector", lambda e: e.memset(ones1[:, :], 1.0), inc="ve")
        t_lg = P("scalar", lambda e: e.activation(lg1[:, :], gam[:, :], AF.Ln),
                 waits=[t_gam], inc="sc")
        t_bc = []
        for hh in range(NH):
            t_bc.append(P("tensor", lambda e, hh=hh: e.matmul(
                ps1[:, hh * 512:(hh + 1) * 512], ones1[:, :],
                brow[0:1, hh * 512:(hh + 1) * 512], start=True, stop=True),
                waits=[t_on, t_brow], inc="te"))
        t_bvb = P("vector", lambda e: e.tensor_copy(bv_b[:, :], ps1[:, 0:D]),
                  waits=t_bc, inc="ve")
        t_brow2 = P("sync", lambda e: e.dma_start(
            out=brow[0:1, 0:D],
            in_=bproj_h[:].rearrange("(o d) -> o d", o=1)),
            waits=t_bc, inc="s_ld", dma=True)
        t_bc2 = []
        for hh in range(NH):
            t_bc2.append(P("tensor", lambda e, hh=hh: e.matmul(
                ps2[:, hh * 512:(hh + 1) * 512], ones1[:, :],
                brow[0:1, hh * 512:(hh + 1) * 512], start=True, stop=True),
                waits=[t_brow2], inc="te"))
        t_bcl = P("tensor", lambda e: e.matmul(
            ps0[:, 0:1], ones1[:, :], lg1[:, :], start=True, stop=True),
            waits=[t_on, t_lg], inc="te")
        t_bpb = P("vector", lambda e: e.tensor_copy(bp_b[:, :], ps2[:, 0:D]),
                  waits=t_bc2, inc="ve")
        t_lgb = P("vector", lambda e: e.tensor_copy(lgb[:, :], ps0[:, 0:1]),
                  waits=[t_bcl], inc="ve")
        t_m0 = P("scalar", lambda e: e.activation(
            maskt[:], mask0[:], AF.Exp, scale=lgb[:, :]),
            waits=[t_e, t_lgb], inc="sc")
        t_mask = P("gpsimd", lambda e: e.affine_select(
            out=maskt[:], in_=maskt[:], compare_op=OP.is_ge, fill=0.0,
            base=0, pattern=[[-1, 128]], channel_multiplier=1),
            waits=[t_m0], inc="gp")

        tr_done = {}
        t_evqk_prev = {}
        t_evv_bank = {}
        t_xn = {}
        t_xld2 = {}
        t_chain = {}
        tr_prev = [None]
        t_ev = {}
        t_ldstats = {}
        ps2c = ps2.rearrange("p (c t) -> p c t", c=DC)

        def emit_loads(st):
            if st >= NST:
                return
            s, t0 = (st * TA) // S, (st * TA) % S
            gate = [t_xn.get((st - 1, j)) for j in range(NSUB)]
            t_xld2[st] = [P("sync", lambda e, s=s, t0=t0, j=j: e.dma_start(
                out=xsA[:, j * D:(j + 1) * D],
                in_=x_h[s, t0 + j * 128:t0 + j * 128 + 128, :]),
                waits=gate, inc="s_lx", dma=True) for j in range(NSUB)]

        def emit_stats(st):
            if st >= NST:
                return
            t_x = t_xld2[st]
            t_sq, prev = [], None
            for j in range(NSUB):
                prev = P("scalar", lambda e, j=j: e.activation(
                    sqA[:, 0:D], xsA[:, j * D:(j + 1) * D], AF.Square,
                    accum_out=sumA_s[:, j:j + 1]),
                    waits=[t_x[j], prev], inc="sc")
                t_sq.append(prev)
            t_sum = []
            for j in range(NSUB):
                t_sum.append(P("vector", lambda e, j=j: e.reduce_sum(
                    sumA_v[:, j:j + 1], xsA[:, j * D:(j + 1) * D], axis=AX.X),
                    waits=[t_x[j]], inc="ve"))
            NSL = slice(0, NSUB)
            t_c1 = P("vector", lambda e: e.tensor_tensor(
                stA[:, 0:NSUB], sumA_v[:, NSL], sumA_v[:, NSL], OP.mult),
                waits=t_sum, inc="ve")
            t_c2 = P("vector", lambda e: e.scalar_tensor_tensor(
                stA[:, NSUB:2 * NSUB], stA[:, 0:NSUB], -1.0 / D,
                sumA_s[:, NSL], OP.mult, OP.add),
                waits=[t_c1] + t_sq, inc="ve")
            t_c3 = P("vector", lambda e: e.tensor_scalar(
                stA[:, 2 * NSUB:3 * NSUB], stA[:, NSUB:2 * NSUB], 1.0 / D, eps,
                OP.mult, OP.add), waits=[t_c2], inc="ve")
            t_c4 = P("scalar", lambda e: e.activation(
                stA_s[:, NSL], stA[:, 2 * NSUB:3 * NSUB], AF.Sqrt),
                waits=[t_c3], inc="sc")
            t_c5 = P("vector", lambda e: e.reciprocal(
                stA[:, 3 * NSUB:4 * NSUB], stA_s[:, NSL]),
                waits=[t_c4], inc="ve")
            t_c6 = P("vector", lambda e: e.scalar_tensor_tensor(
                stA[:, 0:NSUB], sumA_v[:, NSL], -1.0 / D,
                stA[:, 3 * NSUB:4 * NSUB], OP.mult, OP.mult),
                waits=[t_c5, t_c2], inc="ve")
            t_chain[st] = t_c6

        def emit_xn(st, j):
            t_xn[(st, j)] = P("vector", lambda e, j=j: e.tensor_scalar(
                xnA[:, :], xsA[:, j * D:(j + 1) * D],
                stA[:, 3 * NSUB + j:3 * NSUB + j + 1], stA[:, j:j + 1],
                OP.mult, OP.add),
                waits=[t_chain[st], tr_prev[0]], inc="ve")

        def emit_xn_tr(st, j):
            par = st % 2
            xb = par * DC * TA
            tr = None
            for dc in range(DC):
                tr = P("tensor", lambda e, dc=dc: e.transpose(
                    ps2[:, dc * 128:(dc + 1) * 128],
                    xnA[:, dc * 128:(dc + 1) * 128], ident[:]),
                    waits=([t_xn[(st, j)], t_id, t_bpb,
                            t_ev.get((st, j - 1), [None])[0],
                            t_ev.get((st, j - 1), [None, None])[1]]
                           if dc == 0 else ()),
                    inc="te" if dc == DC - 1 else None)
            tr_done[(st, j)] = tr
            tr_prev[0] = tr
            th = P("scalar", lambda e, j=j, xb=xb: e.copy(
                xnf_hi.rearrange("p (x c t) -> p x c t", x=2, c=DC)
                [:, st % 2, :, j * 128:(j + 1) * 128], ps2c[:, :, :]),
                waits=[tr], inc="sc")
            tl = P("vector", lambda e, j=j, xb=xb: e.tensor_tensor(
                xnf_lo.rearrange("p (x c t) -> p x c t", x=2, c=DC)
                [:, st % 2, :, j * 128:(j + 1) * 128], ps2c[:, :, :],
                xnf_hi.rearrange("p (x c t) -> p x c t", x=2, c=DC)
                [:, st % 2, :, j * 128:(j + 1) * 128], OP.subtract),
                waits=[tr, th], inc="ve")
            t_ev[(st, j)] = [th, tl]

        t_wsp = []
        prevs = {0: (None, None), 1: (None, None)}
        t_xld2[0] = []
        for dc in range(DC):
            hs = slice(dc * 3 * D, (dc + 1) * 3 * D)
            buf = wtmp if dc % 2 == 0 else wtmp2
            ph0, pl0 = prevs[dc % 2]
            t_l = P("sync", lambda e, dc=dc, buf=buf: e.dma_start(
                out=buf[:, :], in_=wqkv_h[dc * 128:(dc + 1) * 128, :]),
                waits=[ph0, pl0], inc="s_lw" if dc % 2 == 0 else "s_lw2",
                dma=True)
            if dc < NSUB:
                t_xld2[0].append(P("sync", lambda e, j=dc: e.dma_start(
                    out=xsA[:, j * D:(j + 1) * D],
                    in_=x_h[0, j * 128:j * 128 + 128, :]),
                    inc="s_lx", dma=True))
            ph = P("scalar", lambda e, hs=hs, buf=buf: e.copy(
                wq_hi[:, hs], buf[:, :]),
                waits=[t_l, prevs[(dc + 1) % 2][0]], inc="sc")
            pl = P("vector", lambda e, hs=hs, buf=buf: e.tensor_tensor(
                wq_lo[:, hs], buf[:, :], wq_hi[:, hs], OP.subtract),
                waits=[ph, prevs[(dc + 1) % 2][1]], inc="ve")
            prevs[dc % 2] = (ph, pl)
            t_wsp += [ph, pl]
        t_wsp += [t_bvb, t_bpb, t_lgb]

        emit_stats(0)
        for j in range(NSUB):
            emit_xn(0, j)
            emit_xn_tr(0, j)

        t_wpsp_last = None
        wp_rd = {0: None, 1: None}
        wpprev = {0: (None, None), 1: (None, None)}
        wpst = {0: None, 1: None}
        wq_last = [t_wsp[2 * DC - 2], t_wsp[2 * DC - 1]]
        for dc in range(DC):
            par = dc % 2
            buf = wtmp if par == 0 else wtmp2
            vslice = slice(par * 1024, par * 1024 + 1024)
            ph0, pl0 = wpprev[par]
            t_l = P("sync", lambda e, dc=dc, buf=buf: e.dma_start(
                out=buf[:, 0:D], in_=wproj_h[dc * 128:(dc + 1) * 128, :]),
                waits=[ph0, pl0] + wq_last,
                inc="s_lw" if par == 0 else "s_lw2", dma=True)
            ph = P("scalar", lambda e, buf=buf, vslice=vslice: e.copy(
                vt_hi[:, vslice], buf[:, 0:D]),
                waits=[t_l, wpst[par]], inc="sc")
            pl = P("vector", lambda e, buf=buf, vslice=vslice: e.tensor_tensor(
                vt_lo[:, vslice], buf[:, 0:D], vt_hi[:, vslice], OP.subtract),
                waits=[ph], inc="ve")
            tst = None
            for l, srcv in ((0, vt_hi), (1, vt_lo)):
                tst = P("gpsimd", lambda e, dc=dc, l=l, srcv=srcv,
                        vslice=vslice: e.dma_start(
                            out=wp_scr[l, dc], in_=srcv[:, vslice]),
                        waits=[pl], inc="g_wps", dma=True)
            wpprev[par] = (ph, pl)
            wpst[par] = tst
            wp_rd[par] = pl
            t_wpsp_last = tst

        wgT_r = wgT_h.rearrange("(c p) (hc h) -> hc p c h", p=128, h=128)
        wuT_r = wuT_h.rearrange("(c p) (hc h) -> hc p c h", p=128, h=128)
        for hc in range(HC):
            P("gpsimd", lambda e, hc=hc: e.dma_start(
                out=wg_scr[hc], in_=wgT_r[hc]), inc="g_cv", dma=True)
            P("gpsimd", lambda e, hc=hc: e.dma_start(
                out=wu_scr[hc], in_=wuT_r[hc]), inc="g_cv", dma=True)
        n_cv = plan.cnt["g_cv"]

        t_store = {"q": wp_rd[0], "k": wp_rd[1], "v": None}
        t_vst = {(0, -2): t_wpsp_last, (0, -1): t_wpsp_last}

        for st in range(NST):
            s, t0 = (st * TA) // S, (st * TA) % S
            par = st % 2
            xb = par * DC * TA
            t_xnf = [t_ev[(st, j)][k] for j in range(NSUB) for k in (0, 1)]
            emit_loads(st + 1)

            wsrc = [(wq_hi, xnf_hi), (wq_hi, xnf_lo), (wq_lo, xnf_hi)]
            t_evqk = {}
            for ec in range(16):
                bank = ps0[:, (ec % 4) * 512:(ec % 4) * 512 + TA]
                pairs = [(wx[:, dc * 3 * D + ec * 128:dc * 3 * D + (ec + 1) * 128],
                          xy[:, xb + dc * TA:xb + (dc + 1) * TA])
                         for (wx, xy) in wsrc for dc in range(DC)]
                tmm = mm_group(bank, pairs,
                               waits=t_xnf + t_wsp + [t_evqk.get(ec - 4),
                                                      t_evqk_prev.get(ec + 12)])
                dsth = qf_hi if ec < 8 else kf_hi
                dstl = qf_lo if ec < 8 else kf_lo
                cc = ec % 8
                bias = bq_sb[:, ec:ec + 1]
                th = P("scalar", lambda e, bank=bank, dsth=dsth, cc=cc,
                       bias=bias: e.add(
                           dsth[:, cc * TA:(cc + 1) * TA], bank, bias),
                       waits=[tmm, t_bq, t_store["q" if ec < 8 else "k"]],
                       inc="sc")
                tl = P("vector", lambda e, bank=bank, dsth=dsth, dstl=dstl,
                       cc=cc, bias=bias: e.scalar_tensor_tensor(
                           dstl[:, cc * TA:(cc + 1) * TA], bank, bias,
                           dsth[:, cc * TA:(cc + 1) * TA], OP.add, OP.subtract),
                       waits=[th], inc="ve")
                t_evqk[ec] = tl
                if ec == 3 and st + 1 < NST:
                    emit_stats(st + 1)
                if (ec % 4 == 0 and ec >= 4 and st + 1 < NST
                        and ec // 4 - 1 < NSUB):
                    emit_xn(st + 1, ec // 4 - 1)
                if (ec % 4 == 3 and ec >= 7 and st + 1 < NST
                        and ec // 4 - 1 < NSUB):
                    emit_xn_tr(st + 1, ec // 4 - 1)
            t_evqk_prev = t_evqk

            vsrc = [(xnf_hi, wq_hi), (xnf_lo, wq_hi), (xnf_hi, wq_lo)]
            t_evv = []
            for j in range(NSUB):
                vpar = j % 2
                for half in range(NH):
                    bank = ps1[:, half * 512:(half + 1) * 512]
                    pairs = [(xy[:, xb + dc * TA + j * 128:
                                 xb + dc * TA + (j + 1) * 128],
                              wx[:, dc * 3 * D + 2 * D + half * 512:
                                 dc * 3 * D + 2 * D + (half + 1) * 512])
                             for (xy, wx) in vsrc for dc in range(DC)]
                    tmm = mm_group(bank, pairs,
                                   waits=t_xnf + t_wsp +
                                   [t_evv_bank.get(half)])
                    vs = slice(vpar * D + half * 512, vpar * D + (half + 1) * 512)
                    th = P("scalar", lambda e, bank=bank, vs=vs: e.copy(
                        vt_hi[:, vs], bank),
                        waits=[tmm, t_vst.get((st, j - 2))], inc="sc")
                    tl = P("vector", lambda e, bank=bank, vs=vs:
                           e.tensor_tensor(vt_lo[:, vs], bank, vt_hi[:, vs],
                                           OP.subtract),
                           waits=[th], inc="ve")
                    t_evv += [th, tl]
                    t_evv_bank[half] = tl
                tvs = None
                for l, srcv in ((0, vt_hi), (1, vt_lo)):
                    tvs = P("scalar", lambda e, s=s, l=l, t0=t0, j=j,
                            srcv=srcv, vpar=vpar: e.dma_start(
                                out=v_scr[s, l, t0 // 128 + j, :, :],
                                in_=srcv[:, vpar * D:(vpar + 1) * D]),
                            waits=t_evv[-2:], inc="s_st", dma=True)
                t_vst[(st, j)] = tvs
                if j == 1 and st + 1 < NST and (st + 1, NSUB - 1) not in t_xn:
                    emit_xn(st + 1, NSUB - 1)
                    emit_xn_tr(st + 1, NSUB - 1)

            for (nm, srch, srcl, scr, evs) in (
                    ("q", qf_hi, qf_lo, q_scr, list(t_evqk.values())),
                    ("k", kf_hi, kf_lo, k_scr, list(t_evqk.values()))):
                sem = ("s_st" if st != NST - 1 else
                       ("s_st2" if nm == "q" else "s_st3"))
                tt = None
                for l, src in ((0, srch), (1, srcl)):
                    src_c = src.rearrange("p (c t) -> p c t", c=DC)
                    dst_c = scr[s, l, :, :, t0:t0 + TA].rearrange(
                        "c p t -> p c t")
                    tt = P("scalar", lambda e, src_c=src_c, dst_c=dst_c:
                           e.dma_start(out=dst_c, in_=src_c),
                           waits=evs, inc=sem, dma=True)
                t_store[nm] = tt
            if st == S // TA - 1:
                t_kst_s0 = t_store["k"]
                t_qst_s0 = t_store["q"]
                t_vst_s0 = t_vst[(st, NSUB - 1)]

        t_lastAmm = tmm
        t_qst_final = t_store["q"]
        t_kst_final = t_store["k"]
        t_evqk_last = list(t_evqk_prev.values())
        endA = plan.now("s_st", "s_st2", "s_st3", "te", "ve", "sc",
                        "s_lx", "s_lw")

        t_exp = {}
        t_lastArt = {}
        t_scmm = {}
        t_avmm = {}
        t_p1tok = {}
        t_p2tok = {}
        t_x1rd = {}
        t_qld = {}
        t_xld = {}
        t_kvld = {}
        t_stg2 = {}
        t_aofl = {}
        t_aoev = {}

        def emit_k(s, gate):
            t_kvld[s] = {}
            for kb in range(NKB):
                toks = {}
                for (dst, scr, l, eng, sem) in (
                        (k_hi, k_scr, 0, "sync", "s_lk"),
                        (k_lo, k_scr, 1, "gpsimd", "g_lk")):
                    toks[l] = P(eng, lambda e, dst=dst, scr=scr, l=l,
                                s=s, kb=kb: e.dma_start(
                                    out=dst.rearrange(
                                        "p (c s) -> p c s",
                                        c=DC)[:, :, kb * KW:(kb + 1) * KW],
                                    in_=scr[s, l].rearrange(
                                        "c p s -> p c s")
                                    [:, :, kb * KW:(kb + 1) * KW]),
                                waits=gate, inc=sem, dma=True)
                t_kvld[s][kb] = toks

        def emit_v(s, gate):
            t_vld[s] = {}
            for c in range(NVC):
                js = slice(c * 4, min((c + 1) * 4, NQT))
                toks = []
                for (dst, l, eng, sem) in ((v_hi, 0, "gpsimd", "s_lv"),
                                           (v_lo, 1, "sync", "s_lv2")):
                    toks.append(P(eng, lambda e, dst=dst, l=l, s=s, js=js:
                                  e.dma_start(
                                      out=dst.rearrange(
                                          "p (j d) -> p j d",
                                          j=NQT)[:, js, :],
                                      in_=v_scr[s, l].rearrange(
                                          "j p d -> p j d")[:, js, :]),
                                  waits=gate, inc=sem, dma=True))
                t_vld[s][c] = toks

        def prefetch_q(g):
            if g >= NG or g in t_qld:
                return
            s, i = g // NQT, g % NQT
            gq = ([t_scmm.get(g - 1)] +
                  ([t_lastAmm, t_qst_s0] if g == 0 else []))
            tq1 = P("sync", lambda e, s=s, i=i: e.dma_start(
                out=qt_hl.rearrange("p (l c t) -> p l c t", l=2, c=DC),
                in_=q_scr[s, :, :, :, i * 128:(i + 1) * 128].rearrange(
                    "l c p t -> p l c t")),
                waits=gq, inc="s_lq", dma=True)
            t_qld[g] = [tq1]

        def prefetch_x(g):
            if g >= NG or g in t_xld:
                return
            s, i = g // NQT, g % NQT
            t_xld[g] = P("sync", lambda e, s=s, i=i: e.dma_start(
                out=xres[:, :], in_=x_h[s, i * 128:(i + 1) * 128, :]),
                waits=[t_p2tok.get(g - 1)] + (endA if g == 0 else []),
                inc="s_lxb", dma=True)

        def emit_scores(g):
            s = g // NQT
            lastmm = None
            for kb in range(NKB):
                n = 0
                for (qx, kx) in ((qt_hi, k_hi), (qt_hi, k_lo), (qt_lo, k_hi)):
                    kl = 0 if kx is k_hi else 1
                    for dc in range(DC):
                        lhs = qx[:, dc * 128:(dc + 1) * 128]
                        n += 1
                        first, last = n == 1, n == 3 * DC
                        w = ()
                        if first:
                            w = (t_qld[g] + [t_kvld[s][kb][0],
                                             t_exp.get(g - 1)] +
                                 (t_evqk_last if g == 0 else []))
                        elif n == DC + 1:
                            w = (t_kvld[s][kb][1],)
                        lastmm = P("tensor", lambda e, lhs=lhs, kx=kx, dc=dc,
                                   kb=kb, st=first, sp=last: e.matmul(
                                       ps0[:, kb * 512:kb * 512 + KW], lhs,
                                       kx[:, dc * S + kb * KW:
                                          dc * S + (kb + 1) * KW],
                                       start=st, stop=sp),
                                   waits=w, inc="te" if last else None)
            t_scmm[g] = lastmm

        def emit_softmax(g):
            s, i = g // NQT, g % NQT
            t_mk = P("vector", lambda e, i=i: e.tensor_tensor(
                ps0[:, i * 128:(i + 1) * 128], ps0[:, i * 128:(i + 1) * 128],
                maskt[:], OP.mult),
                waits=[t_scmm[g], t_mask], inc="ve")
            t_mx = P("vector", lambda e: e.reduce_max(
                mx_b[:, 0:1], ps0[:, 0:S], axis=AX.X),
                waits=[t_mk, t_exp.get(g - 1)], inc="ve")
            t_ng = P("vector", lambda e: e.tensor_scalar_mul(
                mx_b[:, 1:2], mx_b[:, 0:1], -isqd),
                waits=[t_mx], inc="ve")
            te_ = P("scalar", lambda e: e.activation(
                a32[:, 0:S], ps0[:, 0:S], AF.Exp, bias=mx_b[:, 1:2],
                scale=isqd, accum_out=ell[:, 0:1]),
                waits=[t_ng, t_lastArt.get(g - 1), t_aoev.get(g - 1),
                       t_aofl.get(g - 1)] +
                ([t_kst_final] if g == 0 else []),
                inc="sc")
            t_exp[g] = te_
            t_il[g] = P("vector", lambda e: e.reciprocal(
                invl[:, 0:1], ell[:, 0:1]), waits=[te_], inc="ve")

        t_il = {}
        t_vld = {}
        pend_s2 = []
        t_s2done = [None]
        x1st = {}
        prefetch_q(0)
        emit_k(0, [t_lastAmm, t_kst_s0])
        emit_v(0, [t_lastAmm, t_vst_s0])
        t_wp = []
        for (dst, l, eng, sem, gw) in (
                (wp_hi, 0, "sync", "s_lw", [t_lastAmm]),
                (wp_lo, 1, "gpsimd", "g_lw", [t_lastAmm, t_qst_final])):
            t_wp.append(P(eng, lambda e, dst=dst, l=l: e.dma_start(
                out=dst.rearrange("p (c d) -> p c d", c=DC),
                in_=wp_scr[l].rearrange("c p d -> p c d")),
                waits=gw + [("g_wps", plan.cnt["g_wps"])],
                inc=sem, dma=True))
        prefetch_x(0)
        emit_scores(0)
        emit_softmax(0)
        for g in range(NG):
            s, i = g // NQT, g % NQT
            te_ = t_exp[g]
            t_at = []
            tr = None
            W2 = min(512, S)
            for r in range(S // W2):
                pp = (r % 2) * 512
                for q4 in range(W2 // 128):
                    kt = r * (W2 // 128) + q4
                    tr = P("tensor", lambda e, kt=kt, q4=q4, pp=pp: e.transpose(
                        ps2[:, pp + q4 * 128:pp + (q4 + 1) * 128],
                        a32[:, kt * 128:(kt + 1) * 128], ident[:]),
                        waits=([te_] + t_stg2.get((r - 2) % 2, []))
                        if q4 == 0 else (),
                        inc="te" if q4 == W2 // 128 - 1 else None)
                th = P("scalar", lambda e, r=r, pp=pp, W2=W2: e.copy(
                    at_hi[:, r * W2:(r + 1) * W2], ps2[:, pp:pp + W2]),
                    waits=[tr] + ([t_kst_final] if g == 0 else []),
                    inc="sc")
                tl = P("vector", lambda e, r=r, pp=pp, W2=W2: e.tensor_tensor(
                    at_lo[:, r * W2:(r + 1) * W2], ps2[:, pp:pp + W2],
                    at_hi[:, r * W2:(r + 1) * W2], OP.subtract),
                    waits=[tr, th], inc="ve")
                t_stg2[r % 2] = [th, tl]
                t_at += [th, tl]
            t_lastArt[g] = tr
            while pend_s2:
                t_s2done[0] = pend_s2.pop(0)()
                t_x1rd[g - 1][0] = t_s2done[0]
            lastav = None
            cnt = {h: 0 for h in range(NH)}
            for kt in range(NQT):
                rd = (kt * 128) // W2
                for (ax, vx) in ((at_hi, v_hi), (at_hi, v_lo), (at_lo, v_hi)):
                    lhs = ax[:, kt * 128:(kt + 1) * 128]
                    for half in range(NH):
                        cnt[half] += 1
                        first = cnt[half] == 1
                        last = cnt[half] == 3 * NQT
                        w = ()
                        if first:
                            w = (t_at[2 * rd:2 * rd + 2] +
                                 [t_p1tok.get(g - 1)] + t_vld[s][kt // 4] +
                                 (t_evv if g == 0 else []))
                        elif (kt % 4 == 0 and half == 0 and ax is at_hi
                              and vx is v_hi):
                            w = (tuple(t_at[2 * rd:2 * rd + 2]) +
                                 tuple(t_vld[s][kt // 4]))
                        lastav = P("tensor", lambda e, lhs=lhs, vx=vx, kt=kt,
                                   half=half, st=first, sp=last: e.matmul(
                                       ps1[:, half * 512:(half + 1) * 512],
                                       lhs,
                                       vx[:, kt * D + half * 512:
                                          kt * D + (half + 1) * 512],
                                       start=st, stop=sp),
                                   waits=w, inc="te" if last else None)
            t_avmm[g] = lastav
            if g + 1 < NG:
                prefetch_q(g + 1)
                if (g + 1) % NQT == 0:
                    emit_v(s + 1, [lastav])
                emit_scores(g + 1)
            if g + 2 < NG and (g + 2) % NQT == 0:
                emit_k(s + 1, [t_scmm[g + 1]])
            t_ao = P("vector", lambda e: e.scalar_tensor_tensor(
                ao32[:, :], ps1[:, :], invl[:, 0:1], bv_b[:, :],
                OP.mult, OP.add),
                waits=[lastav, t_il[g], t_bvb, t_aofl.get(g - 1)], inc="ve")
            t_aoev[g] = t_ao
            tr = None
            for r in range(2):
                pp = (r % 2) * 512
                for q4 in range(4):
                    dc = r * 4 + q4
                    tr = P("tensor", lambda e, dc=dc, q4=q4, pp=pp: e.transpose(
                        ps2[:, pp + q4 * 128:pp + (q4 + 1) * 128],
                        ao32[:, dc * 128:(dc + 1) * 128], ident[:]),
                        waits=([t_ao] + t_stg2.get((r - 2) % 2, []))
                        if q4 == 0 else (),
                        inc="te" if q4 == 3 else None)
                th = P("scalar", lambda e, r=r, pp=pp: e.copy(
                    aof_hi[:, r * 512:(r + 1) * 512], ps2[:, pp:pp + 512]),
                    waits=[tr] + (endA if g == 0 else []),
                    inc="sc")
                tl = P("vector", lambda e, r=r, pp=pp: e.tensor_tensor(
                    aof_lo[:, r * 512:(r + 1) * 512], ps2[:, pp:pp + 512],
                    aof_hi[:, r * 512:(r + 1) * 512], OP.subtract),
                    waits=[tr, th], inc="ve")
                t_stg2[r % 2] = [th, tl]
                if r == 1:
                    t_aof_ev = [th, tl]
            t_aofl[g] = t_aof_ev[1]
            if g + 1 < NG:
                emit_softmax(g + 1)
            lastpj = None
            cnt = {h: 0 for h in range(NH)}
            for (ax, wx) in ((aof_hi, wp_hi), (aof_hi, wp_lo),
                             (aof_lo, wp_hi)):
                for dc in range(DC):
                    lhs = ax[:, dc * 128:(dc + 1) * 128]
                    for half in range(NH):
                        cnt[half] += 1
                        first = cnt[half] == 1
                        last = cnt[half] == 3 * DC
                        w = (t_aof_ev + t_stg2[0] + t_stg2[1] + [t_ao] + t_wp) \
                            if first else ()
                        lastpj = P("tensor", lambda e, lhs=lhs, wx=wx, dc=dc,
                                   half=half, st=first, sp=last: e.matmul(
                                       ps1[:, half * 512:(half + 1) * 512],
                                       lhs,
                                       wx[:, dc * D + half * 512:
                                          dc * D + (half + 1) * 512],
                                       start=st, stop=sp),
                                   waits=w, inc="te" if last else None)
            t_p1 = P("vector", lambda e: e.tensor_tensor(
                x1t[:, :], ps1[:, :], bp_b[:, :], OP.add),
                waits=[lastpj, t_bpb] + t_x1rd.get(g - 1, []), inc="ve")
            t_p1tok[g] = t_p1
            t_p2 = P("vector", lambda e: e.tensor_tensor(
                x1t[:, :], x1t[:, :], xres[:, :], OP.add),
                waits=[t_p1, t_xld[g]], inc="ve")
            t_p2tok[g] = t_p2
            def s2fn(g=g, t_p2=t_p2):
                return P("scalar", lambda e, g=g, t_p2=t_p2: e.activation(
                    sqA[:, 0:D], x1t[:, :], AF.Square,
                    accum_out=sum2_s[:, g:g + 1]),
                    waits=[t_p2, t_s2done[0]], inc="sc")
            pend_s2.append(s2fn)
            t_r2 = P("vector", lambda e, g=g: e.reduce_sum(
                sum2_v[:, g:g + 1], x1t[:, :], axis=AX.X),
                waits=[t_p2], inc="ve")
            t_st = P("gpsimd", lambda e, s=s, i=i: e.dma_start(
                out=x1_scr[(s * S + i * 128):(s * S + (i + 1) * 128), :],
                in_=x1t[:, :]),
                waits=[t_p2], inc="s_sx1", dma=True)
            x1st[g] = t_st
            t_x1rd[g] = [t_r2, t_r2, t_st]
            prefetch_x(g + 1)

        while pend_s2:
            t_s2done[0] = pend_s2.pop(0)()
            t_x1rd[NG - 1][0] = t_s2done[0]

        endB = plan.now("te", "ve", "sc", "s_sx1", "s_lq", "s_lxb")

        t_scmm_last = t_scmm[NG - 1]
        t_avmm_last = t_avmm[NG - 1]
        wd_in = wdT_h.rearrange("(hc p) d -> p hc d", p=128)
        wd_out = wd_sb.rearrange("p (hc d) -> p hc d", hc=HC)
        t_wd = P("gpsimd", lambda e: e.dma_start(out=wd_out, in_=wd_in),
                 waits=[t_scmm_last], inc="g_cv", dma=True)
        t2a = P("vector", lambda e: e.tensor_tensor(
            st2[:, 0:NG], sum2_v[:, :], sum2_v[:, :], OP.mult),
            waits=endB, inc="ve")
        t2b = P("vector", lambda e: e.scalar_tensor_tensor(
            st2[:, NG:2 * NG], st2[:, 0:NG], -1.0 / D, sum2_s[:, :],
            OP.mult, OP.add), waits=[t2a], inc="ve")
        t2c = P("vector", lambda e: e.tensor_scalar(
            st2[:, 2 * NG:3 * NG], st2[:, NG:2 * NG], 1.0 / D, eps,
            OP.mult, OP.add), waits=[t2b], inc="ve")
        t2d = P("scalar", lambda e: e.activation(
            st2_s[:, :], st2[:, 2 * NG:3 * NG], AF.Sqrt),
            waits=[t2c], inc="sc")
        t2e = P("vector", lambda e: e.reciprocal(inv2[:, :], st2_s[:, :]),
                waits=[t2d], inc="ve")
        t2f = P("vector", lambda e: e.scalar_tensor_tensor(
            ngm2[:, :], sum2_v[:, :], -1.0 / D, inv2[:, :], OP.mult, OP.mult),
            waits=[t2e], inc="ve")

        t_wld = {}
        t_wmm = {}
        t_oev = {}
        t_ylast = {}
        t_xn2 = {}
        t_ev2 = {}
        t_hev_pair = []
        t_gu_prev = []
        t_gu_hist = {}
        t_stgC = [t_stg2.get(0, []) + t_stg2.get(1, [])]
        tr2_prev = [None]
        x1_readers = {0: [], 1: []}
        t_x1ld = {}
        x2fh_c = xn2f_hi.rearrange("p (x c t) -> p x c t", x=2, c=DC)
        x2fl_c = xn2f_lo.rearrange("p (x c t) -> p x c t", x=2, c=DC)

        def emit_x1_loads(u):
            if u >= NST or u in t_x1ld:
                return
            parU = u % 2
            t_x1ld[u] = [P("sync", lambda e, u=u, j=j, parU=parU: e.dma_start(
                out=x1A[:, (parU * NSUB + j) * D:(parU * NSUB + j + 1) * D],
                in_=x1_scr[u * TA + j * 128:u * TA + (j + 1) * 128, :]),
                waits=([t_p1tok[NG - 1], x1st[j]] if u == 0 else
                       [t_xn2.get((u - 2, j)), t_oev.get((u - 2, j))]),
                inc="s_lx1", dma=True) for j in range(NSUB)]

        def emit_xn2(u, j):
            parU = u % 2
            col = (u * TA) // 128 + j
            t_xn2[(u, j)] = P("vector", lambda e, j=j, parU=parU, col=col:
                              e.tensor_scalar(
                                  xn2[:, :],
                                  x1A[:, (parU * NSUB + j) * D:
                                      (parU * NSUB + j + 1) * D],
                                  inv2[:, col:col + 1], ngm2[:, col:col + 1],
                                  OP.mult, OP.add),
                              waits=[t_x1ld[u][j], t2f, tr2_prev[0]],
                              inc="ve")

        def emit_xn2_tr(u, j):
            parU = u % 2
            tr = None
            for dc in range(DC):
                tr = P("tensor", lambda e, dc=dc: e.transpose(
                    ps2[:, dc * 128:(dc + 1) * 128],
                    xn2[:, dc * 128:(dc + 1) * 128], ident[:]),
                    waits=([t_xn2[(u, j)]] + t_stgC[0] +
                           ([t_wmm.get((u - 1, HC - 1))] if j == 0 else []))
                    if dc == 0 else (),
                    inc="te" if dc == DC - 1 else None)
            tr2_prev[0] = tr
            th = P("scalar", lambda e, j=j, parU=parU: e.copy(
                x2fh_c[:, parU, :, j * 128:(j + 1) * 128], ps2c[:, :, :]),
                waits=[tr], inc="sc")
            tl = P("vector", lambda e, j=j, parU=parU: e.tensor_tensor(
                x2fl_c[:, parU, :, j * 128:(j + 1) * 128], ps2c[:, :, :],
                x2fh_c[:, parU, :, j * 128:(j + 1) * 128], OP.subtract),
                waits=[tr, th], inc="ve")
            t_ev2[(u, j)] = [th, tl]
            t_stgC[0] = [th, tl]

        emit_x1_loads(0)
        for j in range(NSUB):
            emit_xn2(0, j)
            emit_xn2_tr(0, j)

        for u in range(NST):
            parU = u % 2
            xb2 = parU * DC * TA
            s, t0 = (u * TA) // S, (u * TA) % S
            t_xnf2 = [t_ev2[(u, j)][k] for j in range(NSUB) for k in (0, 1)]
            emit_x1_loads(u + 1)

            t_hev = []
            for hc in range(HC):
                parW = hc % 2
                for look in ([0, 1] if hc == 0 else [1]):
                    hh = hc + look
                    if hh >= HC or (u, hh) in t_wld:
                        continue
                    pw = hh % 2
                    gw = [t_wmm.get((u, hh - 2))]
                    if hh < 2:
                        gw = [t_wmm.get((u - 1, HC - 2 + hh))]
                        if u == 0:
                            gw = [("g_cv", n_cv), t_avmm_last]
                    tg = P("sync", lambda e, hh=hh, pw=pw: e.dma_start(
                        out=wgbuf[:, pw * D:(pw + 1) * D].rearrange(
                            "p (c h) -> p c h", c=DC),
                        in_=wg_scr[hh]),
                        waits=gw, inc="s_lwg", dma=True)
                    tu = P("sync", lambda e, hh=hh, pw=pw: e.dma_start(
                        out=wubuf[:, pw * D:(pw + 1) * D].rearrange(
                            "p (c h) -> p c h", c=DC),
                        in_=wu_scr[hh]),
                        waits=gw, inc="s_lwu", dma=True)
                    t_wld[(u, hh)] = [tg, tu]
                tw = t_wld[(u, hc)]
                gbank = ps0[:, parW * 1024:parW * 1024 + TA]
                ubank = ps0[:, parW * 1024 + 512:parW * 1024 + 512 + TA]
                pairs_g = [(wgbuf[:, parW * D + dc * 128:
                                  parW * D + (dc + 1) * 128],
                            xy[:, xb2 + dc * TA:xb2 + (dc + 1) * TA])
                           for xy in (xn2f_hi, xn2f_lo) for dc in range(DC)]
                pairs_u = [(wubuf[:, parW * D + dc * 128:
                                  parW * D + (dc + 1) * 128],
                            xy[:, xb2 + dc * TA:xb2 + (dc + 1) * TA])
                           for xy in (xn2f_hi, xn2f_lo) for dc in range(DC)]
                ghc = u * HC + hc
                tmm_g = mm_group(gbank, pairs_g,
                                 waits=t_xnf2 + tw + t_gu_hist.get(ghc - 2, []) +
                                 ([t_oev.get((u - 1, 1))] if hc == 0 else []) +
                                 ([t_oev.get((u - 1, 3))]
                                  if hc == 1 and NSUB == 4 else []))
                tmm_u = mm_group(ubank, pairs_u, waits=[])
                t_wmm[(u, hc)] = tmm_u
                t_sg = P("scalar", lambda e, gbank=gbank: e.activation(
                    sgb[:, :], gbank, AF.Silu, scale=1.0 / 127.0),
                    waits=[tmm_g] + ([t_gu_prev[1]] if t_gu_prev else []),
                    inc="sc")
                t_h = P("vector", lambda e, ubank=ubank: e.scalar_tensor_tensor(
                    h32[:, :], ubank, 1.0 / 127.0, sgb[:, :],
                    OP.mult, OP.mult),
                    waits=[tmm_u, t_sg] + t_hev_pair, inc="ve")
                t_hh = P("scalar", lambda e, hc=hc: e.copy(
                    hf_hi[:, hc * TA:(hc + 1) * TA], h32[:, :]),
                    waits=[t_h, t_oev.get((u - 1, NSUB - 1))], inc="sc")
                t_hl = P("vector", lambda e, hc=hc: e.tensor_tensor(
                    hf_lo[:, hc * TA:(hc + 1) * TA], h32[:, :],
                    hf_hi[:, hc * TA:(hc + 1) * TA], OP.subtract),
                    waits=[t_hh], inc="ve")
                t_hev_pair = [t_hh, t_hl]
                t_hev += [t_hh, t_hl]
                t_gu_prev = [t_sg, t_h]
                t_gu_hist[ghc] = [t_sg, t_h]
                if u + 1 < NST and HC >= 12:
                    jj = hc // (HC // NSUB)
                    ph_in = hc % (HC // NSUB)
                    if ph_in == 1:
                        emit_xn2(u + 1, jj)
                    elif ph_in == 4:
                        emit_xn2_tr(u + 1, jj)

            for j in range(NSUB):
                psd = ps1 if j % 2 == 0 else (
                    ps0[:, 0:1024] if j % 4 == 1 else ps0[:, 1024:2048])
                lastdn = None
                cnt = {h: 0 for h in range(NH)}
                for hx in (hf_hi, hf_lo):
                    for hc in range(HC):
                        lhs = hx[:, hc * TA + j * 128:hc * TA + (j + 1) * 128]
                        for half in range(NH):
                            cnt[half] += 1
                            first = cnt[half] == 1
                            last = cnt[half] == 2 * HC
                            w = ()
                            if first:
                                w = t_hev + [t_wd, t_oev.get((u, j - 2))]
                                if j % 2 == 1:
                                    w = w + t_gu_hist[
                                        u * HC + HC - (2 if j % 4 == 1 else 1)]
                            lastdn = P("tensor", lambda e, lhs=lhs, hc=hc,
                                       half=half, psd=psd, st=first, sp=last:
                                       e.matmul(
                                           psd[:, half * 512:(half + 1) * 512],
                                           lhs,
                                           wd_sb[:, hc * D + half * 512:
                                                 hc * D + (half + 1) * 512],
                                           start=st, stop=sp),
                                       waits=w, inc="te" if last else None)
                to = P("vector", lambda e, j=j, parU=parU, psd=psd:
                       e.scalar_tensor_tensor(
                           outA[:, (j % 2) * D:(j % 2 + 1) * D], psd[:, 0:D],
                           1.0 / 127.0,
                           x1A[:, (parU * NSUB + j) * D:
                               (parU * NSUB + j + 1) * D],
                           OP.mult, OP.add),
                       waits=[lastdn, t_ylast.get((u, j - 2)),
                              t_ylast.get((u - 1, j + 2))], inc="ve")
                t_oev[(u, j)] = to
                ty = P("scalar", lambda e, s=s, t0=t0, j=j: e.dma_start(
                    out=y_h[s, t0 + j * 128:t0 + j * 128 + 128, :],
                    in_=outA[:, (j % 2) * D:(j % 2 + 1) * D]),
                    waits=[to], inc="s_sy", dma=True)
                t_ylast[(u, j)] = ty
            for hh in ((0, 1) if u + 1 < NST else ()):
                if hh >= HC or (u + 1, hh) in t_wld:
                    continue
                pw = hh % 2
                gw = [t_wmm.get((u, HC - 2 + hh))]
                tg = P("sync", lambda e, hh=hh, pw=pw: e.dma_start(
                    out=wgbuf[:, pw * D:(pw + 1) * D].rearrange(
                        "p (c h) -> p c h", c=DC),
                    in_=wg_scr[hh]),
                    waits=gw, inc="s_lwg", dma=True)
                tu = P("sync", lambda e, hh=hh, pw=pw: e.dma_start(
                    out=wubuf[:, pw * D:(pw + 1) * D].rearrange(
                        "p (c h) -> p c h", c=DC),
                    in_=wu_scr[hh]),
                    waits=gw, inc="s_lwu", dma=True)
                t_wld[(u + 1, hh)] = [tg, tu]
            if u + 1 < NST and HC < 12:
                for j in range(NSUB):
                    emit_xn2(u + 1, j)
                    emit_xn2_tr(u + 1, j)

        n_y = plan.cnt["s_sy"]
        plan.ops["sync"].append(
            (lambda e: e.nop(), (("s_sy", n_y),), None, 1))

        block = es.enter_context(nc.Block())

        @block.sync
        def _(e):
            plan.play("sync", e)

        @block.gpsimd
        def _(e):
            plan.play("gpsimd", e)

        @block.scalar
        def _(e):
            plan.play("scalar", e)

        @block.vector
        def _(e):
            plan.play("vector", e)

        @block.tensor
        def _(e):
            plan.play("tensor", e)

    return nc


def kernel(L0_attn_x, L0_attn_Wqkv, L0_attn_bqkv, L0_attn_Wproj, L0_attn_bproj,
           L0_attn_gamma, L0_ffn_w_gate, L0_ffn_w_up, L0_ffn_w_down):
    x = np.ascontiguousarray(np.asarray(L0_attn_x, dtype=np.float32))
    Bfull, S, D = x.shape
    ncore = 8
    Bc = Bfull // ncore
    H = L0_ffn_w_gate.shape[0]

    wqkv = np.ascontiguousarray(np.asarray(L0_attn_Wqkv, dtype=np.float32))
    bqkv = np.ascontiguousarray(np.asarray(L0_attn_bqkv, dtype=np.float32))
    wproj = np.ascontiguousarray(np.asarray(L0_attn_Wproj, dtype=np.float32))
    bproj = np.ascontiguousarray(np.asarray(L0_attn_bproj, dtype=np.float32))
    gamma = np.asarray(L0_attn_gamma, dtype=np.float32).reshape(1, 1)
    wgT = np.ascontiguousarray(np.asarray(L0_ffn_w_gate, dtype=np.int32).T)
    wuT = np.ascontiguousarray(np.asarray(L0_ffn_w_up, dtype=np.int32).T)
    wdT = np.ascontiguousarray(np.asarray(L0_ffn_w_down, dtype=np.int32).T)

    nc = build_program(B=Bc, S=S, D=D, H=H)
    in_maps = []
    for c in range(ncore):
        in_maps.append({
            "x": np.ascontiguousarray(x[c * Bc:(c + 1) * Bc]),
            "wqkv": wqkv, "bqkv": bqkv, "wproj": wproj, "bproj": bproj,
            "gamma": gamma, "wgT": wgT, "wuT": wuT, "wdT": wdT,
        })
    res = run_bass_kernel_spmd(nc, in_maps, core_ids=list(range(ncore)))
    out = np.concatenate([r["y"] for r in res.results], axis=0)
    return out.astype(np.float32)


# version 21
# speedup vs baseline: 1.0731x; 1.0731x over previous
import numpy as np
from contextlib import ExitStack

import concourse.bass as bass
import concourse.mybir as mybir
from concourse.bass_utils import run_bass_kernel_spmd

F32 = mybir.dt.float32
BF16 = mybir.dt.bfloat16
I32 = mybir.dt.int32
AF = mybir.ActivationFunctionType
AX = mybir.AxisListType
OP = mybir.AluOpType

ENGS = ("sync", "gpsimd", "scalar", "vector", "tensor")


class Plan:

    def __init__(self, nc, es):
        self.nc = nc
        self.es = es
        self.ops = {e: [] for e in ENGS}
        self.cnt = {}
        self.sems = {}

    def sem(self, name):
        if name not in self.sems:
            self.sems[name] = self.es.enter_context(self.nc.semaphore(name))
            self.cnt[name] = 0
        return self.sems[name]

    def op(self, eng, fn, waits=(), inc=None, dma=False):
        tok = None
        extra = ()
        if inc is not None:
            self.sem(inc)
            if dma and self.cnt[inc] > 0:
                extra = ((inc, self.cnt[inc]),)
            self.cnt[inc] += 16 if dma else 1
            tok = (inc, self.cnt[inc])
        ws = tuple(t for t in waits if t is not None) + extra
        self.ops[eng].append((fn, ws, inc, 16 if dma else 1))
        return tok

    def now(self, *names):
        return [(n, self.cnt[n]) for n in names if self.cnt.get(n, 0) > 0]

    def play(self, engname, eng):
        seen = {}
        for fn, waits, incname, n in self.ops[engname]:
            for (s, v) in waits:
                if v > 0 and seen.get(s, 0) < v:
                    eng.wait_ge(self.sems[s], v)
                    seen[s] = v
            ins = fn(eng)
            if incname is not None:
                ins.then_inc(self.sems[incname], n)


def build_program(B=2, S=2048, D=1024, H=3072):
    nc = bass.Bass("TRN2", target_bir_lowering=False)
    DC = D // 128
    HC = H // 128
    NQT = S // 128
    NG = B * NQT
    TA = min(512, S)
    NSUB = TA // 128
    NST = (B * S) // TA
    NKB = max(S // 512, 1)
    KW = min(512, S)
    NVC = max(S // 512, 1)
    NH = D // 512
    NR = max(S // 1024, 1)
    NTR = min(S // 128, 8)
    eps = 1e-5
    isqd = float(1.0 / np.sqrt(np.float32(D)))

    x_h = nc.dram_tensor("x", [B, S, D], F32, kind="ExternalInput").ap()
    wqkv_h = nc.dram_tensor("wqkv", [D, 3 * D], F32, kind="ExternalInput").ap()
    bqkv_h = nc.dram_tensor("bqkv", [3 * D], F32, kind="ExternalInput").ap()
    wproj_h = nc.dram_tensor("wproj", [D, D], F32, kind="ExternalInput").ap()
    bproj_h = nc.dram_tensor("bproj", [D], F32, kind="ExternalInput").ap()
    gamma_h = nc.dram_tensor("gamma", [1, 1], F32, kind="ExternalInput").ap()
    wgT_h = nc.dram_tensor("wgT", [D, H], I32, kind="ExternalInput").ap()
    wuT_h = nc.dram_tensor("wuT", [D, H], I32, kind="ExternalInput").ap()
    wdT_h = nc.dram_tensor("wdT", [H, D], I32, kind="ExternalInput").ap()
    y_h = nc.dram_tensor("y", [B, S, D], F32, kind="ExternalOutput").ap()

    q_scr = nc.dram_tensor("q_scr", [B, 2, DC, 128, S], BF16, kind="Internal").ap()
    k_scr = nc.dram_tensor("k_scr", [B, 2, DC, 128, S], BF16, kind="Internal").ap()
    v_scr = nc.dram_tensor("v_scr", [B, 2, NQT, 128, D], BF16, kind="Internal").ap()
    x1_scr = nc.dram_tensor("x1_scr", [B * S, D], F32, kind="Internal").ap()
    wg_scr = nc.dram_tensor("wg_scr", [HC, 128, DC, 128], BF16, kind="Internal").ap()
    wu_scr = nc.dram_tensor("wu_scr", [HC, 128, DC, 128], BF16, kind="Internal").ap()
    wp_scr = nc.dram_tensor("wp_scr", [2, DC, 128, D], BF16, kind="Internal").ap()

    es = ExitStack()
    with es:
        def sb(name, shape, dtype):
            return nc.alloc_sbuf_tensor(name, shape, dtype).ap()

        ident = sb("ident", [128, 128], F32)
        maskt = sb("maskt", [128, 128], F32)
        bv_b = sb("bv_b", [128, D], F32)
        bp_b = sb("bp_b", [128, D], F32)
        gam = sb("gam", [1, 1], F32)
        lg1 = sb("lg1", [1, 1], F32)
        lgb = sb("lgb", [128, 1], F32)
        sumA_v = sb("sumA_v", [128, NSUB], F32)
        sumA_s = sb("sumA_s", [128, NSUB], F32)
        stA = sb("stA", [128, 4 * NSUB], F32)
        stA_s = sb("stA_s", [128, NSUB], F32)
        sum2_v = sb("sum2_v", [128, NG], F32)
        sum2_s = sb("sum2_s", [128, NG], F32)
        st2 = sb("st2", [128, 3 * NG], F32)
        st2_s = sb("st2_s", [128, NG], F32)
        inv2 = sb("inv2", [128, NG], F32)
        ngm2 = sb("ngm2", [128, NG], F32)
        mx_b = sb("mx_b", [128, 2], F32)
        ell = sb("ell", [128, 1], F32)
        invl = sb("invl", [128, 1], F32)

        rem = nc.sbuf_bytes_remaining
        got = nc.bump_sbuf(rem - 64)
        assert got is not None
        arena, arena_hi = got

        starts = {}

        def at(name, cols, dtype, off):
            a = (off + 31) // 32 * 32
            t = nc.alloc_sbuf_tensor_at(name, [128, cols], dtype,
                                        offset=arena + a)
            starts[name] = a
            return t.ap(), a + cols * mybir.dt.size(dtype)

        o = 0
        wq_hi, o = at("wq_hi", DC * 3 * D, BF16, o)
        wq_lo, o = at("wq_lo", DC * 3 * D, BF16, o)
        xsA, o = at("xsA", NSUB * D, F32, o)
        xnA, o = at("xnA", D, F32, o)
        xnf_hi, o = at("xnf_hi", 2 * DC * TA, BF16, o)
        xnf_lo, o = at("xnf_lo", 2 * DC * TA, BF16, o)
        qf_hi, o = at("qf_hi", DC * TA, BF16, o)
        qf_lo, o = at("qf_lo", DC * TA, BF16, o)
        kf_hi, o = at("kf_hi", DC * TA, BF16, o)
        kf_lo, o = at("kf_lo", DC * TA, BF16, o)
        vt_hi, o = at("vt_hi", 2 * D, BF16, o)
        vt_lo, o = at("vt_lo", 2 * D, BF16, o)
        brow, o = at("brow", D, F32, o)
        brow = brow[0:1, :]
        ones1, o = at("ones1", 128, F32, o)
        ones1 = ones1[0:1, :]
        mask0, o = at("mask0", 128, F32, o)
        bq_sb, o = at("bq_sb", 16, F32, o)
        if 2 * DC * TA * 2 >= 3 * D * 4:
            wtmp, _ = at("wtmp", 3 * D, F32, starts["qf_hi"])
            wtmp2, _ = at("wtmp2", 3 * D, F32, starts["kf_hi"])
            a_end = o
        else:
            wtmp, o = at("wtmp", 3 * D, F32, o)
            wtmp2, o = at("wtmp2", 3 * D, F32, o)
            a_end = o
        o = 0
        qt_hl, o = at("qt_hl", 2 * D, BF16, o)
        qt_hi = qt_hl[:, 0:D]
        qt_lo = qt_hl[:, D:2 * D]
        k_hi, o = at("k_hi", DC * S, BF16, o)
        k_lo, o = at("k_lo", DC * S, BF16, o)
        v_hi, o = at("v_hi", NQT * D, BF16, o)
        v_lo, o = at("v_lo", NQT * D, BF16, o)
        wp_hi, o = at("wp_hi", DC * D, BF16, o)
        wp_lo, o = at("wp_lo", DC * D, BF16, o)
        a32, o = at("a32", S, F32, o)
        at_hi, o = at("at_hi", S, BF16, o)
        at_lo, o = at("at_lo", S, BF16, o)
        if S >= D:
            ao32, _ = at("ao32", D, F32, starts["a32"])
        else:
            ao32, o = at("ao32", D, F32, o)
        aof_hi, o = at("aof_hi", D, BF16, o)
        aof_lo, o = at("aof_lo", D, BF16, o)
        xres, o = at("xres", D, F32, o)
        x1t, o = at("x1t", D, F32, o)
        b_end = o
        o = 0
        wd_sb, o = at("wd_sb", HC * D, BF16, o)
        wgbuf, o = at("wgbuf", 2 * D, BF16, o)
        wubuf, o = at("wubuf", 2 * D, BF16, o)
        x1A, o = at("x1A", 2 * NSUB * D, F32, o)
        xn2, o = at("xn2", D, F32, o)
        xn2f_hi, o = at("xn2f_hi", 2 * DC * TA, BF16, o)
        xn2f_lo, o = at("xn2f_lo", 2 * DC * TA, BF16, o)
        sgb, o = at("sgb", TA, F32, o)
        h32, o = at("h32", TA, F32, o)
        hf_hi, o = at("hf_hi", HC * TA, BF16, o)
        hf_lo, o = at("hf_lo", HC * TA, BF16, o)
        outA, o = at("outA", 2 * D, F32, o)
        c_end = o
        sqA, sq_end = at("sqA", 1024, F32, max(a_end, b_end, c_end))
        assert sq_end <= arena_hi - arena, (
            a_end, b_end, c_end, sq_end, arena_hi - arena)

        ps0 = es.enter_context(nc.psum_tensor("ps0", [128, 2048], F32)).ap()
        ps1 = es.enter_context(nc.psum_tensor("ps1", [128, 1024], F32)).ap()
        ps2 = es.enter_context(nc.psum_tensor("ps2", [128, 1024], F32)).ap()

        plan = Plan(nc, es)
        P = plan.op

        def mm_group(psum_ap, pairs, waits, inc="te"):
            n = len(pairs)
            tok = None
            for i, (l, r) in enumerate(pairs):
                tok = P("tensor",
                        lambda e, l=l, r=r, st=(i == 0), sp=(i == n - 1):
                        e.matmul(psum_ap, l, r, start=st, stop=sp),
                        waits=waits if i == 0 else (),
                        inc=inc if i == n - 1 else None)
            return tok

        t_bq = P("sync", lambda e: e.dma_start(
            out=bq_sb[:, 0:16],
            in_=bqkv_h[0:2048].rearrange("(j p) -> p j", p=128),
            allow_slow_non_contiguous=True),
            inc="s_ld", dma=True)
        t_brow = P("sync", lambda e: e.dma_start(
            out=brow[0:1, 0:D],
            in_=bqkv_h[2048:3072].rearrange("(o d) -> o d", o=1)),
            inc="s_ld", dma=True)
        t_gam = P("sync", lambda e: e.dma_start(out=gam[:, :], in_=gamma_h[:, :]),
                  inc="s_ld", dma=True)

        t_id0 = P("gpsimd", lambda e: e.memset(ident[:], 0.0), inc="gp")
        t_id = P("gpsimd", lambda e: e.affine_select(
            out=ident[:], in_=ident[:], compare_op=OP.not_equal, fill=1.0,
            base=0, pattern=[[-1, 128]], channel_multiplier=1),
            waits=[t_id0], inc="gp")
        t_e0 = P("gpsimd", lambda e: e.iota(
            mask0[:], pattern=[[-1, 128]], base=0, channel_multiplier=1,
            allow_small_or_imprecise_dtypes=True), inc="gp")
        t_e = P("gpsimd", lambda e: e.tensor_scalar_max(mask0[:], mask0[:], 0.0),
                waits=[t_e0], inc="gp")
        t_on = P("vector", lambda e: e.memset(ones1[:, :], 1.0), inc="ve")
        t_lg = P("scalar", lambda e: e.activation(lg1[:, :], gam[:, :], AF.Ln),
                 waits=[t_gam], inc="sc")
        t_bc = []
        for hh in range(NH):
            t_bc.append(P("tensor", lambda e, hh=hh: e.matmul(
                ps1[:, hh * 512:(hh + 1) * 512], ones1[:, :],
                brow[0:1, hh * 512:(hh + 1) * 512], start=True, stop=True),
                waits=[t_on, t_brow], inc="te"))
        t_bvb = P("vector", lambda e: e.tensor_copy(bv_b[:, :], ps1[:, 0:D]),
                  waits=t_bc, inc="ve")
        t_brow2 = P("sync", lambda e: e.dma_start(
            out=brow[0:1, 0:D],
            in_=bproj_h[:].rearrange("(o d) -> o d", o=1)),
            waits=t_bc, inc="s_ld", dma=True)
        t_bc2 = []
        for hh in range(NH):
            t_bc2.append(P("tensor", lambda e, hh=hh: e.matmul(
                ps2[:, hh * 512:(hh + 1) * 512], ones1[:, :],
                brow[0:1, hh * 512:(hh + 1) * 512], start=True, stop=True),
                waits=[t_brow2], inc="te"))
        t_bcl = P("tensor", lambda e: e.matmul(
            ps0[:, 0:1], ones1[:, :], lg1[:, :], start=True, stop=True),
            waits=[t_on, t_lg], inc="te")
        t_bpb = P("vector", lambda e: e.tensor_copy(bp_b[:, :], ps2[:, 0:D]),
                  waits=t_bc2, inc="ve")
        t_lgb = P("vector", lambda e: e.tensor_copy(lgb[:, :], ps0[:, 0:1]),
                  waits=[t_bcl], inc="ve")
        t_m0 = P("scalar", lambda e: e.activation(
            maskt[:], mask0[:], AF.Exp, scale=lgb[:, :]),
            waits=[t_e, t_lgb], inc="sc")
        t_mask = P("gpsimd", lambda e: e.affine_select(
            out=maskt[:], in_=maskt[:], compare_op=OP.is_ge, fill=0.0,
            base=0, pattern=[[-1, 128]], channel_multiplier=1),
            waits=[t_m0], inc="gp")

        tr_done = {}
        t_evqk_prev = {}
        t_evv_bank = {}
        t_xn = {}
        t_xld2 = {}
        t_chain = {}
        tr_prev = [None]
        t_ev = {}
        t_ldstats = {}
        ps2c = ps2.rearrange("p (c t) -> p c t", c=DC)

        def emit_loads(st):
            if st >= NST:
                return
            s, t0 = (st * TA) // S, (st * TA) % S
            gate = [t_xn.get((st - 1, j)) for j in range(NSUB)]
            t_xld2[st] = [P("sync", lambda e, s=s, t0=t0, j=j: e.dma_start(
                out=xsA[:, j * D:(j + 1) * D],
                in_=x_h[s, t0 + j * 128:t0 + j * 128 + 128, :]),
                waits=gate, inc="s_lx", dma=True) for j in range(NSUB)]

        def emit_stats(st):
            if st >= NST:
                return
            t_x = t_xld2[st]
            t_sq, prev = [], None
            for j in range(NSUB):
                prev = P("scalar", lambda e, j=j: e.activation(
                    sqA[:, 0:D], xsA[:, j * D:(j + 1) * D], AF.Square,
                    accum_out=sumA_s[:, j:j + 1]),
                    waits=[t_x[j], prev], inc="sc")
                t_sq.append(prev)
            t_sum = []
            for j in range(NSUB):
                t_sum.append(P("vector", lambda e, j=j: e.reduce_sum(
                    sumA_v[:, j:j + 1], xsA[:, j * D:(j + 1) * D], axis=AX.X),
                    waits=[t_x[j]], inc="ve"))
            NSL = slice(0, NSUB)
            t_c1 = P("vector", lambda e: e.tensor_tensor(
                stA[:, 0:NSUB], sumA_v[:, NSL], sumA_v[:, NSL], OP.mult),
                waits=t_sum, inc="ve")
            t_c2 = P("vector", lambda e: e.scalar_tensor_tensor(
                stA[:, NSUB:2 * NSUB], stA[:, 0:NSUB], -1.0 / D,
                sumA_s[:, NSL], OP.mult, OP.add),
                waits=[t_c1] + t_sq, inc="ve")
            t_c3 = P("vector", lambda e: e.tensor_scalar(
                stA[:, 2 * NSUB:3 * NSUB], stA[:, NSUB:2 * NSUB], 1.0 / D, eps,
                OP.mult, OP.add), waits=[t_c2], inc="ve")
            t_c4 = P("scalar", lambda e: e.activation(
                stA_s[:, NSL], stA[:, 2 * NSUB:3 * NSUB], AF.Sqrt),
                waits=[t_c3], inc="sc")
            t_c5 = P("vector", lambda e: e.reciprocal(
                stA[:, 3 * NSUB:4 * NSUB], stA_s[:, NSL]),
                waits=[t_c4], inc="ve")
            t_c6 = P("vector", lambda e: e.scalar_tensor_tensor(
                stA[:, 0:NSUB], sumA_v[:, NSL], -1.0 / D,
                stA[:, 3 * NSUB:4 * NSUB], OP.mult, OP.mult),
                waits=[t_c5, t_c2], inc="ve")
            t_chain[st] = t_c6

        def emit_xn(st, j):
            t_xn[(st, j)] = P("vector", lambda e, j=j: e.tensor_scalar(
                xnA[:, :], xsA[:, j * D:(j + 1) * D],
                stA[:, 3 * NSUB + j:3 * NSUB + j + 1], stA[:, j:j + 1],
                OP.mult, OP.add),
                waits=[t_chain[st], tr_prev[0]], inc="ve")

        def emit_xn_tr(st, j):
            par = st % 2
            xb = par * DC * TA
            tr = None
            for dc in range(DC):
                tr = P("tensor", lambda e, dc=dc: e.transpose(
                    ps2[:, dc * 128:(dc + 1) * 128],
                    xnA[:, dc * 128:(dc + 1) * 128], ident[:]),
                    waits=([t_xn[(st, j)], t_id, t_bpb,
                            t_ev.get((st, j - 1), [None])[0],
                            t_ev.get((st, j - 1), [None, None])[1]]
                           if dc == 0 else ()),
                    inc="te" if dc == DC - 1 else None)
            tr_done[(st, j)] = tr
            tr_prev[0] = tr
            th = P("scalar", lambda e, j=j, xb=xb: e.copy(
                xnf_hi.rearrange("p (x c t) -> p x c t", x=2, c=DC)
                [:, st % 2, :, j * 128:(j + 1) * 128], ps2c[:, :, :]),
                waits=[tr], inc="sc")
            tl = P("vector", lambda e, j=j, xb=xb: e.tensor_tensor(
                xnf_lo.rearrange("p (x c t) -> p x c t", x=2, c=DC)
                [:, st % 2, :, j * 128:(j + 1) * 128], ps2c[:, :, :],
                xnf_hi.rearrange("p (x c t) -> p x c t", x=2, c=DC)
                [:, st % 2, :, j * 128:(j + 1) * 128], OP.subtract),
                waits=[tr, th], inc="ve")
            t_ev[(st, j)] = [th, tl]

        t_wsp_p = {0: [], 1: [], 2: []}
        slot_prev = {}
        t_xld2[0] = []
        i = 0
        for part in range(3):
            for dc in range(DC):
                slot = i % 6
                buf = wtmp if slot < 3 else wtmp2
                sb_ = buf[:, (slot % 3) * D:(slot % 3 + 1) * D]
                ph0, pl0 = slot_prev.get(slot, (None, None))
                hs2 = slice(dc * 3 * D + part * D,
                            dc * 3 * D + (part + 1) * D)
                t_l = P("sync", lambda e, dc=dc, part=part, sb_=sb_:
                        e.dma_start(
                            out=sb_, in_=wqkv_h[dc * 128:(dc + 1) * 128,
                                                part * D:(part + 1) * D]),
                        waits=[ph0, pl0],
                        inc="s_lw" if i % 2 == 0 else "s_lw2", dma=True)
                if i % 2 == 1 and i // 2 < NSUB:
                    t_xld2[0].append(P("sync", lambda e, j=i // 2:
                                       e.dma_start(
                                           out=xsA[:, j * D:(j + 1) * D],
                                           in_=x_h[0, j * 128:
                                                   j * 128 + 128, :]),
                                       inc="s_lx", dma=True))
                ph = P("scalar", lambda e, hs2=hs2, sb_=sb_: e.copy(
                    wq_hi[:, hs2], sb_), waits=[t_l], inc="sc")
                pl = P("vector", lambda e, hs2=hs2, sb_=sb_: e.tensor_tensor(
                    wq_lo[:, hs2], sb_, wq_hi[:, hs2], OP.subtract),
                    waits=[ph], inc="ve")
                slot_prev[slot] = (ph, pl)
                t_wsp_p[part] += [ph, pl]
                i += 1
        t_wsp_q = t_wsp_p[0] + [t_lgb]
        t_wsp_k = t_wsp_p[1]
        t_wsp_v = t_wsp_p[2] + [t_bvb]

        emit_stats(0)
        for j in range(NSUB):
            emit_xn(0, j)
            emit_xn_tr(0, j)

        t_wpsp_last = None
        wp_rd = {0: None, 1: None}
        wpprev = {0: (None, None), 1: (None, None)}
        wpst = {0: None, 1: None}
        wq_last = [t_wsp_p[2][-2], t_wsp_p[2][-1]]
        for dc in range(DC):
            par = dc % 2
            buf = wtmp if par == 0 else wtmp2
            vslice = slice(par * 1024, par * 1024 + 1024)
            ph0, pl0 = wpprev[par]
            t_l = P("sync", lambda e, dc=dc, buf=buf: e.dma_start(
                out=buf[:, 0:D], in_=wproj_h[dc * 128:(dc + 1) * 128, :]),
                waits=[ph0, pl0] + wq_last,
                inc="s_lw" if par == 0 else "s_lw2", dma=True)
            ph = P("scalar", lambda e, buf=buf, vslice=vslice: e.copy(
                vt_hi[:, vslice], buf[:, 0:D]),
                waits=[t_l, wpst[par]], inc="sc")
            pl = P("vector", lambda e, buf=buf, vslice=vslice: e.tensor_tensor(
                vt_lo[:, vslice], buf[:, 0:D], vt_hi[:, vslice], OP.subtract),
                waits=[ph], inc="ve")
            tst = None
            for l, srcv in ((0, vt_hi), (1, vt_lo)):
                tst = P("gpsimd", lambda e, dc=dc, l=l, srcv=srcv,
                        vslice=vslice: e.dma_start(
                            out=wp_scr[l, dc], in_=srcv[:, vslice]),
                        waits=[pl], inc="g_wps", dma=True)
            wpprev[par] = (ph, pl)
            wpst[par] = tst
            wp_rd[par] = pl
            t_wpsp_last = tst

        wgT_r = wgT_h.rearrange("(c p) (hc h) -> hc p c h", p=128, h=128)
        wuT_r = wuT_h.rearrange("(c p) (hc h) -> hc p c h", p=128, h=128)
        for hc in range(HC):
            P("gpsimd", lambda e, hc=hc: e.dma_start(
                out=wg_scr[hc], in_=wgT_r[hc]), inc="g_cv", dma=True)
            P("gpsimd", lambda e, hc=hc: e.dma_start(
                out=wu_scr[hc], in_=wuT_r[hc]), inc="g_cv", dma=True)
        n_cv = plan.cnt["g_cv"]

        t_store = {"q": wp_rd[0], "k": wp_rd[1], "v": None}
        t_vst = {(0, -2): t_wpsp_last, (0, -1): t_wpsp_last}

        for st in range(NST):
            s, t0 = (st * TA) // S, (st * TA) % S
            par = st % 2
            xb = par * DC * TA
            t_xnf = [t_ev[(st, j)][k] for j in range(NSUB) for k in (0, 1)]
            emit_loads(st + 1)

            wsrc = [(wq_hi, xnf_hi), (wq_hi, xnf_lo), (wq_lo, xnf_hi)]
            t_evqk = {}
            for ec in range(16):
                bank = ps0[:, (ec % 4) * 512:(ec % 4) * 512 + TA]
                pairs = [(wx[:, dc * 3 * D + ec * 128:dc * 3 * D + (ec + 1) * 128],
                          xy[:, xb + dc * TA:xb + (dc + 1) * TA])
                         for (wx, xy) in wsrc for dc in range(DC)]
                tmm = mm_group(bank, pairs,
                               waits=t_xnf + (t_wsp_q if ec < 8 else t_wsp_k)
                               + [t_evqk.get(ec - 4),
                                  t_evqk_prev.get(ec + 12)])
                dsth = qf_hi if ec < 8 else kf_hi
                dstl = qf_lo if ec < 8 else kf_lo
                cc = ec % 8
                bias = bq_sb[:, ec:ec + 1]
                th = P("scalar", lambda e, bank=bank, dsth=dsth, cc=cc,
                       bias=bias: e.add(
                           dsth[:, cc * TA:(cc + 1) * TA], bank, bias),
                       waits=[tmm, t_bq, t_store["q" if ec < 8 else "k"]],
                       inc="sc")
                tl = P("vector", lambda e, bank=bank, dsth=dsth, dstl=dstl,
                       cc=cc, bias=bias: e.scalar_tensor_tensor(
                           dstl[:, cc * TA:(cc + 1) * TA], bank, bias,
                           dsth[:, cc * TA:(cc + 1) * TA], OP.add, OP.subtract),
                       waits=[th], inc="ve")
                t_evqk[ec] = tl
                if ec == 3 and st + 1 < NST:
                    emit_stats(st + 1)
                if (ec % 4 == 0 and ec >= 4 and st + 1 < NST
                        and ec // 4 - 1 < NSUB):
                    emit_xn(st + 1, ec // 4 - 1)
                if (ec % 4 == 3 and ec >= 7 and st + 1 < NST
                        and ec // 4 - 1 < NSUB):
                    emit_xn_tr(st + 1, ec // 4 - 1)
            t_evqk_prev = t_evqk

            vsrc = [(xnf_hi, wq_hi), (xnf_lo, wq_hi), (xnf_hi, wq_lo)]
            t_evv = []
            for j in range(NSUB):
                vpar = j % 2
                for half in range(NH):
                    bank = ps1[:, half * 512:(half + 1) * 512]
                    pairs = [(xy[:, xb + dc * TA + j * 128:
                                 xb + dc * TA + (j + 1) * 128],
                              wx[:, dc * 3 * D + 2 * D + half * 512:
                                 dc * 3 * D + 2 * D + (half + 1) * 512])
                             for (xy, wx) in vsrc for dc in range(DC)]
                    tmm = mm_group(bank, pairs,
                                   waits=t_xnf + t_wsp_v +
                                   [t_evv_bank.get(half)])
                    vs = slice(vpar * D + half * 512, vpar * D + (half + 1) * 512)
                    th = P("scalar", lambda e, bank=bank, vs=vs: e.copy(
                        vt_hi[:, vs], bank),
                        waits=[tmm, t_vst.get((st, j - 2))], inc="sc")
                    tl = P("vector", lambda e, bank=bank, vs=vs:
                           e.tensor_tensor(vt_lo[:, vs], bank, vt_hi[:, vs],
                                           OP.subtract),
                           waits=[th], inc="ve")
                    t_evv += [th, tl]
                    t_evv_bank[half] = tl
                tvs = None
                for l, srcv in ((0, vt_hi), (1, vt_lo)):
                    tvs = P("scalar", lambda e, s=s, l=l, t0=t0, j=j,
                            srcv=srcv, vpar=vpar: e.dma_start(
                                out=v_scr[s, l, t0 // 128 + j, :, :],
                                in_=srcv[:, vpar * D:(vpar + 1) * D]),
                            waits=t_evv[-2:], inc="s_st", dma=True)
                t_vst[(st, j)] = tvs
                if j == 1 and st + 1 < NST and (st + 1, NSUB - 1) not in t_xn:
                    emit_xn(st + 1, NSUB - 1)
                    emit_xn_tr(st + 1, NSUB - 1)

            for (nm, srch, srcl, scr, evs) in (
                    ("q", qf_hi, qf_lo, q_scr, list(t_evqk.values())),
                    ("k", kf_hi, kf_lo, k_scr, list(t_evqk.values()))):
                sem = ("s_st" if st != NST - 1 else
                       ("s_st2" if nm == "q" else "s_st3"))
                tt = None
                for l, src in ((0, srch), (1, srcl)):
                    src_c = src.rearrange("p (c t) -> p c t", c=DC)
                    dst_c = scr[s, l, :, :, t0:t0 + TA].rearrange(
                        "c p t -> p c t")
                    tt = P("scalar", lambda e, src_c=src_c, dst_c=dst_c:
                           e.dma_start(out=dst_c, in_=src_c),
                           waits=evs, inc=sem, dma=True)
                t_store[nm] = tt
            if st == S // TA - 1:
                t_kst_s0 = t_store["k"]
                t_qst_s0 = t_store["q"]
                t_vst_s0 = t_vst[(st, NSUB - 1)]

        t_lastAmm = tmm
        t_qst_final = t_store["q"]
        t_kst_final = t_store["k"]
        t_evqk_last = list(t_evqk_prev.values())
        endA = plan.now("s_st", "s_st2", "s_st3", "te", "ve", "sc",
                        "s_lx", "s_lw")

        t_exp = {}
        t_lastArt = {}
        t_scmm = {}
        t_avmm = {}
        t_p1tok = {}
        t_p2tok = {}
        t_x1rd = {}
        t_qld = {}
        t_xld = {}
        t_kvld = {}
        t_stg2 = {}
        t_aofl = {}
        t_aoev = {}

        def emit_k(s, gate):
            t_kvld[s] = {}
            for kb in range(NKB):
                toks = {}
                for (dst, scr, l, eng, sem) in (
                        (k_hi, k_scr, 0, "sync", "s_lk"),
                        (k_lo, k_scr, 1, "gpsimd", "g_lk")):
                    toks[l] = P(eng, lambda e, dst=dst, scr=scr, l=l,
                                s=s, kb=kb: e.dma_start(
                                    out=dst.rearrange(
                                        "p (c s) -> p c s",
                                        c=DC)[:, :, kb * KW:(kb + 1) * KW],
                                    in_=scr[s, l].rearrange(
                                        "c p s -> p c s")
                                    [:, :, kb * KW:(kb + 1) * KW]),
                                waits=gate, inc=sem, dma=True)
                t_kvld[s][kb] = toks

        def emit_v(s, gate):
            t_vld[s] = {}
            for c in range(NVC):
                js = slice(c * 4, min((c + 1) * 4, NQT))
                toks = []
                for (dst, l, eng, sem) in ((v_hi, 0, "gpsimd", "s_lv"),
                                           (v_lo, 1, "sync", "s_lv2")):
                    toks.append(P(eng, lambda e, dst=dst, l=l, s=s, js=js:
                                  e.dma_start(
                                      out=dst.rearrange(
                                          "p (j d) -> p j d",
                                          j=NQT)[:, js, :],
                                      in_=v_scr[s, l].rearrange(
                                          "j p d -> p j d")[:, js, :]),
                                  waits=gate, inc=sem, dma=True))
                t_vld[s][c] = toks

        def prefetch_q(g):
            if g >= NG or g in t_qld:
                return
            s, i = g // NQT, g % NQT
            gq = ([t_scmm.get(g - 1)] +
                  ([t_lastAmm, t_qst_s0] if g == 0 else []))
            tq1 = P("sync", lambda e, s=s, i=i: e.dma_start(
                out=qt_hl.rearrange("p (l c t) -> p l c t", l=2, c=DC),
                in_=q_scr[s, :, :, :, i * 128:(i + 1) * 128].rearrange(
                    "l c p t -> p l c t")),
                waits=gq, inc="s_lq", dma=True)
            t_qld[g] = [tq1]

        def prefetch_x(g):
            if g >= NG or g in t_xld:
                return
            s, i = g // NQT, g % NQT
            t_xld[g] = P("sync", lambda e, s=s, i=i: e.dma_start(
                out=xres[:, :], in_=x_h[s, i * 128:(i + 1) * 128, :]),
                waits=[t_p2tok.get(g - 1)] + (endA if g == 0 else []),
                inc="s_lxb", dma=True)

        def emit_scores(g):
            s = g // NQT
            lastmm = None
            for kb in range(NKB):
                n = 0
                for (qx, kx) in ((qt_hi, k_hi), (qt_hi, k_lo), (qt_lo, k_hi)):
                    kl = 0 if kx is k_hi else 1
                    for dc in range(DC):
                        lhs = qx[:, dc * 128:(dc + 1) * 128]
                        n += 1
                        first, last = n == 1, n == 3 * DC
                        w = ()
                        if first:
                            w = (t_qld[g] + [t_kvld[s][kb][0],
                                             t_exp.get(g - 1)] +
                                 (t_evqk_last if g == 0 else []))
                        elif n == DC + 1:
                            w = (t_kvld[s][kb][1],)
                        lastmm = P("tensor", lambda e, lhs=lhs, kx=kx, dc=dc,
                                   kb=kb, st=first, sp=last: e.matmul(
                                       ps0[:, kb * 512:kb * 512 + KW], lhs,
                                       kx[:, dc * S + kb * KW:
                                          dc * S + (kb + 1) * KW],
                                       start=st, stop=sp),
                                   waits=w, inc="te" if last else None)
            t_scmm[g] = lastmm

        def emit_softmax(g):
            s, i = g // NQT, g % NQT
            t_mk = P("vector", lambda e, i=i: e.tensor_tensor(
                ps0[:, i * 128:(i + 1) * 128], ps0[:, i * 128:(i + 1) * 128],
                maskt[:], OP.mult),
                waits=[t_scmm[g], t_mask], inc="ve")
            t_mx = P("vector", lambda e: e.reduce_max(
                mx_b[:, 0:1], ps0[:, 0:S], axis=AX.X),
                waits=[t_mk, t_exp.get(g - 1)], inc="ve")
            t_ng = P("vector", lambda e: e.tensor_scalar_mul(
                mx_b[:, 1:2], mx_b[:, 0:1], -isqd),
                waits=[t_mx], inc="ve")
            te_ = P("scalar", lambda e: e.activation(
                a32[:, 0:S], ps0[:, 0:S], AF.Exp, bias=mx_b[:, 1:2],
                scale=isqd, accum_out=ell[:, 0:1]),
                waits=[t_ng, t_lastArt.get(g - 1), t_aoev.get(g - 1),
                       t_aofl.get(g - 1)] +
                ([t_kst_final] if g == 0 else []),
                inc="sc")
            t_exp[g] = te_
            t_il[g] = P("vector", lambda e: e.reciprocal(
                invl[:, 0:1], ell[:, 0:1]), waits=[te_], inc="ve")

        t_il = {}
        t_vld = {}
        pend_s2 = []
        t_s2done = [None]
        x1st = {}
        prefetch_q(0)
        emit_k(0, [t_lastAmm, t_kst_s0])
        emit_v(0, [t_lastAmm, t_vst_s0])
        t_wp = []
        for (dst, l, eng, sem, gw) in (
                (wp_hi, 0, "sync", "s_lw", [t_lastAmm]),
                (wp_lo, 1, "gpsimd", "g_lw", [t_lastAmm, t_qst_final])):
            t_wp.append(P(eng, lambda e, dst=dst, l=l: e.dma_start(
                out=dst.rearrange("p (c d) -> p c d", c=DC),
                in_=wp_scr[l].rearrange("c p d -> p c d")),
                waits=gw + [("g_wps", plan.cnt["g_wps"])],
                inc=sem, dma=True))
        prefetch_x(0)
        emit_scores(0)
        emit_softmax(0)
        for g in range(NG):
            s, i = g // NQT, g % NQT
            te_ = t_exp[g]
            t_at = []
            tr = None
            W2 = min(512, S)
            for r in range(S // W2):
                pp = (r % 2) * 512
                for q4 in range(W2 // 128):
                    kt = r * (W2 // 128) + q4
                    tr = P("tensor", lambda e, kt=kt, q4=q4, pp=pp: e.transpose(
                        ps2[:, pp + q4 * 128:pp + (q4 + 1) * 128],
                        a32[:, kt * 128:(kt + 1) * 128], ident[:]),
                        waits=([te_] + t_stg2.get((r - 2) % 2, []))
                        if q4 == 0 else (),
                        inc="te" if q4 == W2 // 128 - 1 else None)
                th = P("scalar", lambda e, r=r, pp=pp, W2=W2: e.copy(
                    at_hi[:, r * W2:(r + 1) * W2], ps2[:, pp:pp + W2]),
                    waits=[tr] + ([t_kst_final] if g == 0 else []),
                    inc="sc")
                tl = P("vector", lambda e, r=r, pp=pp, W2=W2: e.tensor_tensor(
                    at_lo[:, r * W2:(r + 1) * W2], ps2[:, pp:pp + W2],
                    at_hi[:, r * W2:(r + 1) * W2], OP.subtract),
                    waits=[tr, th], inc="ve")
                t_stg2[r % 2] = [th, tl]
                t_at += [th, tl]
            t_lastArt[g] = tr
            while pend_s2:
                t_s2done[0] = pend_s2.pop(0)()
                t_x1rd[g - 1][0] = t_s2done[0]
            lastav = None
            cnt = {h: 0 for h in range(NH)}
            for kt in range(NQT):
                rd = (kt * 128) // W2
                for (ax, vx) in ((at_hi, v_hi), (at_hi, v_lo), (at_lo, v_hi)):
                    lhs = ax[:, kt * 128:(kt + 1) * 128]
                    for half in range(NH):
                        cnt[half] += 1
                        first = cnt[half] == 1
                        last = cnt[half] == 3 * NQT
                        w = ()
                        if first:
                            w = (t_at[2 * rd:2 * rd + 2] +
                                 [t_p1tok.get(g - 1)] + t_vld[s][kt // 4] +
                                 (t_evv if g == 0 else []))
                        elif (kt % 4 == 0 and half == 0 and ax is at_hi
                              and vx is v_hi):
                            w = (tuple(t_at[2 * rd:2 * rd + 2]) +
                                 tuple(t_vld[s][kt // 4]))
                        lastav = P("tensor", lambda e, lhs=lhs, vx=vx, kt=kt,
                                   half=half, st=first, sp=last: e.matmul(
                                       ps1[:, half * 512:(half + 1) * 512],
                                       lhs,
                                       vx[:, kt * D + half * 512:
                                          kt * D + (half + 1) * 512],
                                       start=st, stop=sp),
                                   waits=w, inc="te" if last else None)
            t_avmm[g] = lastav
            if g + 1 < NG:
                prefetch_q(g + 1)
                if (g + 1) % NQT == 0:
                    emit_v(s + 1, [lastav])
                emit_scores(g + 1)
            if g + 2 < NG and (g + 2) % NQT == 0:
                emit_k(s + 1, [t_scmm[g + 1]])
            t_ao = P("vector", lambda e: e.scalar_tensor_tensor(
                ao32[:, :], ps1[:, :], invl[:, 0:1], bv_b[:, :],
                OP.mult, OP.add),
                waits=[lastav, t_il[g], t_bvb, t_aofl.get(g - 1)], inc="ve")
            t_aoev[g] = t_ao
            tr = None
            for r in range(2):
                pp = (r % 2) * 512
                for q4 in range(4):
                    dc = r * 4 + q4
                    tr = P("tensor", lambda e, dc=dc, q4=q4, pp=pp: e.transpose(
                        ps2[:, pp + q4 * 128:pp + (q4 + 1) * 128],
                        ao32[:, dc * 128:(dc + 1) * 128], ident[:]),
                        waits=([t_ao] + t_stg2.get((r - 2) % 2, []))
                        if q4 == 0 else (),
                        inc="te" if q4 == 3 else None)
                th = P("scalar", lambda e, r=r, pp=pp: e.copy(
                    aof_hi[:, r * 512:(r + 1) * 512], ps2[:, pp:pp + 512]),
                    waits=[tr] + (endA if g == 0 else []),
                    inc="sc")
                tl = P("vector", lambda e, r=r, pp=pp: e.tensor_tensor(
                    aof_lo[:, r * 512:(r + 1) * 512], ps2[:, pp:pp + 512],
                    aof_hi[:, r * 512:(r + 1) * 512], OP.subtract),
                    waits=[tr, th], inc="ve")
                t_stg2[r % 2] = [th, tl]
                if r == 1:
                    t_aof_ev = [th, tl]
            t_aofl[g] = t_aof_ev[1]
            if g + 1 < NG:
                emit_softmax(g + 1)
            lastpj = None
            cnt = {h: 0 for h in range(NH)}
            for (ax, wx) in ((aof_hi, wp_hi), (aof_hi, wp_lo),
                             (aof_lo, wp_hi)):
                for dc in range(DC):
                    lhs = ax[:, dc * 128:(dc + 1) * 128]
                    for half in range(NH):
                        cnt[half] += 1
                        first = cnt[half] == 1
                        last = cnt[half] == 3 * DC
                        w = (t_aof_ev + t_stg2[0] + t_stg2[1] + [t_ao] + t_wp) \
                            if first else ()
                        lastpj = P("tensor", lambda e, lhs=lhs, wx=wx, dc=dc,
                                   half=half, st=first, sp=last: e.matmul(
                                       ps1[:, half * 512:(half + 1) * 512],
                                       lhs,
                                       wx[:, dc * D + half * 512:
                                          dc * D + (half + 1) * 512],
                                       start=st, stop=sp),
                                   waits=w, inc="te" if last else None)
            t_p1 = P("vector", lambda e: e.tensor_tensor(
                x1t[:, :], ps1[:, :], bp_b[:, :], OP.add),
                waits=[lastpj, t_bpb] + t_x1rd.get(g - 1, []), inc="ve")
            t_p1tok[g] = t_p1
            t_p2 = P("vector", lambda e: e.tensor_tensor(
                x1t[:, :], x1t[:, :], xres[:, :], OP.add),
                waits=[t_p1, t_xld[g]], inc="ve")
            t_p2tok[g] = t_p2
            def s2fn(g=g, t_p2=t_p2):
                return P("scalar", lambda e, g=g, t_p2=t_p2: e.activation(
                    sqA[:, 0:D], x1t[:, :], AF.Square,
                    accum_out=sum2_s[:, g:g + 1]),
                    waits=[t_p2, t_s2done[0]], inc="sc")
            pend_s2.append(s2fn)
            t_r2 = P("vector", lambda e, g=g: e.reduce_sum(
                sum2_v[:, g:g + 1], x1t[:, :], axis=AX.X),
                waits=[t_p2], inc="ve")
            t_st = P("gpsimd", lambda e, s=s, i=i: e.dma_start(
                out=x1_scr[(s * S + i * 128):(s * S + (i + 1) * 128), :],
                in_=x1t[:, :]),
                waits=[t_p2], inc="s_sx1", dma=True)
            x1st[g] = t_st
            t_x1rd[g] = [t_r2, t_r2, t_st]
            prefetch_x(g + 1)

        while pend_s2:
            t_s2done[0] = pend_s2.pop(0)()
            t_x1rd[NG - 1][0] = t_s2done[0]

        endB = plan.now("te", "ve", "sc", "s_sx1", "s_lq", "s_lxb")

        t_scmm_last = t_scmm[NG - 1]
        t_avmm_last = t_avmm[NG - 1]
        wd_in = wdT_h.rearrange("(hc p) d -> p hc d", p=128)
        wd_out = wd_sb.rearrange("p (hc d) -> p hc d", hc=HC)
        t_wd = P("gpsimd", lambda e: e.dma_start(out=wd_out, in_=wd_in),
                 waits=[t_scmm_last], inc="g_cv", dma=True)
        t2a = P("vector", lambda e: e.tensor_tensor(
            st2[:, 0:NG], sum2_v[:, :], sum2_v[:, :], OP.mult),
            waits=endB, inc="ve")
        t2b = P("vector", lambda e: e.scalar_tensor_tensor(
            st2[:, NG:2 * NG], st2[:, 0:NG], -1.0 / D, sum2_s[:, :],
            OP.mult, OP.add), waits=[t2a], inc="ve")
        t2c = P("vector", lambda e: e.tensor_scalar(
            st2[:, 2 * NG:3 * NG], st2[:, NG:2 * NG], 1.0 / D, eps,
            OP.mult, OP.add), waits=[t2b], inc="ve")
        t2d = P("scalar", lambda e: e.activation(
            st2_s[:, :], st2[:, 2 * NG:3 * NG], AF.Sqrt),
            waits=[t2c], inc="sc")
        t2e = P("vector", lambda e: e.reciprocal(inv2[:, :], st2_s[:, :]),
                waits=[t2d], inc="ve")
        t2f = P("vector", lambda e: e.scalar_tensor_tensor(
            ngm2[:, :], sum2_v[:, :], -1.0 / D, inv2[:, :], OP.mult, OP.mult),
            waits=[t2e], inc="ve")

        t_wld = {}
        t_wmm = {}
        t_oev = {}
        t_ylast = {}
        t_xn2 = {}
        t_ev2 = {}
        t_hev_pair = []
        t_gu_prev = []
        t_gu_hist = {}
        t_stgC = [t_stg2.get(0, []) + t_stg2.get(1, [])]
        tr2_prev = [None]
        x1_readers = {0: [], 1: []}
        t_x1ld = {}
        x2fh_c = xn2f_hi.rearrange("p (x c t) -> p x c t", x=2, c=DC)
        x2fl_c = xn2f_lo.rearrange("p (x c t) -> p x c t", x=2, c=DC)

        def emit_x1_loads(u):
            if u >= NST or u in t_x1ld:
                return
            parU = u % 2
            t_x1ld[u] = [P("sync", lambda e, u=u, j=j, parU=parU: e.dma_start(
                out=x1A[:, (parU * NSUB + j) * D:(parU * NSUB + j + 1) * D],
                in_=x1_scr[u * TA + j * 128:u * TA + (j + 1) * 128, :]),
                waits=([t_p1tok[NG - 1], x1st[j]] if u == 0 else
                       [t_xn2.get((u - 2, j)), t_oev.get((u - 2, j))]),
                inc="s_lx1", dma=True) for j in range(NSUB)]

        def emit_xn2(u, j):
            parU = u % 2
            col = (u * TA) // 128 + j
            t_xn2[(u, j)] = P("vector", lambda e, j=j, parU=parU, col=col:
                              e.tensor_scalar(
                                  xn2[:, :],
                                  x1A[:, (parU * NSUB + j) * D:
                                      (parU * NSUB + j + 1) * D],
                                  inv2[:, col:col + 1], ngm2[:, col:col + 1],
                                  OP.mult, OP.add),
                              waits=[t_x1ld[u][j], t2f, tr2_prev[0]],
                              inc="ve")

        def emit_xn2_tr(u, j):
            parU = u % 2
            tr = None
            for dc in range(DC):
                tr = P("tensor", lambda e, dc=dc: e.transpose(
                    ps2[:, dc * 128:(dc + 1) * 128],
                    xn2[:, dc * 128:(dc + 1) * 128], ident[:]),
                    waits=([t_xn2[(u, j)]] + t_stgC[0] +
                           ([t_wmm.get((u - 1, HC - 1))] if j == 0 else []))
                    if dc == 0 else (),
                    inc="te" if dc == DC - 1 else None)
            tr2_prev[0] = tr
            th = P("scalar", lambda e, j=j, parU=parU: e.copy(
                x2fh_c[:, parU, :, j * 128:(j + 1) * 128], ps2c[:, :, :]),
                waits=[tr], inc="sc")
            tl = P("vector", lambda e, j=j, parU=parU: e.tensor_tensor(
                x2fl_c[:, parU, :, j * 128:(j + 1) * 128], ps2c[:, :, :],
                x2fh_c[:, parU, :, j * 128:(j + 1) * 128], OP.subtract),
                waits=[tr, th], inc="ve")
            t_ev2[(u, j)] = [th, tl]
            t_stgC[0] = [th, tl]

        emit_x1_loads(0)
        for j in range(NSUB):
            emit_xn2(0, j)
            emit_xn2_tr(0, j)

        for u in range(NST):
            parU = u % 2
            xb2 = parU * DC * TA
            s, t0 = (u * TA) // S, (u * TA) % S
            t_xnf2 = [t_ev2[(u, j)][k] for j in range(NSUB) for k in (0, 1)]
            emit_x1_loads(u + 1)

            t_hev = []
            for hc in range(HC):
                parW = hc % 2
                for look in ([0, 1] if hc == 0 else [1]):
                    hh = hc + look
                    if hh >= HC or (u, hh) in t_wld:
                        continue
                    pw = hh % 2
                    gw = [t_wmm.get((u, hh - 2))]
                    if hh < 2:
                        gw = [t_wmm.get((u - 1, HC - 2 + hh))]
                        if u == 0:
                            gw = [("g_cv", n_cv), t_avmm_last]
                    tg = P("sync", lambda e, hh=hh, pw=pw: e.dma_start(
                        out=wgbuf[:, pw * D:(pw + 1) * D].rearrange(
                            "p (c h) -> p c h", c=DC),
                        in_=wg_scr[hh]),
                        waits=gw, inc="s_lwg", dma=True)
                    tu = P("sync", lambda e, hh=hh, pw=pw: e.dma_start(
                        out=wubuf[:, pw * D:(pw + 1) * D].rearrange(
                            "p (c h) -> p c h", c=DC),
                        in_=wu_scr[hh]),
                        waits=gw, inc="s_lwu", dma=True)
                    t_wld[(u, hh)] = [tg, tu]
                tw = t_wld[(u, hc)]
                gbank = ps0[:, parW * 1024:parW * 1024 + TA]
                ubank = ps0[:, parW * 1024 + 512:parW * 1024 + 512 + TA]
                pairs_g = [(wgbuf[:, parW * D + dc * 128:
                                  parW * D + (dc + 1) * 128],
                            xy[:, xb2 + dc * TA:xb2 + (dc + 1) * TA])
                           for xy in (xn2f_hi, xn2f_lo) for dc in range(DC)]
                pairs_u = [(wubuf[:, parW * D + dc * 128:
                                  parW * D + (dc + 1) * 128],
                            xy[:, xb2 + dc * TA:xb2 + (dc + 1) * TA])
                           for xy in (xn2f_hi, xn2f_lo) for dc in range(DC)]
                ghc = u * HC + hc
                tmm_g = mm_group(gbank, pairs_g,
                                 waits=t_xnf2 + tw + t_gu_hist.get(ghc - 2, []) +
                                 ([t_oev.get((u - 1, 1))] if hc == 0 else []) +
                                 ([t_oev.get((u - 1, 3))]
                                  if hc == 1 and NSUB == 4 else []))
                tmm_u = mm_group(ubank, pairs_u, waits=[])
                t_wmm[(u, hc)] = tmm_u
                t_sg = P("scalar", lambda e, gbank=gbank: e.activation(
                    sgb[:, :], gbank, AF.Silu, scale=1.0 / 127.0),
                    waits=[tmm_g] + ([t_gu_prev[1]] if t_gu_prev else []),
                    inc="sc")
                t_h = P("vector", lambda e, ubank=ubank: e.scalar_tensor_tensor(
                    h32[:, :], ubank, 1.0 / 127.0, sgb[:, :],
                    OP.mult, OP.mult),
                    waits=[tmm_u, t_sg] + t_hev_pair, inc="ve")
                t_hh = P("scalar", lambda e, hc=hc: e.copy(
                    hf_hi[:, hc * TA:(hc + 1) * TA], h32[:, :]),
                    waits=[t_h, t_oev.get((u - 1, NSUB - 1))], inc="sc")
                t_hl = P("vector", lambda e, hc=hc: e.tensor_tensor(
                    hf_lo[:, hc * TA:(hc + 1) * TA], h32[:, :],
                    hf_hi[:, hc * TA:(hc + 1) * TA], OP.subtract),
                    waits=[t_hh], inc="ve")
                t_hev_pair = [t_hh, t_hl]
                t_hev += [t_hh, t_hl]
                t_gu_prev = [t_sg, t_h]
                t_gu_hist[ghc] = [t_sg, t_h]
                if u + 1 < NST and HC >= 12:
                    jj = hc // (HC // NSUB)
                    ph_in = hc % (HC // NSUB)
                    if ph_in == 1:
                        emit_xn2(u + 1, jj)
                    elif ph_in == 4:
                        emit_xn2_tr(u + 1, jj)

            for j in range(NSUB):
                psd = ps1 if j % 2 == 0 else (
                    ps0[:, 0:1024] if j % 4 == 1 else ps0[:, 1024:2048])
                lastdn = None
                cnt = {h: 0 for h in range(NH)}
                for hx in (hf_hi, hf_lo):
                    for hc in range(HC):
                        lhs = hx[:, hc * TA + j * 128:hc * TA + (j + 1) * 128]
                        for half in range(NH):
                            cnt[half] += 1
                            first = cnt[half] == 1
                            last = cnt[half] == 2 * HC
                            w = ()
                            if first:
                                w = t_hev + [t_wd, t_oev.get((u, j - 2))]
                                if j % 2 == 1:
                                    w = w + t_gu_hist[
                                        u * HC + HC - (2 if j % 4 == 1 else 1)]
                            lastdn = P("tensor", lambda e, lhs=lhs, hc=hc,
                                       half=half, psd=psd, st=first, sp=last:
                                       e.matmul(
                                           psd[:, half * 512:(half + 1) * 512],
                                           lhs,
                                           wd_sb[:, hc * D + half * 512:
                                                 hc * D + (half + 1) * 512],
                                           start=st, stop=sp),
                                       waits=w, inc="te" if last else None)
                to = P("vector", lambda e, j=j, parU=parU, psd=psd:
                       e.scalar_tensor_tensor(
                           outA[:, (j % 2) * D:(j % 2 + 1) * D], psd[:, 0:D],
                           1.0 / 127.0,
                           x1A[:, (parU * NSUB + j) * D:
                               (parU * NSUB + j + 1) * D],
                           OP.mult, OP.add),
                       waits=[lastdn, t_ylast.get((u, j - 2)),
                              t_ylast.get((u - 1, j + 2))], inc="ve")
                t_oev[(u, j)] = to
                ty = P("scalar", lambda e, s=s, t0=t0, j=j: e.dma_start(
                    out=y_h[s, t0 + j * 128:t0 + j * 128 + 128, :],
                    in_=outA[:, (j % 2) * D:(j % 2 + 1) * D]),
                    waits=[to], inc="s_sy", dma=True)
                t_ylast[(u, j)] = ty
            for hh in ((0, 1) if u + 1 < NST else ()):
                if hh >= HC or (u + 1, hh) in t_wld:
                    continue
                pw = hh % 2
                gw = [t_wmm.get((u, HC - 2 + hh))]
                tg = P("sync", lambda e, hh=hh, pw=pw: e.dma_start(
                    out=wgbuf[:, pw * D:(pw + 1) * D].rearrange(
                        "p (c h) -> p c h", c=DC),
                    in_=wg_scr[hh]),
                    waits=gw, inc="s_lwg", dma=True)
                tu = P("sync", lambda e, hh=hh, pw=pw: e.dma_start(
                    out=wubuf[:, pw * D:(pw + 1) * D].rearrange(
                        "p (c h) -> p c h", c=DC),
                    in_=wu_scr[hh]),
                    waits=gw, inc="s_lwu", dma=True)
                t_wld[(u + 1, hh)] = [tg, tu]
            if u + 1 < NST and HC < 12:
                for j in range(NSUB):
                    emit_xn2(u + 1, j)
                    emit_xn2_tr(u + 1, j)

        n_y = plan.cnt["s_sy"]
        plan.ops["sync"].append(
            (lambda e: e.nop(), (("s_sy", n_y),), None, 1))

        block = es.enter_context(nc.Block())

        @block.sync
        def _(e):
            plan.play("sync", e)

        @block.gpsimd
        def _(e):
            plan.play("gpsimd", e)

        @block.scalar
        def _(e):
            plan.play("scalar", e)

        @block.vector
        def _(e):
            plan.play("vector", e)

        @block.tensor
        def _(e):
            plan.play("tensor", e)

    return nc


def kernel(L0_attn_x, L0_attn_Wqkv, L0_attn_bqkv, L0_attn_Wproj, L0_attn_bproj,
           L0_attn_gamma, L0_ffn_w_gate, L0_ffn_w_up, L0_ffn_w_down):
    x = np.ascontiguousarray(np.asarray(L0_attn_x, dtype=np.float32))
    Bfull, S, D = x.shape
    ncore = 8
    Bc = Bfull // ncore
    H = L0_ffn_w_gate.shape[0]

    wqkv = np.ascontiguousarray(np.asarray(L0_attn_Wqkv, dtype=np.float32))
    bqkv = np.ascontiguousarray(np.asarray(L0_attn_bqkv, dtype=np.float32))
    wproj = np.ascontiguousarray(np.asarray(L0_attn_Wproj, dtype=np.float32))
    bproj = np.ascontiguousarray(np.asarray(L0_attn_bproj, dtype=np.float32))
    gamma = np.asarray(L0_attn_gamma, dtype=np.float32).reshape(1, 1)
    wgT = np.ascontiguousarray(np.asarray(L0_ffn_w_gate, dtype=np.int32).T)
    wuT = np.ascontiguousarray(np.asarray(L0_ffn_w_up, dtype=np.int32).T)
    wdT = np.ascontiguousarray(np.asarray(L0_ffn_w_down, dtype=np.int32).T)

    nc = build_program(B=Bc, S=S, D=D, H=H)
    in_maps = []
    for c in range(ncore):
        in_maps.append({
            "x": np.ascontiguousarray(x[c * Bc:(c + 1) * Bc]),
            "wqkv": wqkv, "bqkv": bqkv, "wproj": wproj, "bproj": bproj,
            "gamma": gamma, "wgT": wgT, "wuT": wuT, "wdT": wdT,
        })
    res = run_bass_kernel_spmd(nc, in_maps, core_ids=list(range(ncore)))
    out = np.concatenate([r["y"] for r in res.results], axis=0)
    return out.astype(np.float32)


# version 24
# speedup vs baseline: 1.0744x; 1.0013x over previous
import numpy as np
from contextlib import ExitStack

import concourse.bass as bass
import concourse.mybir as mybir
from concourse.bass_utils import run_bass_kernel_spmd

F32 = mybir.dt.float32
BF16 = mybir.dt.bfloat16
I32 = mybir.dt.int32
AF = mybir.ActivationFunctionType
AX = mybir.AxisListType
OP = mybir.AluOpType

ENGS = ("sync", "gpsimd", "scalar", "vector", "tensor")


class Plan:

    def __init__(self, nc, es):
        self.nc = nc
        self.es = es
        self.ops = {e: [] for e in ENGS}
        self.cnt = {}
        self.sems = {}

    def sem(self, name):
        if name not in self.sems:
            self.sems[name] = self.es.enter_context(self.nc.semaphore(name))
            self.cnt[name] = 0
        return self.sems[name]

    def op(self, eng, fn, waits=(), inc=None, dma=False):
        tok = None
        extra = ()
        if inc is not None:
            self.sem(inc)
            if dma and self.cnt[inc] > 0:
                extra = ((inc, self.cnt[inc]),)
            self.cnt[inc] += 16 if dma else 1
            tok = (inc, self.cnt[inc])
        ws = tuple(t for t in waits if t is not None) + extra
        self.ops[eng].append((fn, ws, inc, 16 if dma else 1))
        return tok

    def now(self, *names):
        return [(n, self.cnt[n]) for n in names if self.cnt.get(n, 0) > 0]

    def play(self, engname, eng):
        seen = {}
        for fn, waits, incname, n in self.ops[engname]:
            for (s, v) in waits:
                if v > 0 and seen.get(s, 0) < v:
                    eng.wait_ge(self.sems[s], v)
                    seen[s] = v
            ins = fn(eng)
            if incname is not None:
                ins.then_inc(self.sems[incname], n)


def build_program(B=2, S=2048, D=1024, H=3072):
    nc = bass.Bass("TRN2", target_bir_lowering=False)
    DC = D // 128
    HC = H // 128
    NQT = S // 128
    NG = B * NQT
    TA = min(512, S)
    NSUB = TA // 128
    NST = (B * S) // TA
    NKB = max(S // 512, 1)
    KW = min(512, S)
    NVC = max(S // 512, 1)
    NH = D // 512
    NR = max(S // 1024, 1)
    NTR = min(S // 128, 8)
    eps = 1e-5
    isqd = float(1.0 / np.sqrt(np.float32(D)))

    x_h = nc.dram_tensor("x", [B, S, D], F32, kind="ExternalInput").ap()
    wqkv_h = nc.dram_tensor("wqkv", [D, 3 * D], F32, kind="ExternalInput").ap()
    bqkv_h = nc.dram_tensor("bqkv", [3 * D], F32, kind="ExternalInput").ap()
    wproj_h = nc.dram_tensor("wproj", [D, D], F32, kind="ExternalInput").ap()
    bproj_h = nc.dram_tensor("bproj", [D], F32, kind="ExternalInput").ap()
    gamma_h = nc.dram_tensor("gamma", [1, 1], F32, kind="ExternalInput").ap()
    wgT_h = nc.dram_tensor("wgT", [D, H], I32, kind="ExternalInput").ap()
    wuT_h = nc.dram_tensor("wuT", [D, H], I32, kind="ExternalInput").ap()
    wdT_h = nc.dram_tensor("wdT", [H, D], I32, kind="ExternalInput").ap()
    y_h = nc.dram_tensor("y", [B, S, D], F32, kind="ExternalOutput").ap()

    q_scr = nc.dram_tensor("q_scr", [B, 2, DC, 128, S], BF16, kind="Internal").ap()
    k_scr = nc.dram_tensor("k_scr", [B, 2, DC, 128, S], BF16, kind="Internal").ap()
    v_scr = nc.dram_tensor("v_scr", [B, 2, NQT, 128, D], BF16, kind="Internal").ap()
    x1_scr = nc.dram_tensor("x1_scr", [B * S, D], F32, kind="Internal").ap()
    wg_scr = nc.dram_tensor("wg_scr", [HC, 128, DC, 128], BF16, kind="Internal").ap()
    wu_scr = nc.dram_tensor("wu_scr", [HC, 128, DC, 128], BF16, kind="Internal").ap()
    wp_scr = nc.dram_tensor("wp_scr", [2, DC, 128, D], BF16, kind="Internal").ap()

    es = ExitStack()
    with es:
        def sb(name, shape, dtype):
            return nc.alloc_sbuf_tensor(name, shape, dtype).ap()

        ident = sb("ident", [128, 128], F32)
        maskt = sb("maskt", [128, 128], F32)
        bv_b = sb("bv_b", [128, D], F32)
        bp_b = sb("bp_b", [128, D], F32)
        gam = sb("gam", [1, 1], F32)
        lg1 = sb("lg1", [1, 1], F32)
        lgb = sb("lgb", [128, 1], F32)
        sumA_v = sb("sumA_v", [128, NSUB], F32)
        sumA_s = sb("sumA_s", [128, NSUB], F32)
        stA = sb("stA", [128, 4 * NSUB], F32)
        stA_s = sb("stA_s", [128, NSUB], F32)
        sum2_v = sb("sum2_v", [128, NG], F32)
        sum2_s = sb("sum2_s", [128, NG], F32)
        st2 = sb("st2", [128, 3 * NG], F32)
        st2_s = sb("st2_s", [128, NG], F32)
        inv2 = sb("inv2", [128, NG], F32)
        ngm2 = sb("ngm2", [128, NG], F32)
        mx_b = sb("mx_b", [128, 2], F32)
        ell = sb("ell", [128, 1], F32)
        invl = sb("invl", [128, 1], F32)

        rem = nc.sbuf_bytes_remaining
        got = nc.bump_sbuf(rem - 64)
        assert got is not None
        arena, arena_hi = got

        starts = {}

        def at(name, cols, dtype, off):
            a = (off + 31) // 32 * 32
            t = nc.alloc_sbuf_tensor_at(name, [128, cols], dtype,
                                        offset=arena + a)
            starts[name] = a
            return t.ap(), a + cols * mybir.dt.size(dtype)

        o = 0
        wq_hi, o = at("wq_hi", DC * 3 * D, BF16, o)
        wq_lo, o = at("wq_lo", DC * 3 * D, BF16, o)
        xsA, o = at("xsA", NSUB * D, F32, o)
        xnA, o = at("xnA", D, F32, o)
        xnf_hi, o = at("xnf_hi", 2 * DC * TA, BF16, o)
        xnf_lo, o = at("xnf_lo", 2 * DC * TA, BF16, o)
        qf_hi, o = at("qf_hi", DC * TA, BF16, o)
        qf_lo, o = at("qf_lo", DC * TA, BF16, o)
        kf_hi, o = at("kf_hi", DC * TA, BF16, o)
        kf_lo, o = at("kf_lo", DC * TA, BF16, o)
        vt_hi, o = at("vt_hi", 2 * D, BF16, o)
        vt_lo, o = at("vt_lo", 2 * D, BF16, o)
        brow, o = at("brow", D, F32, o)
        brow = brow[0:1, :]
        ones1, o = at("ones1", 128, F32, o)
        ones1 = ones1[0:1, :]
        mask0, o = at("mask0", 128, F32, o)
        bq_sb, o = at("bq_sb", 16, F32, o)
        if 2 * DC * TA * 2 >= 3 * D * 4:
            wtmp, _ = at("wtmp", 3 * D, F32, starts["qf_hi"])
            wtmp2, _ = at("wtmp2", 3 * D, F32, starts["kf_hi"])
            a_end = o
        else:
            wtmp, o = at("wtmp", 3 * D, F32, o)
            wtmp2, o = at("wtmp2", 3 * D, F32, o)
            a_end = o
        o = 0
        qt_hl, o = at("qt_hl", 2 * D, BF16, o)
        qt_hi = qt_hl[:, 0:D]
        qt_lo = qt_hl[:, D:2 * D]
        k_hi, o = at("k_hi", DC * S, BF16, o)
        k_lo, o = at("k_lo", DC * S, BF16, o)
        v_hi, o = at("v_hi", NQT * D, BF16, o)
        v_lo, o = at("v_lo", NQT * D, BF16, o)
        wp_hi, o = at("wp_hi", DC * D, BF16, o)
        wp_lo, o = at("wp_lo", DC * D, BF16, o)
        a32, o = at("a32", S, F32, o)
        at_hi, o = at("at_hi", S, BF16, o)
        at_lo, o = at("at_lo", S, BF16, o)
        if S >= D:
            ao32, _ = at("ao32", D, F32, starts["a32"])
        else:
            ao32, o = at("ao32", D, F32, o)
        aof_hi, o = at("aof_hi", D, BF16, o)
        aof_lo, o = at("aof_lo", D, BF16, o)
        xres, o = at("xres", D, F32, o)
        x1t, o = at("x1t", D, F32, o)
        b_end = o
        o = 0
        wd_sb, o = at("wd_sb", HC * D, BF16, o)
        wgbuf, o = at("wgbuf", 2 * D, BF16, o)
        wubuf, o = at("wubuf", 2 * D, BF16, o)
        x1A, o = at("x1A", 2 * NSUB * D, F32, o)
        xn2, o = at("xn2", D, F32, o)
        xn2f_hi, o = at("xn2f_hi", 2 * DC * TA, BF16, o)
        xn2f_lo, o = at("xn2f_lo", 2 * DC * TA, BF16, o)
        sgb, o = at("sgb", TA, F32, o)
        h32, o = at("h32", TA, F32, o)
        hf_hi, o = at("hf_hi", HC * TA, BF16, o)
        hf_lo, o = at("hf_lo", HC * TA, BF16, o)
        outA, o = at("outA", 2 * D, F32, o)
        c_end = o
        sqA, sq_end = at("sqA", 1024, F32, max(a_end, b_end, c_end))
        assert sq_end <= arena_hi - arena, (
            a_end, b_end, c_end, sq_end, arena_hi - arena)

        ps0 = es.enter_context(nc.psum_tensor("ps0", [128, 2048], F32)).ap()
        ps1 = es.enter_context(nc.psum_tensor("ps1", [128, 1024], F32)).ap()
        ps2 = es.enter_context(nc.psum_tensor("ps2", [128, 1024], F32)).ap()

        plan = Plan(nc, es)
        P = plan.op

        def mm_group(psum_ap, pairs, waits, inc="te"):
            n = len(pairs)
            tok = None
            for i, (l, r) in enumerate(pairs):
                tok = P("tensor",
                        lambda e, l=l, r=r, st=(i == 0), sp=(i == n - 1):
                        e.matmul(psum_ap, l, r, start=st, stop=sp),
                        waits=waits if i == 0 else (),
                        inc=inc if i == n - 1 else None)
            return tok

        t_bq = P("sync", lambda e: e.dma_start(
            out=bq_sb[:, 0:16],
            in_=bqkv_h[0:2048].rearrange("(j p) -> p j", p=128),
            allow_slow_non_contiguous=True),
            inc="s_ld", dma=True)
        t_brow = P("sync", lambda e: e.dma_start(
            out=brow[0:1, 0:D],
            in_=bqkv_h[2048:3072].rearrange("(o d) -> o d", o=1)),
            inc="s_ld", dma=True)
        t_gam = P("sync", lambda e: e.dma_start(out=gam[:, :], in_=gamma_h[:, :]),
                  inc="s_ld", dma=True)

        t_id0 = P("gpsimd", lambda e: e.memset(ident[:], 0.0), inc="gp")
        t_id = P("gpsimd", lambda e: e.affine_select(
            out=ident[:], in_=ident[:], compare_op=OP.not_equal, fill=1.0,
            base=0, pattern=[[-1, 128]], channel_multiplier=1),
            waits=[t_id0], inc="gp")
        t_e0 = P("gpsimd", lambda e: e.iota(
            mask0[:], pattern=[[-1, 128]], base=0, channel_multiplier=1,
            allow_small_or_imprecise_dtypes=True), inc="gp")
        t_e = P("gpsimd", lambda e: e.tensor_scalar_max(mask0[:], mask0[:], 0.0),
                waits=[t_e0], inc="gp")
        t_on = P("vector", lambda e: e.memset(ones1[:, :], 1.0), inc="ve")
        t_lg = P("scalar", lambda e: e.activation(lg1[:, :], gam[:, :], AF.Ln),
                 waits=[t_gam], inc="sc")
        t_bc = []
        for hh in range(NH):
            t_bc.append(P("tensor", lambda e, hh=hh: e.matmul(
                ps1[:, hh * 512:(hh + 1) * 512], ones1[:, :],
                brow[0:1, hh * 512:(hh + 1) * 512], start=True, stop=True),
                waits=[t_on, t_brow], inc="te"))
        t_bvb = P("vector", lambda e: e.tensor_copy(bv_b[:, :], ps1[:, 0:D]),
                  waits=t_bc, inc="ve")
        t_brow2 = P("sync", lambda e: e.dma_start(
            out=brow[0:1, 0:D],
            in_=bproj_h[:].rearrange("(o d) -> o d", o=1)),
            waits=t_bc, inc="s_ld", dma=True)
        t_bc2 = []
        for hh in range(NH):
            t_bc2.append(P("tensor", lambda e, hh=hh: e.matmul(
                ps2[:, hh * 512:(hh + 1) * 512], ones1[:, :],
                brow[0:1, hh * 512:(hh + 1) * 512], start=True, stop=True),
                waits=[t_brow2], inc="te"))
        t_bcl = P("tensor", lambda e: e.matmul(
            ps0[:, 0:1], ones1[:, :], lg1[:, :], start=True, stop=True),
            waits=[t_on, t_lg], inc="te")
        t_bpb = P("vector", lambda e: e.tensor_copy(bp_b[:, :], ps2[:, 0:D]),
                  waits=t_bc2, inc="ve")
        t_lgb = P("vector", lambda e: e.tensor_copy(lgb[:, :], ps0[:, 0:1]),
                  waits=[t_bcl], inc="ve")
        t_m0 = P("scalar", lambda e: e.activation(
            maskt[:], mask0[:], AF.Exp, scale=lgb[:, :]),
            waits=[t_e, t_lgb], inc="sc")
        t_mask = P("gpsimd", lambda e: e.affine_select(
            out=maskt[:], in_=maskt[:], compare_op=OP.is_ge, fill=0.0,
            base=0, pattern=[[-1, 128]], channel_multiplier=1),
            waits=[t_m0], inc="gp")

        tr_done = {}
        t_evqk_prev = {}
        t_evv_bank = {}
        t_xn = {}
        t_xld2 = {}
        t_chain = {}
        tr_prev = [None]
        t_ev = {}
        t_ldstats = {}
        ps2c = ps2.rearrange("p (c t) -> p c t", c=DC)

        def emit_loads(st):
            if st >= NST:
                return
            s, t0 = (st * TA) // S, (st * TA) % S
            gate = [t_xn.get((st - 1, j)) for j in range(NSUB)]
            t_xld2[st] = [P("sync", lambda e, s=s, t0=t0, j=j: e.dma_start(
                out=xsA[:, j * D:(j + 1) * D],
                in_=x_h[s, t0 + j * 128:t0 + j * 128 + 128, :]),
                waits=gate, inc="s_lx", dma=True) for j in range(NSUB)]

        def emit_sq(st, j, prev):
            return P("scalar", lambda e, j=j: e.activation(
                sqA[:, 0:D], xsA[:, j * D:(j + 1) * D], AF.Square,
                accum_out=sumA_s[:, j:j + 1]),
                waits=[t_xld2[st][j], prev], inc="sc")

        def emit_rs(st, j):
            return P("vector", lambda e, j=j: e.reduce_sum(
                sumA_v[:, j:j + 1], xsA[:, j * D:(j + 1) * D], axis=AX.X),
                waits=[t_xld2[st][j]], inc="ve")

        def emit_stats(st):
            if st >= NST:
                return
            t_sq, prev = [], None
            for j in range(NSUB):
                prev = emit_sq(st, j, prev)
                t_sq.append(prev)
            t_sum = [emit_rs(st, j) for j in range(NSUB)]
            emit_stats_chain(st, t_sq, t_sum)

        def emit_stats_chain(st, t_sq, t_sum):
            NSL = slice(0, NSUB)
            t_c1 = P("vector", lambda e: e.tensor_tensor(
                stA[:, 0:NSUB], sumA_v[:, NSL], sumA_v[:, NSL], OP.mult),
                waits=t_sum, inc="ve")
            t_c2 = P("vector", lambda e: e.scalar_tensor_tensor(
                stA[:, NSUB:2 * NSUB], stA[:, 0:NSUB], -1.0 / D,
                sumA_s[:, NSL], OP.mult, OP.add),
                waits=[t_c1] + t_sq, inc="ve")
            t_c3 = P("vector", lambda e: e.tensor_scalar(
                stA[:, 2 * NSUB:3 * NSUB], stA[:, NSUB:2 * NSUB], 1.0 / D, eps,
                OP.mult, OP.add), waits=[t_c2], inc="ve")
            t_c4 = P("scalar", lambda e: e.activation(
                stA_s[:, NSL], stA[:, 2 * NSUB:3 * NSUB], AF.Sqrt),
                waits=[t_c3], inc="sc")
            t_c5 = P("vector", lambda e: e.reciprocal(
                stA[:, 3 * NSUB:4 * NSUB], stA_s[:, NSL]),
                waits=[t_c4], inc="ve")
            t_c6 = P("vector", lambda e: e.scalar_tensor_tensor(
                stA[:, 0:NSUB], sumA_v[:, NSL], -1.0 / D,
                stA[:, 3 * NSUB:4 * NSUB], OP.mult, OP.mult),
                waits=[t_c5, t_c2], inc="ve")
            t_chain[st] = t_c6

        def emit_xn(st, j):
            t_xn[(st, j)] = P("vector", lambda e, j=j: e.tensor_scalar(
                xnA[:, :], xsA[:, j * D:(j + 1) * D],
                stA[:, 3 * NSUB + j:3 * NSUB + j + 1], stA[:, j:j + 1],
                OP.mult, OP.add),
                waits=[t_chain[st], tr_prev[0]], inc="ve")

        def emit_xn_tr(st, j):
            par = st % 2
            xb = par * DC * TA
            tr = None
            for dc in range(DC):
                tr = P("tensor", lambda e, dc=dc: e.transpose(
                    ps2[:, dc * 128:(dc + 1) * 128],
                    xnA[:, dc * 128:(dc + 1) * 128], ident[:]),
                    waits=([t_xn[(st, j)], t_id, t_bpb,
                            t_ev.get((st, j - 1), [None])[0],
                            t_ev.get((st, j - 1), [None, None])[1]]
                           if dc == 0 else ()),
                    inc="te" if dc == DC - 1 else None)
            tr_done[(st, j)] = tr
            tr_prev[0] = tr
            th = P("scalar", lambda e, j=j, xb=xb: e.copy(
                xnf_hi.rearrange("p (x c t) -> p x c t", x=2, c=DC)
                [:, st % 2, :, j * 128:(j + 1) * 128], ps2c[:, :, :]),
                waits=[tr], inc="sc")
            tl = P("vector", lambda e, j=j, xb=xb: e.tensor_tensor(
                xnf_lo.rearrange("p (x c t) -> p x c t", x=2, c=DC)
                [:, st % 2, :, j * 128:(j + 1) * 128], ps2c[:, :, :],
                xnf_hi.rearrange("p (x c t) -> p x c t", x=2, c=DC)
                [:, st % 2, :, j * 128:(j + 1) * 128], OP.subtract),
                waits=[tr, th], inc="ve")
            t_ev[(st, j)] = [th, tl]

        t_wsp_p = {0: [], 1: [], 2: []}
        slot_prev = {}
        slot_seq = ([0, 1, 2, 3, 4, 5, 0, 1] + [2, 3] * 4 + [4, 5] * 8)
        t_xld2[0] = []
        sq_prev = [None]
        t_sq0, t_sum0 = [], []
        t_wpsp_last = None
        wp_rd = {0: None, 1: None}
        wpst = {0: None, 1: None}
        i = 0
        for part in (0, "p", 1, 2):
            for dc in range(DC):
                slot = slot_seq[i]
                buf = wtmp if slot < 3 else wtmp2
                sb_ = buf[:, (slot % 3) * D:(slot % 3 + 1) * D]
                ph0, pl0 = slot_prev.get(slot, (None, None))
                src = (wproj_h[dc * 128:(dc + 1) * 128, :] if part == "p"
                       else wqkv_h[dc * 128:(dc + 1) * 128,
                                   part * D:(part + 1) * D])
                t_l = P("sync", lambda e, src=src, sb_=sb_: e.dma_start(
                    out=sb_, in_=src),
                    waits=[ph0, pl0],
                    inc="s_lw" if i % 2 == 0 else "s_lw2", dma=True)
                if i % 2 == 1 and i // 2 < NSUB:
                    j = i // 2
                    t_xld2[0].append(P("sync", lambda e, j=j: e.dma_start(
                        out=xsA[:, j * D:(j + 1) * D],
                        in_=x_h[0, j * 128:j * 128 + 128, :]),
                        inc="s_lx", dma=True))
                    sq_prev[0] = emit_sq(0, j, sq_prev[0])
                    t_sq0.append(sq_prev[0])
                    t_sum0.append(emit_rs(0, j))
                if part == "p":
                    par = dc % 2
                    vslice = slice(par * D, (par + 1) * D)
                    ph = P("scalar", lambda e, sb_=sb_, vslice=vslice: e.copy(
                        vt_hi[:, vslice], sb_),
                        waits=[t_l, wpst[par]], inc="sc")
                    pl = P("vector", lambda e, sb_=sb_, vslice=vslice:
                           e.tensor_tensor(vt_lo[:, vslice], sb_,
                                           vt_hi[:, vslice], OP.subtract),
                           waits=[ph], inc="ve")
                    tst = None
                    for l, srcv in ((0, vt_hi), (1, vt_lo)):
                        tst = P("gpsimd", lambda e, dc=dc, l=l, srcv=srcv,
                                vslice=vslice: e.dma_start(
                                    out=wp_scr[l, dc], in_=srcv[:, vslice]),
                                waits=[pl], inc="g_wps", dma=True)
                    wpst[par] = tst
                    wp_rd[par] = pl
                    t_wpsp_last = tst
                else:
                    hs2 = slice(dc * 3 * D + part * D,
                                dc * 3 * D + (part + 1) * D)
                    ph = P("scalar", lambda e, hs2=hs2, sb_=sb_: e.copy(
                        wq_hi[:, hs2], sb_), waits=[t_l], inc="sc")
                    pl = P("vector", lambda e, hs2=hs2, sb_=sb_:
                           e.tensor_tensor(wq_lo[:, hs2], sb_,
                                           wq_hi[:, hs2], OP.subtract),
                           waits=[ph], inc="ve")
                    t_wsp_p[part] += [ph, pl]
                slot_prev[slot] = (ph, pl)
                if i == 2 * NSUB - 1:
                    emit_stats_chain(0, t_sq0, t_sum0)
                    emit_xn(0, 0)
                    emit_xn_tr(0, 0)
                if i % 2 == 1 and NSUB <= i // 2 < 2 * NSUB - 1:
                    jx = i // 2 - NSUB + 1
                    emit_xn(0, jx)
                    emit_xn_tr(0, jx)
                i += 1
        t_wsp_q = t_wsp_p[0] + [t_lgb]
        t_wsp_k = t_wsp_p[1]
        t_wsp_v = t_wsp_p[2] + [t_bvb]
        t_qgate = [slot_prev[0][1], slot_prev[1][1], slot_prev[2][1]]
        t_kgate = [slot_prev[3][1], slot_prev[4][1], slot_prev[5][1]]

        wgT_r = wgT_h.rearrange("(c p) (hc h) -> hc p c h", p=128, h=128)
        wuT_r = wuT_h.rearrange("(c p) (hc h) -> hc p c h", p=128, h=128)
        for hc in range(HC):
            P("gpsimd", lambda e, hc=hc: e.dma_start(
                out=wg_scr[hc], in_=wgT_r[hc]), inc="g_cv", dma=True)
            P("gpsimd", lambda e, hc=hc: e.dma_start(
                out=wu_scr[hc], in_=wuT_r[hc]), inc="g_cv", dma=True)
        n_cv = plan.cnt["g_cv"]

        t_store = {"q": t_qgate, "k": t_kgate, "v": None}
        t_vst = {(0, -2): t_wpsp_last, (0, -1): t_wpsp_last}

        for st in range(NST):
            s, t0 = (st * TA) // S, (st * TA) % S
            par = st % 2
            xb = par * DC * TA
            t_xnf = [t_ev[(st, j)][k] for j in range(NSUB) for k in (0, 1)]
            emit_loads(st + 1)

            wsrc = [(wq_hi, xnf_hi), (wq_hi, xnf_lo), (wq_lo, xnf_hi)]
            t_evqk = {}
            for ec in range(16):
                bank = ps0[:, (ec % 4) * 512:(ec % 4) * 512 + TA]
                pairs = [(wx[:, dc * 3 * D + ec * 128:dc * 3 * D + (ec + 1) * 128],
                          xy[:, xb + dc * TA:xb + (dc + 1) * TA])
                         for (wx, xy) in wsrc for dc in range(DC)]
                tmm = mm_group(bank, pairs,
                               waits=t_xnf + (t_wsp_q if ec < 8 else t_wsp_k)
                               + [t_evqk.get(ec - 4),
                                  t_evqk_prev.get(ec + 12)])
                dsth = qf_hi if ec < 8 else kf_hi
                dstl = qf_lo if ec < 8 else kf_lo
                cc = ec % 8
                bias = bq_sb[:, ec:ec + 1]
                th = P("scalar", lambda e, bank=bank, dsth=dsth, cc=cc,
                       bias=bias: e.add(
                           dsth[:, cc * TA:(cc + 1) * TA], bank, bias),
                       waits=[tmm, t_bq] + t_store["q" if ec < 8 else "k"],
                       inc="sc")
                tl = P("vector", lambda e, bank=bank, dsth=dsth, dstl=dstl,
                       cc=cc, bias=bias: e.scalar_tensor_tensor(
                           dstl[:, cc * TA:(cc + 1) * TA], bank, bias,
                           dsth[:, cc * TA:(cc + 1) * TA], OP.add, OP.subtract),
                       waits=[th], inc="ve")
                t_evqk[ec] = tl
                if ec == 3 and st + 1 < NST:
                    emit_stats(st + 1)
                if (ec % 4 == 0 and ec >= 4 and st + 1 < NST
                        and ec // 4 - 1 < NSUB):
                    emit_xn(st + 1, ec // 4 - 1)
                if (ec % 4 == 3 and ec >= 7 and st + 1 < NST
                        and ec // 4 - 1 < NSUB):
                    emit_xn_tr(st + 1, ec // 4 - 1)
            t_evqk_prev = t_evqk

            vsrc = [(xnf_hi, wq_hi), (xnf_lo, wq_hi), (xnf_hi, wq_lo)]
            t_evv = []
            for j in range(NSUB):
                vpar = j % 2
                for half in range(NH):
                    bank = ps1[:, half * 512:(half + 1) * 512]
                    pairs = [(xy[:, xb + dc * TA + j * 128:
                                 xb + dc * TA + (j + 1) * 128],
                              wx[:, dc * 3 * D + 2 * D + half * 512:
                                 dc * 3 * D + 2 * D + (half + 1) * 512])
                             for (xy, wx) in vsrc for dc in range(DC)]
                    tmm = mm_group(bank, pairs,
                                   waits=t_xnf + t_wsp_v +
                                   [t_evv_bank.get(half)])
                    vs = slice(vpar * D + half * 512, vpar * D + (half + 1) * 512)
                    th = P("scalar", lambda e, bank=bank, vs=vs: e.copy(
                        vt_hi[:, vs], bank),
                        waits=[tmm, t_vst.get((st, j - 2))], inc="sc")
                    tl = P("vector", lambda e, bank=bank, vs=vs:
                           e.tensor_tensor(vt_lo[:, vs], bank, vt_hi[:, vs],
                                           OP.subtract),
                           waits=[th], inc="ve")
                    t_evv += [th, tl]
                    t_evv_bank[half] = tl
                tvs = None
                for l, srcv in ((0, vt_hi), (1, vt_lo)):
                    tvs = P("scalar", lambda e, s=s, l=l, t0=t0, j=j,
                            srcv=srcv, vpar=vpar: e.dma_start(
                                out=v_scr[s, l, t0 // 128 + j, :, :],
                                in_=srcv[:, vpar * D:(vpar + 1) * D]),
                            waits=t_evv[-2:], inc="s_st", dma=True)
                t_vst[(st, j)] = tvs
                if j == 1 and st + 1 < NST and (st + 1, NSUB - 1) not in t_xn:
                    emit_xn(st + 1, NSUB - 1)
                    emit_xn_tr(st + 1, NSUB - 1)

            for (nm, srch, srcl, scr, evs) in (
                    ("q", qf_hi, qf_lo, q_scr, list(t_evqk.values())),
                    ("k", kf_hi, kf_lo, k_scr, list(t_evqk.values()))):
                sem = ("s_st" if st != NST - 1 else
                       ("s_st2" if nm == "q" else "s_st3"))
                tt = None
                for l, src in ((0, srch), (1, srcl)):
                    src_c = src.rearrange("p (c t) -> p c t", c=DC)
                    dst_c = scr[s, l, :, :, t0:t0 + TA].rearrange(
                        "c p t -> p c t")
                    tt = P("scalar", lambda e, src_c=src_c, dst_c=dst_c:
                           e.dma_start(out=dst_c, in_=src_c),
                           waits=evs, inc=sem, dma=True)
                t_store[nm] = [tt]
            if st == S // TA - 1:
                t_kst_s0 = t_store["k"][0]
                t_qst_s0 = t_store["q"][0]
                t_vst_s0 = t_vst[(st, NSUB - 1)]

        t_lastAmm = tmm
        t_qst_final = t_store["q"][0]
        t_kst_final = t_store["k"][0]
        t_evqk_last = list(t_evqk_prev.values())
        endA = plan.now("s_st", "s_st2", "s_st3", "te", "ve", "sc",
                        "s_lx", "s_lw")

        t_exp = {}
        t_lastArt = {}
        t_scmm = {}
        t_avmm = {}
        t_p1tok = {}
        t_p2tok = {}
        t_x1rd = {}
        t_qld = {}
        t_xld = {}
        t_kvld = {}
        t_stg2 = {}
        t_aofl = {}
        t_aoev = {}

        def emit_k(s, gate):
            t_kvld[s] = {}
            for kb in range(NKB):
                toks = {}
                for (dst, scr, l, eng, sem) in (
                        (k_hi, k_scr, 0, "sync", "s_lk"),
                        (k_lo, k_scr, 1, "gpsimd", "g_lk")):
                    toks[l] = P(eng, lambda e, dst=dst, scr=scr, l=l,
                                s=s, kb=kb: e.dma_start(
                                    out=dst.rearrange(
                                        "p (c s) -> p c s",
                                        c=DC)[:, :, kb * KW:(kb + 1) * KW],
                                    in_=scr[s, l].rearrange(
                                        "c p s -> p c s")
                                    [:, :, kb * KW:(kb + 1) * KW]),
                                waits=gate, inc=sem, dma=True)
                t_kvld[s][kb] = toks

        def emit_v(s, gate):
            t_vld[s] = {}
            for c in range(NVC):
                js = slice(c * 4, min((c + 1) * 4, NQT))
                toks = []
                for (dst, l, eng, sem) in ((v_hi, 0, "gpsimd", "s_lv"),
                                           (v_lo, 1, "sync", "s_lv2")):
                    toks.append(P(eng, lambda e, dst=dst, l=l, s=s, js=js:
                                  e.dma_start(
                                      out=dst.rearrange(
                                          "p (j d) -> p j d",
                                          j=NQT)[:, js, :],
                                      in_=v_scr[s, l].rearrange(
                                          "j p d -> p j d")[:, js, :]),
                                  waits=gate, inc=sem, dma=True))
                t_vld[s][c] = toks

        def prefetch_q(g):
            if g >= NG or g in t_qld:
                return
            s, i = g // NQT, g % NQT
            gq = ([t_scmm.get(g - 1)] +
                  ([t_lastAmm, t_qst_s0] if g == 0 else []))
            tq1 = P("sync", lambda e, s=s, i=i: e.dma_start(
                out=qt_hl.rearrange("p (l c t) -> p l c t", l=2, c=DC),
                in_=q_scr[s, :, :, :, i * 128:(i + 1) * 128].rearrange(
                    "l c p t -> p l c t")),
                waits=gq, inc="s_lq", dma=True)
            t_qld[g] = [tq1]

        def prefetch_x(g):
            if g >= NG or g in t_xld:
                return
            s, i = g // NQT, g % NQT
            t_xld[g] = P("sync", lambda e, s=s, i=i: e.dma_start(
                out=xres[:, :], in_=x_h[s, i * 128:(i + 1) * 128, :]),
                waits=[t_p2tok.get(g - 1)] + (endA if g == 0 else []),
                inc="s_lxb", dma=True)

        def emit_scores(g):
            s = g // NQT
            lastmm = None
            for kb in range(NKB):
                n = 0
                for (qx, kx) in ((qt_hi, k_hi), (qt_hi, k_lo), (qt_lo, k_hi)):
                    kl = 0 if kx is k_hi else 1
                    for dc in range(DC):
                        lhs = qx[:, dc * 128:(dc + 1) * 128]
                        n += 1
                        first, last = n == 1, n == 3 * DC
                        w = ()
                        if first:
                            w = (t_qld[g] + [t_kvld[s][kb][0],
                                             t_exp.get(g - 1)] +
                                 (t_evqk_last if g == 0 else []))
                        elif n == DC + 1:
                            w = (t_kvld[s][kb][1],)
                        lastmm = P("tensor", lambda e, lhs=lhs, kx=kx, dc=dc,
                                   kb=kb, st=first, sp=last: e.matmul(
                                       ps0[:, kb * 512:kb * 512 + KW], lhs,
                                       kx[:, dc * S + kb * KW:
                                          dc * S + (kb + 1) * KW],
                                       start=st, stop=sp),
                                   waits=w, inc="te" if last else None)
            t_scmm[g] = lastmm

        def emit_softmax(g):
            s, i = g // NQT, g % NQT
            t_mk = P("vector", lambda e, i=i: e.tensor_tensor(
                ps0[:, i * 128:(i + 1) * 128], ps0[:, i * 128:(i + 1) * 128],
                maskt[:], OP.mult),
                waits=[t_scmm[g], t_mask], inc="ve")
            t_mx = P("vector", lambda e: e.reduce_max(
                mx_b[:, 0:1], ps0[:, 0:S], axis=AX.X),
                waits=[t_mk, t_exp.get(g - 1)], inc="ve")
            t_ng = P("vector", lambda e: e.tensor_scalar_mul(
                mx_b[:, 1:2], mx_b[:, 0:1], -isqd),
                waits=[t_mx], inc="ve")
            te_ = P("scalar", lambda e: e.activation(
                a32[:, 0:S], ps0[:, 0:S], AF.Exp, bias=mx_b[:, 1:2],
                scale=isqd, accum_out=ell[:, 0:1]),
                waits=[t_ng, t_lastArt.get(g - 1), t_aoev.get(g - 1),
                       t_aofl.get(g - 1)] +
                ([t_kst_final] if g == 0 else []),
                inc="sc")
            t_exp[g] = te_
            t_il[g] = P("vector", lambda e: e.reciprocal(
                invl[:, 0:1], ell[:, 0:1]), waits=[te_], inc="ve")

        t_il = {}
        t_vld = {}
        pend_s2 = []
        t_s2done = [None]
        x1st = {}
        prefetch_q(0)
        emit_k(0, [t_lastAmm, t_kst_s0])
        emit_v(0, [t_lastAmm, t_vst_s0])
        t_wp = []
        for (dst, l, eng, sem, gw) in (
                (wp_hi, 0, "sync", "s_lw", [t_lastAmm]),
                (wp_lo, 1, "gpsimd", "g_lw", [t_lastAmm, t_qst_final])):
            t_wp.append(P(eng, lambda e, dst=dst, l=l: e.dma_start(
                out=dst.rearrange("p (c d) -> p c d", c=DC),
                in_=wp_scr[l].rearrange("c p d -> p c d")),
                waits=gw + [("g_wps", plan.cnt["g_wps"])],
                inc=sem, dma=True))
        prefetch_x(0)
        emit_scores(0)
        emit_softmax(0)
        for g in range(NG):
            s, i = g // NQT, g % NQT
            te_ = t_exp[g]
            t_at = []
            tr = None
            W2 = min(512, S)
            for r in range(S // W2):
                pp = (r % 2) * 512
                for q4 in range(W2 // 128):
                    kt = r * (W2 // 128) + q4
                    tr = P("tensor", lambda e, kt=kt, q4=q4, pp=pp: e.transpose(
                        ps2[:, pp + q4 * 128:pp + (q4 + 1) * 128],
                        a32[:, kt * 128:(kt + 1) * 128], ident[:]),
                        waits=([te_] + t_stg2.get((r - 2) % 2, []))
                        if q4 == 0 else (),
                        inc="te" if q4 == W2 // 128 - 1 else None)
                th = P("scalar", lambda e, r=r, pp=pp, W2=W2: e.copy(
                    at_hi[:, r * W2:(r + 1) * W2], ps2[:, pp:pp + W2]),
                    waits=[tr] + ([t_kst_final] if g == 0 else []),
                    inc="sc")
                tl = P("vector", lambda e, r=r, pp=pp, W2=W2: e.tensor_tensor(
                    at_lo[:, r * W2:(r + 1) * W2], ps2[:, pp:pp + W2],
                    at_hi[:, r * W2:(r + 1) * W2], OP.subtract),
                    waits=[tr, th], inc="ve")
                t_stg2[r % 2] = [th, tl]
                t_at += [th, tl]
            t_lastArt[g] = tr
            while pend_s2:
                t_s2done[0] = pend_s2.pop(0)()
                t_x1rd[g - 1][0] = t_s2done[0]
            lastav = None
            cnt = {h: 0 for h in range(NH)}
            for kt in range(NQT):
                rd = (kt * 128) // W2
                for (ax, vx) in ((at_hi, v_hi), (at_hi, v_lo), (at_lo, v_hi)):
                    lhs = ax[:, kt * 128:(kt + 1) * 128]
                    for half in range(NH):
                        cnt[half] += 1
                        first = cnt[half] == 1
                        last = cnt[half] == 3 * NQT
                        w = ()
                        if first:
                            w = (t_at[2 * rd:2 * rd + 2] +
                                 [t_p1tok.get(g - 1)] + t_vld[s][kt // 4] +
                                 (t_evv if g == 0 else []))
                        elif (kt % 4 == 0 and half == 0 and ax is at_hi
                              and vx is v_hi):
                            w = (tuple(t_at[2 * rd:2 * rd + 2]) +
                                 tuple(t_vld[s][kt // 4]))
                        lastav = P("tensor", lambda e, lhs=lhs, vx=vx, kt=kt,
                                   half=half, st=first, sp=last: e.matmul(
                                       ps1[:, half * 512:(half + 1) * 512],
                                       lhs,
                                       vx[:, kt * D + half * 512:
                                          kt * D + (half + 1) * 512],
                                       start=st, stop=sp),
                                   waits=w, inc="te" if last else None)
            t_avmm[g] = lastav
            if g + 1 < NG:
                prefetch_q(g + 1)
                if (g + 1) % NQT == 0:
                    emit_v(s + 1, [lastav])
                emit_scores(g + 1)
            if g + 2 < NG and (g + 2) % NQT == 0:
                emit_k(s + 1, [t_scmm[g + 1]])
            t_ao = P("vector", lambda e: e.scalar_tensor_tensor(
                ao32[:, :], ps1[:, :], invl[:, 0:1], bv_b[:, :],
                OP.mult, OP.add),
                waits=[lastav, t_il[g], t_bvb, t_aofl.get(g - 1)], inc="ve")
            t_aoev[g] = t_ao
            tr = None
            for r in range(2):
                pp = (r % 2) * 512
                for q4 in range(4):
                    dc = r * 4 + q4
                    tr = P("tensor", lambda e, dc=dc, q4=q4, pp=pp: e.transpose(
                        ps2[:, pp + q4 * 128:pp + (q4 + 1) * 128],
                        ao32[:, dc * 128:(dc + 1) * 128], ident[:]),
                        waits=([t_ao] + t_stg2.get((r - 2) % 2, []))
                        if q4 == 0 else (),
                        inc="te" if q4 == 3 else None)
                th = P("scalar", lambda e, r=r, pp=pp: e.copy(
                    aof_hi[:, r * 512:(r + 1) * 512], ps2[:, pp:pp + 512]),
                    waits=[tr] + (endA if g == 0 else []),
                    inc="sc")
                tl = P("vector", lambda e, r=r, pp=pp: e.tensor_tensor(
                    aof_lo[:, r * 512:(r + 1) * 512], ps2[:, pp:pp + 512],
                    aof_hi[:, r * 512:(r + 1) * 512], OP.subtract),
                    waits=[tr, th], inc="ve")
                t_stg2[r % 2] = [th, tl]
                if r == 1:
                    t_aof_ev = [th, tl]
            t_aofl[g] = t_aof_ev[1]
            if g + 1 < NG:
                emit_softmax(g + 1)
            lastpj = None
            cnt = {h: 0 for h in range(NH)}
            for (ax, wx) in ((aof_hi, wp_hi), (aof_hi, wp_lo),
                             (aof_lo, wp_hi)):
                for dc in range(DC):
                    lhs = ax[:, dc * 128:(dc + 1) * 128]
                    for half in range(NH):
                        cnt[half] += 1
                        first = cnt[half] == 1
                        last = cnt[half] == 3 * DC
                        w = (t_aof_ev + t_stg2[0] + t_stg2[1] + [t_ao] + t_wp) \
                            if first else ()
                        lastpj = P("tensor", lambda e, lhs=lhs, wx=wx, dc=dc,
                                   half=half, st=first, sp=last: e.matmul(
                                       ps1[:, half * 512:(half + 1) * 512],
                                       lhs,
                                       wx[:, dc * D + half * 512:
                                          dc * D + (half + 1) * 512],
                                       start=st, stop=sp),
                                   waits=w, inc="te" if last else None)
            t_p1 = P("vector", lambda e: e.tensor_tensor(
                x1t[:, :], ps1[:, :], bp_b[:, :], OP.add),
                waits=[lastpj, t_bpb] + t_x1rd.get(g - 1, []), inc="ve")
            t_p1tok[g] = t_p1
            t_p2 = P("vector", lambda e: e.tensor_tensor(
                x1t[:, :], x1t[:, :], xres[:, :], OP.add),
                waits=[t_p1, t_xld[g]], inc="ve")
            t_p2tok[g] = t_p2
            def s2fn(g=g, t_p2=t_p2):
                return P("scalar", lambda e, g=g, t_p2=t_p2: e.activation(
                    sqA[:, 0:D], x1t[:, :], AF.Square,
                    accum_out=sum2_s[:, g:g + 1]),
                    waits=[t_p2, t_s2done[0]], inc="sc")
            pend_s2.append(s2fn)
            t_r2 = P("vector", lambda e, g=g: e.reduce_sum(
                sum2_v[:, g:g + 1], x1t[:, :], axis=AX.X),
                waits=[t_p2], inc="ve")
            t_st = P("gpsimd", lambda e, s=s, i=i: e.dma_start(
                out=x1_scr[(s * S + i * 128):(s * S + (i + 1) * 128), :],
                in_=x1t[:, :]),
                waits=[t_p2], inc="s_sx1", dma=True)
            x1st[g] = t_st
            t_x1rd[g] = [t_r2, t_r2, t_st]
            prefetch_x(g + 1)

        while pend_s2:
            t_s2done[0] = pend_s2.pop(0)()
            t_x1rd[NG - 1][0] = t_s2done[0]

        endB = plan.now("te", "ve", "sc", "s_sx1", "s_lq", "s_lxb")

        t_scmm_last = t_scmm[NG - 1]
        t_avmm_last = t_avmm[NG - 1]
        wd_in = wdT_h.rearrange("(hc p) d -> p hc d", p=128)
        wd_out = wd_sb.rearrange("p (hc d) -> p hc d", hc=HC)
        t_wd = P("gpsimd", lambda e: e.dma_start(out=wd_out, in_=wd_in),
                 waits=[t_scmm_last], inc="g_cv", dma=True)
        t2a = P("vector", lambda e: e.tensor_tensor(
            st2[:, 0:NG], sum2_v[:, :], sum2_v[:, :], OP.mult),
            waits=endB, inc="ve")
        t2b = P("vector", lambda e: e.scalar_tensor_tensor(
            st2[:, NG:2 * NG], st2[:, 0:NG], -1.0 / D, sum2_s[:, :],
            OP.mult, OP.add), waits=[t2a], inc="ve")
        t2c = P("vector", lambda e: e.tensor_scalar(
            st2[:, 2 * NG:3 * NG], st2[:, NG:2 * NG], 1.0 / D, eps,
            OP.mult, OP.add), waits=[t2b], inc="ve")
        t2d = P("scalar", lambda e: e.activation(
            st2_s[:, :], st2[:, 2 * NG:3 * NG], AF.Sqrt),
            waits=[t2c], inc="sc")
        t2e = P("vector", lambda e: e.reciprocal(inv2[:, :], st2_s[:, :]),
                waits=[t2d], inc="ve")
        t2f = P("vector", lambda e: e.scalar_tensor_tensor(
            ngm2[:, :], sum2_v[:, :], -1.0 / D, inv2[:, :], OP.mult, OP.mult),
            waits=[t2e], inc="ve")

        t_wld = {}
        t_wmm = {}
        t_oev = {}
        t_ylast = {}
        t_xn2 = {}
        t_ev2 = {}
        t_hev_pair = []
        t_gu_prev = []
        t_gu_hist = {}
        t_stgC = [t_stg2.get(0, []) + t_stg2.get(1, [])]
        tr2_prev = [None]
        x1_readers = {0: [], 1: []}
        t_x1ld = {}
        x2fh_c = xn2f_hi.rearrange("p (x c t) -> p x c t", x=2, c=DC)
        x2fl_c = xn2f_lo.rearrange("p (x c t) -> p x c t", x=2, c=DC)

        def emit_x1_loads(u):
            if u >= NST or u in t_x1ld:
                return
            parU = u % 2
            t_x1ld[u] = [P("sync", lambda e, u=u, j=j, parU=parU: e.dma_start(
                out=x1A[:, (parU * NSUB + j) * D:(parU * NSUB + j + 1) * D],
                in_=x1_scr[u * TA + j * 128:u * TA + (j + 1) * 128, :]),
                waits=([t_p1tok[NG - 1], x1st[j]] if u == 0 else
                       [t_xn2.get((u - 2, j)), t_oev.get((u - 2, j))]),
                inc="s_lx1", dma=True) for j in range(NSUB)]

        def emit_xn2(u, j):
            parU = u % 2
            col = (u * TA) // 128 + j
            t_xn2[(u, j)] = P("vector", lambda e, j=j, parU=parU, col=col:
                              e.tensor_scalar(
                                  xn2[:, :],
                                  x1A[:, (parU * NSUB + j) * D:
                                      (parU * NSUB + j + 1) * D],
                                  inv2[:, col:col + 1], ngm2[:, col:col + 1],
                                  OP.mult, OP.add),
                              waits=[t_x1ld[u][j], t2f, tr2_prev[0]],
                              inc="ve")

        def emit_xn2_tr(u, j):
            parU = u % 2
            tr = None
            for dc in range(DC):
                tr = P("tensor", lambda e, dc=dc: e.transpose(
                    ps2[:, dc * 128:(dc + 1) * 128],
                    xn2[:, dc * 128:(dc + 1) * 128], ident[:]),
                    waits=([t_xn2[(u, j)]] + t_stgC[0] +
                           ([t_wmm.get((u - 1, HC - 1))] if j == 0 else []))
                    if dc == 0 else (),
                    inc="te" if dc == DC - 1 else None)
            tr2_prev[0] = tr
            th = P("scalar", lambda e, j=j, parU=parU: e.copy(
                x2fh_c[:, parU, :, j * 128:(j + 1) * 128], ps2c[:, :, :]),
                waits=[tr], inc="sc")
            tl = P("vector", lambda e, j=j, parU=parU: e.tensor_tensor(
                x2fl_c[:, parU, :, j * 128:(j + 1) * 128], ps2c[:, :, :],
                x2fh_c[:, parU, :, j * 128:(j + 1) * 128], OP.subtract),
                waits=[tr, th], inc="ve")
            t_ev2[(u, j)] = [th, tl]
            t_stgC[0] = [th, tl]

        emit_x1_loads(0)
        for j in range(NSUB):
            emit_xn2(0, j)
            emit_xn2_tr(0, j)

        for u in range(NST):
            parU = u % 2
            xb2 = parU * DC * TA
            s, t0 = (u * TA) // S, (u * TA) % S
            t_xnf2 = [t_ev2[(u, j)][k] for j in range(NSUB) for k in (0, 1)]
            emit_x1_loads(u + 1)

            t_hev = []
            for hc in range(HC):
                parW = hc % 2
                for look in ([0, 1] if hc == 0 else [1]):
                    hh = hc + look
                    if hh >= HC or (u, hh) in t_wld:
                        continue
                    pw = hh % 2
                    gw = [t_wmm.get((u, hh - 2))]
                    if hh < 2:
                        gw = [t_wmm.get((u - 1, HC - 2 + hh))]
                        if u == 0:
                            gw = [("g_cv", n_cv), t_avmm_last]
                    tg = P("sync", lambda e, hh=hh, pw=pw: e.dma_start(
                        out=wgbuf[:, pw * D:(pw + 1) * D].rearrange(
                            "p (c h) -> p c h", c=DC),
                        in_=wg_scr[hh]),
                        waits=gw, inc="s_lwg", dma=True)
                    tu = P("sync", lambda e, hh=hh, pw=pw: e.dma_start(
                        out=wubuf[:, pw * D:(pw + 1) * D].rearrange(
                            "p (c h) -> p c h", c=DC),
                        in_=wu_scr[hh]),
                        waits=gw, inc="s_lwu", dma=True)
                    t_wld[(u, hh)] = [tg, tu]
                tw = t_wld[(u, hc)]
                gbank = ps0[:, parW * 1024:parW * 1024 + TA]
                ubank = ps0[:, parW * 1024 + 512:parW * 1024 + 512 + TA]
                pairs_g = [(wgbuf[:, parW * D + dc * 128:
                                  parW * D + (dc + 1) * 128],
                            xy[:, xb2 + dc * TA:xb2 + (dc + 1) * TA])
                           for xy in (xn2f_hi, xn2f_lo) for dc in range(DC)]
                pairs_u = [(wubuf[:, parW * D + dc * 128:
                                  parW * D + (dc + 1) * 128],
                            xy[:, xb2 + dc * TA:xb2 + (dc + 1) * TA])
                           for xy in (xn2f_hi, xn2f_lo) for dc in range(DC)]
                ghc = u * HC + hc
                tmm_g = mm_group(gbank, pairs_g,
                                 waits=t_xnf2 + tw + t_gu_hist.get(ghc - 2, []) +
                                 ([t_oev.get((u - 1, 1))] if hc == 0 else []) +
                                 ([t_oev.get((u - 1, 3))]
                                  if hc == 1 and NSUB == 4 else []))
                tmm_u = mm_group(ubank, pairs_u, waits=[])
                t_wmm[(u, hc)] = tmm_u
                t_sg = P("scalar", lambda e, gbank=gbank: e.activation(
                    sgb[:, :], gbank, AF.Silu, scale=1.0 / 127.0),
                    waits=[tmm_g] + ([t_gu_prev[1]] if t_gu_prev else []),
                    inc="sc")
                t_h = P("vector", lambda e, ubank=ubank: e.scalar_tensor_tensor(
                    h32[:, :], ubank, 1.0 / 127.0, sgb[:, :],
                    OP.mult, OP.mult),
                    waits=[tmm_u, t_sg] + t_hev_pair, inc="ve")
                t_hh = P("scalar", lambda e, hc=hc: e.copy(
                    hf_hi[:, hc * TA:(hc + 1) * TA], h32[:, :]),
                    waits=[t_h, t_oev.get((u - 1, NSUB - 1))], inc="sc")
                t_hl = P("vector", lambda e, hc=hc: e.tensor_tensor(
                    hf_lo[:, hc * TA:(hc + 1) * TA], h32[:, :],
                    hf_hi[:, hc * TA:(hc + 1) * TA], OP.subtract),
                    waits=[t_hh], inc="ve")
                t_hev_pair = [t_hh, t_hl]
                t_hev += [t_hh, t_hl]
                t_gu_prev = [t_sg, t_h]
                t_gu_hist[ghc] = [t_sg, t_h]
                if u + 1 < NST and HC >= 12:
                    jj = hc // (HC // NSUB)
                    ph_in = hc % (HC // NSUB)
                    if ph_in == 1:
                        emit_xn2(u + 1, jj)
                    elif ph_in == 4:
                        emit_xn2_tr(u + 1, jj)

            for j in range(NSUB):
                psd = ps1 if j % 2 == 0 else (
                    ps0[:, 0:1024] if j % 4 == 1 else ps0[:, 1024:2048])
                lastdn = None
                cnt = {h: 0 for h in range(NH)}
                for hx in (hf_hi, hf_lo):
                    for hc in range(HC):
                        lhs = hx[:, hc * TA + j * 128:hc * TA + (j + 1) * 128]
                        for half in range(NH):
                            cnt[half] += 1
                            first = cnt[half] == 1
                            last = cnt[half] == 2 * HC
                            w = ()
                            if first:
                                w = t_hev + [t_wd, t_oev.get((u, j - 2))]
                                if j % 2 == 1:
                                    w = w + t_gu_hist[
                                        u * HC + HC - (2 if j % 4 == 1 else 1)]
                            lastdn = P("tensor", lambda e, lhs=lhs, hc=hc,
                                       half=half, psd=psd, st=first, sp=last:
                                       e.matmul(
                                           psd[:, half * 512:(half + 1) * 512],
                                           lhs,
                                           wd_sb[:, hc * D + half * 512:
                                                 hc * D + (half + 1) * 512],
                                           start=st, stop=sp),
                                       waits=w, inc="te" if last else None)
                to = P("vector", lambda e, j=j, parU=parU, psd=psd:
                       e.scalar_tensor_tensor(
                           outA[:, (j % 2) * D:(j % 2 + 1) * D], psd[:, 0:D],
                           1.0 / 127.0,
                           x1A[:, (parU * NSUB + j) * D:
                               (parU * NSUB + j + 1) * D],
                           OP.mult, OP.add),
                       waits=[lastdn, t_ylast.get((u, j - 2)),
                              t_ylast.get((u - 1, j + 2))], inc="ve")
                t_oev[(u, j)] = to
                ty = P("scalar", lambda e, s=s, t0=t0, j=j: e.dma_start(
                    out=y_h[s, t0 + j * 128:t0 + j * 128 + 128, :],
                    in_=outA[:, (j % 2) * D:(j % 2 + 1) * D]),
                    waits=[to], inc="s_sy", dma=True)
                t_ylast[(u, j)] = ty
            for hh in ((0, 1) if u + 1 < NST else ()):
                if hh >= HC or (u + 1, hh) in t_wld:
                    continue
                pw = hh % 2
                gw = [t_wmm.get((u, HC - 2 + hh))]
                tg = P("sync", lambda e, hh=hh, pw=pw: e.dma_start(
                    out=wgbuf[:, pw * D:(pw + 1) * D].rearrange(
                        "p (c h) -> p c h", c=DC),
                    in_=wg_scr[hh]),
                    waits=gw, inc="s_lwg", dma=True)
                tu = P("sync", lambda e, hh=hh, pw=pw: e.dma_start(
                    out=wubuf[:, pw * D:(pw + 1) * D].rearrange(
                        "p (c h) -> p c h", c=DC),
                    in_=wu_scr[hh]),
                    waits=gw, inc="s_lwu", dma=True)
                t_wld[(u + 1, hh)] = [tg, tu]
            if u + 1 < NST and HC < 12:
                for j in range(NSUB):
                    emit_xn2(u + 1, j)
                    emit_xn2_tr(u + 1, j)

        n_y = plan.cnt["s_sy"]
        plan.ops["sync"].append(
            (lambda e: e.nop(), (("s_sy", n_y),), None, 1))

        block = es.enter_context(nc.Block())

        @block.sync
        def _(e):
            plan.play("sync", e)

        @block.gpsimd
        def _(e):
            plan.play("gpsimd", e)

        @block.scalar
        def _(e):
            plan.play("scalar", e)

        @block.vector
        def _(e):
            plan.play("vector", e)

        @block.tensor
        def _(e):
            plan.play("tensor", e)

    return nc


def kernel(L0_attn_x, L0_attn_Wqkv, L0_attn_bqkv, L0_attn_Wproj, L0_attn_bproj,
           L0_attn_gamma, L0_ffn_w_gate, L0_ffn_w_up, L0_ffn_w_down):
    x = np.ascontiguousarray(np.asarray(L0_attn_x, dtype=np.float32))
    Bfull, S, D = x.shape
    ncore = 8
    Bc = Bfull // ncore
    H = L0_ffn_w_gate.shape[0]

    wqkv = np.ascontiguousarray(np.asarray(L0_attn_Wqkv, dtype=np.float32))
    bqkv = np.ascontiguousarray(np.asarray(L0_attn_bqkv, dtype=np.float32))
    wproj = np.ascontiguousarray(np.asarray(L0_attn_Wproj, dtype=np.float32))
    bproj = np.ascontiguousarray(np.asarray(L0_attn_bproj, dtype=np.float32))
    gamma = np.asarray(L0_attn_gamma, dtype=np.float32).reshape(1, 1)
    wgT = np.ascontiguousarray(np.asarray(L0_ffn_w_gate, dtype=np.int32).T)
    wuT = np.ascontiguousarray(np.asarray(L0_ffn_w_up, dtype=np.int32).T)
    wdT = np.ascontiguousarray(np.asarray(L0_ffn_w_down, dtype=np.int32).T)

    nc = build_program(B=Bc, S=S, D=D, H=H)
    in_maps = []
    for c in range(ncore):
        in_maps.append({
            "x": np.ascontiguousarray(x[c * Bc:(c + 1) * Bc]),
            "wqkv": wqkv, "bqkv": bqkv, "wproj": wproj, "bproj": bproj,
            "gamma": gamma, "wgT": wgT, "wuT": wuT, "wdT": wdT,
        })
    res = run_bass_kernel_spmd(nc, in_maps, core_ids=list(range(ncore)))
    out = np.concatenate([r["y"] for r in res.results], axis=0)
    return out.astype(np.float32)


# version 25
# speedup vs baseline: 1.0803x; 1.0055x over previous
import numpy as np
from contextlib import ExitStack

import concourse.bass as bass
import concourse.mybir as mybir
from concourse.bass_utils import run_bass_kernel_spmd

F32 = mybir.dt.float32
BF16 = mybir.dt.bfloat16
I32 = mybir.dt.int32
AF = mybir.ActivationFunctionType
AX = mybir.AxisListType
OP = mybir.AluOpType

ENGS = ("sync", "gpsimd", "scalar", "vector", "tensor")


class Plan:

    def __init__(self, nc, es):
        self.nc = nc
        self.es = es
        self.ops = {e: [] for e in ENGS}
        self.cnt = {}
        self.sems = {}

    def sem(self, name):
        if name not in self.sems:
            self.sems[name] = self.es.enter_context(self.nc.semaphore(name))
            self.cnt[name] = 0
        return self.sems[name]

    def op(self, eng, fn, waits=(), inc=None, dma=False):
        tok = None
        extra = ()
        if inc is not None:
            self.sem(inc)
            if dma and self.cnt[inc] > 0:
                extra = ((inc, self.cnt[inc]),)
            self.cnt[inc] += 16 if dma else 1
            tok = (inc, self.cnt[inc])
        ws = tuple(t for t in waits if t is not None) + extra
        self.ops[eng].append((fn, ws, inc, 16 if dma else 1))
        return tok

    def now(self, *names):
        return [(n, self.cnt[n]) for n in names if self.cnt.get(n, 0) > 0]

    def play(self, engname, eng):
        seen = {}
        for fn, waits, incname, n in self.ops[engname]:
            for (s, v) in waits:
                if v > 0 and seen.get(s, 0) < v:
                    eng.wait_ge(self.sems[s], v)
                    seen[s] = v
            ins = fn(eng)
            if incname is not None:
                ins.then_inc(self.sems[incname], n)


def build_program(B=2, S=2048, D=1024, H=3072):
    nc = bass.Bass("TRN2", target_bir_lowering=False)
    DC = D // 128
    HC = H // 128
    NQT = S // 128
    NG = B * NQT
    TA = min(512, S)
    NSUB = TA // 128
    NST = (B * S) // TA
    NKB = max(S // 512, 1)
    KW = min(512, S)
    NVC = max(S // 512, 1)
    NH = D // 512
    NR = max(S // 1024, 1)
    NTR = min(S // 128, 8)
    eps = 1e-5
    isqd = float(1.0 / np.sqrt(np.float32(D)))

    x_h = nc.dram_tensor("x", [B, S, D], F32, kind="ExternalInput").ap()
    wqkv_h = nc.dram_tensor("wqkv", [D, 3 * D], F32, kind="ExternalInput").ap()
    bqkv_h = nc.dram_tensor("bqkv", [3 * D], F32, kind="ExternalInput").ap()
    wproj_h = nc.dram_tensor("wproj", [D, D], F32, kind="ExternalInput").ap()
    bproj_h = nc.dram_tensor("bproj", [D], F32, kind="ExternalInput").ap()
    gamma_h = nc.dram_tensor("gamma", [1, 1], F32, kind="ExternalInput").ap()
    wgT_h = nc.dram_tensor("wgT", [D, H], I32, kind="ExternalInput").ap()
    wuT_h = nc.dram_tensor("wuT", [D, H], I32, kind="ExternalInput").ap()
    wdT_h = nc.dram_tensor("wdT", [H, D], I32, kind="ExternalInput").ap()
    y_h = nc.dram_tensor("y", [B, S, D], F32, kind="ExternalOutput").ap()

    q_scr = nc.dram_tensor("q_scr", [B, 2, DC, 128, S], BF16, kind="Internal").ap()
    k_scr = nc.dram_tensor("k_scr", [B, 2, DC, 128, S], BF16, kind="Internal").ap()
    v_scr = nc.dram_tensor("v_scr", [B, 2, NQT, 128, D], BF16, kind="Internal").ap()
    x1_scr = nc.dram_tensor("x1_scr", [B * S, D], F32, kind="Internal").ap()
    wg_scr = nc.dram_tensor("wg_scr", [HC, 128, DC, 128], BF16, kind="Internal").ap()
    wu_scr = nc.dram_tensor("wu_scr", [HC, 128, DC, 128], BF16, kind="Internal").ap()
    wp_scr = nc.dram_tensor("wp_scr", [2, DC, 128, D], BF16, kind="Internal").ap()

    es = ExitStack()
    with es:
        def sb(name, shape, dtype):
            return nc.alloc_sbuf_tensor(name, shape, dtype).ap()

        ident = sb("ident", [128, 128], F32)
        maskt = sb("maskt", [128, 128], F32)
        bv_b = sb("bv_b", [128, D], F32)
        bp_b = sb("bp_b", [128, D], F32)
        gam = sb("gam", [1, 1], F32)
        lg1 = sb("lg1", [1, 1], F32)
        lgb = sb("lgb", [128, 1], F32)
        sumA_v = sb("sumA_v", [128, NSUB], F32)
        sumA_s = sb("sumA_s", [128, NSUB], F32)
        stA = sb("stA", [128, 4 * NSUB], F32)
        stA_s = sb("stA_s", [128, NSUB], F32)
        sum2_v = sb("sum2_v", [128, NG], F32)
        sum2_s = sb("sum2_s", [128, NG], F32)
        st2 = sb("st2", [128, 3 * NG], F32)
        st2_s = sb("st2_s", [128, NG], F32)
        inv2 = sb("inv2", [128, NG], F32)
        ngm2 = sb("ngm2", [128, NG], F32)
        mx_b = sb("mx_b", [128, 2], F32)
        ell = sb("ell", [128, 1], F32)
        invl = sb("invl", [128, 1], F32)

        rem = nc.sbuf_bytes_remaining
        got = nc.bump_sbuf(rem - 64)
        assert got is not None
        arena, arena_hi = got

        starts = {}

        def at(name, cols, dtype, off):
            a = (off + 31) // 32 * 32
            t = nc.alloc_sbuf_tensor_at(name, [128, cols], dtype,
                                        offset=arena + a)
            starts[name] = a
            return t.ap(), a + cols * mybir.dt.size(dtype)

        o = 0
        wq_hi, o = at("wq_hi", DC * 3 * D, BF16, o)
        wq_lo, o = at("wq_lo", DC * 3 * D, BF16, o)
        xsA, o = at("xsA", NSUB * D, F32, o)
        xnA, o = at("xnA", D, F32, o)
        xnf_hi, o = at("xnf_hi", 2 * DC * TA, BF16, o)
        xnf_lo, o = at("xnf_lo", 2 * DC * TA, BF16, o)
        qf_hi, o = at("qf_hi", DC * TA, BF16, o)
        qf_lo, o = at("qf_lo", DC * TA, BF16, o)
        kf_hi, o = at("kf_hi", DC * TA, BF16, o)
        kf_lo, o = at("kf_lo", DC * TA, BF16, o)
        vt_hi, o = at("vt_hi", 2 * D, BF16, o)
        vt_lo, o = at("vt_lo", 2 * D, BF16, o)
        brow, o = at("brow", D, F32, o)
        brow = brow[0:1, :]
        ones1, o = at("ones1", 128, F32, o)
        ones1 = ones1[0:1, :]
        mask0, o = at("mask0", 128, F32, o)
        bq_sb, o = at("bq_sb", 16, F32, o)
        if 2 * DC * TA * 2 >= 3 * D * 4:
            wtmp, _ = at("wtmp", 3 * D, F32, starts["qf_hi"])
            wtmp2, _ = at("wtmp2", 3 * D, F32, starts["kf_hi"])
            a_end = o
        else:
            wtmp, o = at("wtmp", 3 * D, F32, o)
            wtmp2, o = at("wtmp2", 3 * D, F32, o)
            a_end = o
        o = 0
        qt_hl, o = at("qt_hl", 2 * D, BF16, o)
        qt_hi = qt_hl[:, 0:D]
        qt_lo = qt_hl[:, D:2 * D]
        k_hi, o = at("k_hi", DC * S, BF16, o)
        k_lo, o = at("k_lo", DC * S, BF16, o)
        v_hi, o = at("v_hi", NQT * D, BF16, o)
        v_lo, o = at("v_lo", NQT * D, BF16, o)
        wp_hi, o = at("wp_hi", DC * D, BF16, o)
        wp_lo, o = at("wp_lo", DC * D, BF16, o)
        a32, o = at("a32", S, F32, o)
        at_hi, o = at("at_hi", S, BF16, o)
        at_lo, o = at("at_lo", S, BF16, o)
        if S >= D:
            ao32, _ = at("ao32", D, F32, starts["a32"])
        else:
            ao32, o = at("ao32", D, F32, o)
        aof_hi, o = at("aof_hi", D, BF16, o)
        aof_lo, o = at("aof_lo", D, BF16, o)
        xres, o = at("xres", D, F32, o)
        x1t, o = at("x1t", D, F32, o)
        b_end = o
        o = 0
        wd_sb, o = at("wd_sb", HC * D, BF16, o)
        wgbuf, o = at("wgbuf", 2 * D, BF16, o)
        wubuf, o = at("wubuf", 2 * D, BF16, o)
        x1A, o = at("x1A", 2 * NSUB * D, F32, o)
        xn2, o = at("xn2", D, F32, o)
        xn2f_hi, o = at("xn2f_hi", 2 * DC * TA, BF16, o)
        xn2f_lo, o = at("xn2f_lo", 2 * DC * TA, BF16, o)
        sgb, o = at("sgb", TA, F32, o)
        h32, o = at("h32", TA, F32, o)
        hf_hi, o = at("hf_hi", HC * TA, BF16, o)
        hf_lo, o = at("hf_lo", HC * TA, BF16, o)
        outA, o = at("outA", 2 * D, F32, o)
        c_end = o
        sqA, sq_end = at("sqA", 1024, F32, max(a_end, b_end, c_end))
        assert sq_end <= arena_hi - arena, (
            a_end, b_end, c_end, sq_end, arena_hi - arena)

        ps0 = es.enter_context(nc.psum_tensor("ps0", [128, 2048], F32)).ap()
        ps1 = es.enter_context(nc.psum_tensor("ps1", [128, 1024], F32)).ap()
        ps2 = es.enter_context(nc.psum_tensor("ps2", [128, 1024], F32)).ap()

        plan = Plan(nc, es)
        P = plan.op

        def mm_group(psum_ap, pairs, waits, inc="te"):
            n = len(pairs)
            tok = None
            for i, (l, r) in enumerate(pairs):
                tok = P("tensor",
                        lambda e, l=l, r=r, st=(i == 0), sp=(i == n - 1):
                        e.matmul(psum_ap, l, r, start=st, stop=sp),
                        waits=waits if i == 0 else (),
                        inc=inc if i == n - 1 else None)
            return tok

        t_bq = P("sync", lambda e: e.dma_start(
            out=bq_sb[:, 0:16],
            in_=bqkv_h[0:2048].rearrange("(j p) -> p j", p=128),
            allow_slow_non_contiguous=True),
            inc="s_ld", dma=True)
        t_brow = P("sync", lambda e: e.dma_start(
            out=brow[0:1, 0:D],
            in_=bqkv_h[2048:3072].rearrange("(o d) -> o d", o=1)),
            inc="s_ld", dma=True)
        t_gam = P("sync", lambda e: e.dma_start(out=gam[:, :], in_=gamma_h[:, :]),
                  inc="s_ld", dma=True)

        t_id0 = P("gpsimd", lambda e: e.memset(ident[:], 0.0), inc="gp")
        t_id = P("gpsimd", lambda e: e.affine_select(
            out=ident[:], in_=ident[:], compare_op=OP.not_equal, fill=1.0,
            base=0, pattern=[[-1, 128]], channel_multiplier=1),
            waits=[t_id0], inc="gp")
        t_e0 = P("gpsimd", lambda e: e.iota(
            mask0[:], pattern=[[-1, 128]], base=0, channel_multiplier=1,
            allow_small_or_imprecise_dtypes=True), inc="gp")
        t_e = P("gpsimd", lambda e: e.tensor_scalar_max(mask0[:], mask0[:], 0.0),
                waits=[t_e0], inc="gp")
        t_on = P("vector", lambda e: e.memset(ones1[:, :], 1.0), inc="ve")
        t_lg = P("scalar", lambda e: e.activation(lg1[:, :], gam[:, :], AF.Ln),
                 waits=[t_gam], inc="sc")
        t_bc = []
        for hh in range(NH):
            t_bc.append(P("tensor", lambda e, hh=hh: e.matmul(
                ps1[:, hh * 512:(hh + 1) * 512], ones1[:, :],
                brow[0:1, hh * 512:(hh + 1) * 512], start=True, stop=True),
                waits=[t_on, t_brow], inc="te"))
        t_bvb = P("vector", lambda e: e.tensor_copy(bv_b[:, :], ps1[:, 0:D]),
                  waits=t_bc, inc="ve")
        t_brow2 = P("sync", lambda e: e.dma_start(
            out=brow[0:1, 0:D],
            in_=bproj_h[:].rearrange("(o d) -> o d", o=1)),
            waits=t_bc, inc="s_ld", dma=True)
        t_bc2 = []
        for hh in range(NH):
            t_bc2.append(P("tensor", lambda e, hh=hh: e.matmul(
                ps2[:, hh * 512:(hh + 1) * 512], ones1[:, :],
                brow[0:1, hh * 512:(hh + 1) * 512], start=True, stop=True),
                waits=[t_brow2], inc="te"))
        t_bcl = P("tensor", lambda e: e.matmul(
            ps0[:, 0:1], ones1[:, :], lg1[:, :], start=True, stop=True),
            waits=[t_on, t_lg], inc="te")
        t_bpb = P("vector", lambda e: e.tensor_copy(bp_b[:, :], ps2[:, 0:D]),
                  waits=t_bc2, inc="ve")
        t_lgb = P("vector", lambda e: e.tensor_copy(lgb[:, :], ps0[:, 0:1]),
                  waits=[t_bcl], inc="ve")
        t_m0 = P("scalar", lambda e: e.activation(
            maskt[:], mask0[:], AF.Exp, scale=lgb[:, :]),
            waits=[t_e, t_lgb], inc="sc")
        t_mask = P("gpsimd", lambda e: e.affine_select(
            out=maskt[:], in_=maskt[:], compare_op=OP.is_ge, fill=0.0,
            base=0, pattern=[[-1, 128]], channel_multiplier=1),
            waits=[t_m0], inc="gp")

        tr_done = {}
        t_evqk_prev = {}
        t_evv_bank = {}
        t_xn = {}
        t_xld2 = {}
        t_chain = {}
        tr_prev = [None]
        t_ev = {}
        t_ldstats = {}
        ps2c = ps2.rearrange("p (c t) -> p c t", c=DC)

        def emit_loads(st):
            if st >= NST:
                return
            s, t0 = (st * TA) // S, (st * TA) % S
            gate = [t_xn.get((st - 1, j)) for j in range(NSUB)]
            t_xld2[st] = [P("sync", lambda e, s=s, t0=t0, j=j: e.dma_start(
                out=xsA[:, j * D:(j + 1) * D],
                in_=x_h[s, t0 + j * 128:t0 + j * 128 + 128, :]),
                waits=gate, inc="s_lx", dma=True) for j in range(NSUB)]

        def emit_sq(st, j, prev):
            return P("scalar", lambda e, j=j: e.activation(
                sqA[:, 0:D], xsA[:, j * D:(j + 1) * D], AF.Square,
                accum_out=sumA_s[:, j:j + 1]),
                waits=[t_xld2[st][j], prev], inc="sc")

        def emit_rs(st, j):
            return P("vector", lambda e, j=j: e.reduce_sum(
                sumA_v[:, j:j + 1], xsA[:, j * D:(j + 1) * D], axis=AX.X),
                waits=[t_xld2[st][j]], inc="ve")

        def emit_stats(st):
            if st >= NST:
                return
            t_sq, prev = [], None
            for j in range(NSUB):
                prev = emit_sq(st, j, prev)
                t_sq.append(prev)
            t_sum = [emit_rs(st, j) for j in range(NSUB)]
            emit_stats_chain(st, t_sq, t_sum)

        def emit_stats_chain(st, t_sq, t_sum):
            NSL = slice(0, NSUB)
            t_c1 = P("vector", lambda e: e.tensor_tensor(
                stA[:, 0:NSUB], sumA_v[:, NSL], sumA_v[:, NSL], OP.mult),
                waits=t_sum, inc="ve")
            t_c2 = P("vector", lambda e: e.scalar_tensor_tensor(
                stA[:, NSUB:2 * NSUB], stA[:, 0:NSUB], -1.0 / D,
                sumA_s[:, NSL], OP.mult, OP.add),
                waits=[t_c1] + t_sq, inc="ve")
            t_c3 = P("vector", lambda e: e.tensor_scalar(
                stA[:, 2 * NSUB:3 * NSUB], stA[:, NSUB:2 * NSUB], 1.0 / D, eps,
                OP.mult, OP.add), waits=[t_c2], inc="ve")
            t_c4 = P("scalar", lambda e: e.activation(
                stA_s[:, NSL], stA[:, 2 * NSUB:3 * NSUB], AF.Sqrt),
                waits=[t_c3], inc="sc")
            t_c5 = P("vector", lambda e: e.reciprocal(
                stA[:, 3 * NSUB:4 * NSUB], stA_s[:, NSL]),
                waits=[t_c4], inc="ve")
            t_c6 = P("vector", lambda e: e.scalar_tensor_tensor(
                stA[:, 0:NSUB], sumA_v[:, NSL], -1.0 / D,
                stA[:, 3 * NSUB:4 * NSUB], OP.mult, OP.mult),
                waits=[t_c5, t_c2], inc="ve")
            t_chain[st] = t_c6

        def emit_xn(st, j):
            t_xn[(st, j)] = P("vector", lambda e, j=j: e.tensor_scalar(
                xnA[:, :], xsA[:, j * D:(j + 1) * D],
                stA[:, 3 * NSUB + j:3 * NSUB + j + 1], stA[:, j:j + 1],
                OP.mult, OP.add),
                waits=[t_chain[st], tr_prev[0]], inc="ve")

        def emit_xn_tr(st, j):
            par = st % 2
            xb = par * DC * TA
            tr = None
            for dc in range(DC):
                tr = P("tensor", lambda e, dc=dc: e.transpose(
                    ps2[:, dc * 128:(dc + 1) * 128],
                    xnA[:, dc * 128:(dc + 1) * 128], ident[:]),
                    waits=([t_xn[(st, j)], t_id, t_bpb,
                            t_ev.get((st, j - 1), [None])[0],
                            t_ev.get((st, j - 1), [None, None])[1]]
                           if dc == 0 else ()),
                    inc="te" if dc == DC - 1 else None)
            tr_done[(st, j)] = tr
            tr_prev[0] = tr
            th = P("scalar", lambda e, j=j, xb=xb: e.copy(
                xnf_hi.rearrange("p (x c t) -> p x c t", x=2, c=DC)
                [:, st % 2, :, j * 128:(j + 1) * 128], ps2c[:, :, :]),
                waits=[tr], inc="sc")
            tl = P("vector", lambda e, j=j, xb=xb: e.tensor_tensor(
                xnf_lo.rearrange("p (x c t) -> p x c t", x=2, c=DC)
                [:, st % 2, :, j * 128:(j + 1) * 128], ps2c[:, :, :],
                xnf_hi.rearrange("p (x c t) -> p x c t", x=2, c=DC)
                [:, st % 2, :, j * 128:(j + 1) * 128], OP.subtract),
                waits=[tr, th], inc="ve")
            t_ev[(st, j)] = [th, tl]

        t_wsp_p = {0: [], 1: [], 2: []}
        slot_prev = {}
        slot_seq = ([0, 1, 2, 3, 4, 5, 0, 1] + [2, 3] * 4 + [4, 5] * 8)
        t_xld2[0] = []
        sq_prev = [None]
        t_sq0, t_sum0 = [], []
        t_wpsp_last = None
        wp_rd = {0: None, 1: None}
        wpst = {0: None, 1: None}
        i = 0
        for part in (0, "p", 1, 2):
            for dc in range(DC):
                slot = slot_seq[i]
                buf = wtmp if slot < 3 else wtmp2
                sb_ = buf[:, (slot % 3) * D:(slot % 3 + 1) * D]
                ph0, pl0 = slot_prev.get(slot, (None, None))
                src = (wproj_h[dc * 128:(dc + 1) * 128, :] if part == "p"
                       else wqkv_h[dc * 128:(dc + 1) * 128,
                                   part * D:(part + 1) * D])
                t_l = P("sync", lambda e, src=src, sb_=sb_: e.dma_start(
                    out=sb_, in_=src),
                    waits=[ph0, pl0],
                    inc="s_lw" if i % 2 == 0 else "s_lw2", dma=True)
                if i % 2 == 1 and i // 2 < NSUB:
                    j = i // 2
                    t_xld2[0].append(P("sync", lambda e, j=j: e.dma_start(
                        out=xsA[:, j * D:(j + 1) * D],
                        in_=x_h[0, j * 128:j * 128 + 128, :]),
                        inc="s_lx", dma=True))
                    sq_prev[0] = emit_sq(0, j, sq_prev[0])
                    t_sq0.append(sq_prev[0])
                    t_sum0.append(emit_rs(0, j))
                if part == "p":
                    par = dc % 2
                    vslice = slice(par * D, (par + 1) * D)
                    ph = P("scalar", lambda e, sb_=sb_, vslice=vslice: e.copy(
                        vt_hi[:, vslice], sb_),
                        waits=[t_l, wpst[par]], inc="sc")
                    pl = P("vector", lambda e, sb_=sb_, vslice=vslice:
                           e.tensor_tensor(vt_lo[:, vslice], sb_,
                                           vt_hi[:, vslice], OP.subtract),
                           waits=[ph], inc="ve")
                    tst = None
                    for l, srcv in ((0, vt_hi), (1, vt_lo)):
                        tst = P("gpsimd", lambda e, dc=dc, l=l, srcv=srcv,
                                vslice=vslice: e.dma_start(
                                    out=wp_scr[l, dc], in_=srcv[:, vslice]),
                                waits=[pl], inc="g_wps", dma=True)
                    wpst[par] = tst
                    wp_rd[par] = pl
                    t_wpsp_last = tst
                else:
                    hs2 = slice(dc * 3 * D + part * D,
                                dc * 3 * D + (part + 1) * D)
                    ph = P("scalar", lambda e, hs2=hs2, sb_=sb_: e.copy(
                        wq_hi[:, hs2], sb_), waits=[t_l], inc="sc")
                    pl = P("vector", lambda e, hs2=hs2, sb_=sb_:
                           e.tensor_tensor(wq_lo[:, hs2], sb_,
                                           wq_hi[:, hs2], OP.subtract),
                           waits=[ph], inc="ve")
                    t_wsp_p[part] += [ph, pl]
                slot_prev[slot] = (ph, pl)
                if i == 2 * NSUB - 1:
                    emit_stats_chain(0, t_sq0, t_sum0)
                    for jx in range(NSUB):
                        emit_xn(0, jx)
                        emit_xn_tr(0, jx)
                i += 1
        t_wsp_q = t_wsp_p[0] + [t_lgb]
        t_wsp_k = t_wsp_p[1]
        t_wsp_v = t_wsp_p[2] + [t_bvb]
        t_qgate = [slot_prev[0][1], slot_prev[1][1], slot_prev[2][1]]
        t_kgate = [slot_prev[3][1], slot_prev[4][1], slot_prev[5][1]]

        wgT_r = wgT_h.rearrange("(c p) (hc h) -> hc p c h", p=128, h=128)
        wuT_r = wuT_h.rearrange("(c p) (hc h) -> hc p c h", p=128, h=128)
        for hc in range(HC):
            P("gpsimd", lambda e, hc=hc: e.dma_start(
                out=wg_scr[hc], in_=wgT_r[hc]), inc="g_cv", dma=True)
            P("gpsimd", lambda e, hc=hc: e.dma_start(
                out=wu_scr[hc], in_=wuT_r[hc]), inc="g_cv", dma=True)
        n_cv = plan.cnt["g_cv"]

        t_store = {"q": t_qgate, "k": t_kgate, "v": None}
        t_vst = {(0, -2): t_wpsp_last, (0, -1): t_wpsp_last}

        for st in range(NST):
            s, t0 = (st * TA) // S, (st * TA) % S
            par = st % 2
            xb = par * DC * TA
            t_xnf = [t_ev[(st, j)][k] for j in range(NSUB) for k in (0, 1)]
            emit_loads(st + 1)

            wsrc = [(wq_hi, xnf_hi), (wq_hi, xnf_lo), (wq_lo, xnf_hi)]
            t_evqk = {}
            for ec in range(16):
                bank = ps0[:, (ec % 4) * 512:(ec % 4) * 512 + TA]
                pairs = [(wx[:, dc * 3 * D + ec * 128:dc * 3 * D + (ec + 1) * 128],
                          xy[:, xb + dc * TA:xb + (dc + 1) * TA])
                         for (wx, xy) in wsrc for dc in range(DC)]
                tmm = mm_group(bank, pairs,
                               waits=t_xnf + (t_wsp_q if ec < 8 else t_wsp_k)
                               + [t_evqk.get(ec - 4),
                                  t_evqk_prev.get(ec + 12)])
                dsth = qf_hi if ec < 8 else kf_hi
                dstl = qf_lo if ec < 8 else kf_lo
                cc = ec % 8
                bias = bq_sb[:, ec:ec + 1]
                th = P("scalar", lambda e, bank=bank, dsth=dsth, cc=cc,
                       bias=bias: e.add(
                           dsth[:, cc * TA:(cc + 1) * TA], bank, bias),
                       waits=[tmm, t_bq] + t_store["q" if ec < 8 else "k"],
                       inc="sc")
                tl = P("vector", lambda e, bank=bank, dsth=dsth, dstl=dstl,
                       cc=cc, bias=bias: e.scalar_tensor_tensor(
                           dstl[:, cc * TA:(cc + 1) * TA], bank, bias,
                           dsth[:, cc * TA:(cc + 1) * TA], OP.add, OP.subtract),
                       waits=[th], inc="ve")
                t_evqk[ec] = tl
                if ec == 3 and st + 1 < NST:
                    emit_stats(st + 1)
                if (ec % 4 == 0 and ec >= 4 and st + 1 < NST
                        and ec // 4 - 1 < NSUB):
                    emit_xn(st + 1, ec // 4 - 1)
                if (ec % 4 == 3 and ec >= 7 and st + 1 < NST
                        and ec // 4 - 1 < NSUB):
                    emit_xn_tr(st + 1, ec // 4 - 1)
            t_evqk_prev = t_evqk

            vsrc = [(xnf_hi, wq_hi), (xnf_lo, wq_hi), (xnf_hi, wq_lo)]
            t_evv = []
            for j in range(NSUB):
                vpar = j % 2
                for half in range(NH):
                    bank = ps1[:, half * 512:(half + 1) * 512]
                    pairs = [(xy[:, xb + dc * TA + j * 128:
                                 xb + dc * TA + (j + 1) * 128],
                              wx[:, dc * 3 * D + 2 * D + half * 512:
                                 dc * 3 * D + 2 * D + (half + 1) * 512])
                             for (xy, wx) in vsrc for dc in range(DC)]
                    tmm = mm_group(bank, pairs,
                                   waits=t_xnf + t_wsp_v +
                                   [t_evv_bank.get(half)])
                    vs = slice(vpar * D + half * 512, vpar * D + (half + 1) * 512)
                    th = P("scalar", lambda e, bank=bank, vs=vs: e.copy(
                        vt_hi[:, vs], bank),
                        waits=[tmm, t_vst.get((st, j - 2))], inc="sc")
                    tl = P("vector", lambda e, bank=bank, vs=vs:
                           e.tensor_tensor(vt_lo[:, vs], bank, vt_hi[:, vs],
                                           OP.subtract),
                           waits=[th], inc="ve")
                    t_evv += [th, tl]
                    t_evv_bank[half] = tl
                tvs = None
                for l, srcv in ((0, vt_hi), (1, vt_lo)):
                    tvs = P("scalar", lambda e, s=s, l=l, t0=t0, j=j,
                            srcv=srcv, vpar=vpar: e.dma_start(
                                out=v_scr[s, l, t0 // 128 + j, :, :],
                                in_=srcv[:, vpar * D:(vpar + 1) * D]),
                            waits=t_evv[-2:], inc="s_st", dma=True)
                t_vst[(st, j)] = tvs
                if j == 1 and st + 1 < NST and (st + 1, NSUB - 1) not in t_xn:
                    emit_xn(st + 1, NSUB - 1)
                    emit_xn_tr(st + 1, NSUB - 1)

            for (nm, srch, srcl, scr, evs) in (
                    ("q", qf_hi, qf_lo, q_scr, list(t_evqk.values())),
                    ("k", kf_hi, kf_lo, k_scr, list(t_evqk.values()))):
                sem = ("s_st" if st != NST - 1 else
                       ("s_st2" if nm == "q" else "s_st3"))
                tt = None
                for l, src in ((0, srch), (1, srcl)):
                    src_c = src.rearrange("p (c t) -> p c t", c=DC)
                    dst_c = scr[s, l, :, :, t0:t0 + TA].rearrange(
                        "c p t -> p c t")
                    tt = P("scalar", lambda e, src_c=src_c, dst_c=dst_c:
                           e.dma_start(out=dst_c, in_=src_c),
                           waits=evs, inc=sem, dma=True)
                t_store[nm] = [tt]
            if st == S // TA - 1:
                t_kst_s0 = t_store["k"][0]
                t_qst_s0 = t_store["q"][0]
                t_vst_s0 = t_vst[(st, NSUB - 1)]

        t_lastAmm = tmm
        t_qst_final = t_store["q"][0]
        t_kst_final = t_store["k"][0]
        t_evqk_last = list(t_evqk_prev.values())
        endA = plan.now("s_st", "s_st2", "s_st3", "te", "ve", "sc",
                        "s_lx", "s_lw")

        t_exp = {}
        t_lastArt = {}
        t_scmm = {}
        t_avmm = {}
        t_p1tok = {}
        t_p2tok = {}
        t_x1rd = {}
        t_qld = {}
        t_xld = {}
        t_kvld = {}
        t_stg2 = {}
        t_aofl = {}
        t_aoev = {}

        def emit_k(s, gate):
            t_kvld[s] = {}
            for kb in range(NKB):
                toks = {}
                for (dst, scr, l, eng, sem) in (
                        (k_hi, k_scr, 0, "sync", "s_lk"),
                        (k_lo, k_scr, 1, "gpsimd", "g_lk")):
                    toks[l] = P(eng, lambda e, dst=dst, scr=scr, l=l,
                                s=s, kb=kb: e.dma_start(
                                    out=dst.rearrange(
                                        "p (c s) -> p c s",
                                        c=DC)[:, :, kb * KW:(kb + 1) * KW],
                                    in_=scr[s, l].rearrange(
                                        "c p s -> p c s")
                                    [:, :, kb * KW:(kb + 1) * KW]),
                                waits=gate, inc=sem, dma=True)
                t_kvld[s][kb] = toks

        def emit_v(s, gate):
            t_vld[s] = {}
            for c in range(NVC):
                js = slice(c * 4, min((c + 1) * 4, NQT))
                toks = []
                for (dst, l, eng, sem) in ((v_hi, 0, "gpsimd", "s_lv"),
                                           (v_lo, 1, "sync", "s_lv2")):
                    toks.append(P(eng, lambda e, dst=dst, l=l, s=s, js=js:
                                  e.dma_start(
                                      out=dst.rearrange(
                                          "p (j d) -> p j d",
                                          j=NQT)[:, js, :],
                                      in_=v_scr[s, l].rearrange(
                                          "j p d -> p j d")[:, js, :]),
                                  waits=gate, inc=sem, dma=True))
                t_vld[s][c] = toks

        def prefetch_q(g):
            if g >= NG or g in t_qld:
                return
            s, i = g // NQT, g % NQT
            gq = ([t_scmm.get(g - 1)] +
                  ([t_lastAmm, t_qst_s0] if g == 0 else []))
            tq1 = P("sync", lambda e, s=s, i=i: e.dma_start(
                out=qt_hl.rearrange("p (l c t) -> p l c t", l=2, c=DC),
                in_=q_scr[s, :, :, :, i * 128:(i + 1) * 128].rearrange(
                    "l c p t -> p l c t")),
                waits=gq, inc="s_lq", dma=True)
            t_qld[g] = [tq1]

        def prefetch_x(g):
            if g >= NG or g in t_xld:
                return
            s, i = g // NQT, g % NQT
            t_xld[g] = P("sync", lambda e, s=s, i=i: e.dma_start(
                out=xres[:, :], in_=x_h[s, i * 128:(i + 1) * 128, :]),
                waits=[t_p2tok.get(g - 1)] + (endA if g == 0 else []),
                inc="s_lxb", dma=True)

        def emit_scores(g):
            s = g // NQT
            lastmm = None
            for kb in range(NKB):
                n = 0
                for (qx, kx) in ((qt_hi, k_hi), (qt_hi, k_lo), (qt_lo, k_hi)):
                    kl = 0 if kx is k_hi else 1
                    for dc in range(DC):
                        lhs = qx[:, dc * 128:(dc + 1) * 128]
                        n += 1
                        first, last = n == 1, n == 3 * DC
                        w = ()
                        if first:
                            w = (t_qld[g] + [t_kvld[s][kb][0],
                                             t_exp.get(g - 1)] +
                                 (t_evqk_last if g == 0 else []))
                        elif n == DC + 1:
                            w = (t_kvld[s][kb][1],)
                        lastmm = P("tensor", lambda e, lhs=lhs, kx=kx, dc=dc,
                                   kb=kb, st=first, sp=last: e.matmul(
                                       ps0[:, kb * 512:kb * 512 + KW], lhs,
                                       kx[:, dc * S + kb * KW:
                                          dc * S + (kb + 1) * KW],
                                       start=st, stop=sp),
                                   waits=w, inc="te" if last else None)
            t_scmm[g] = lastmm

        def emit_softmax(g):
            s, i = g // NQT, g % NQT
            t_mk = P("vector", lambda e, i=i: e.tensor_tensor(
                ps0[:, i * 128:(i + 1) * 128], ps0[:, i * 128:(i + 1) * 128],
                maskt[:], OP.mult),
                waits=[t_scmm[g], t_mask], inc="ve")
            t_mx = P("vector", lambda e: e.reduce_max(
                mx_b[:, 0:1], ps0[:, 0:S], axis=AX.X),
                waits=[t_mk, t_exp.get(g - 1)], inc="ve")
            t_ng = P("vector", lambda e: e.tensor_scalar_mul(
                mx_b[:, 1:2], mx_b[:, 0:1], -isqd),
                waits=[t_mx], inc="ve")
            te_ = P("scalar", lambda e: e.activation(
                a32[:, 0:S], ps0[:, 0:S], AF.Exp, bias=mx_b[:, 1:2],
                scale=isqd, accum_out=ell[:, 0:1]),
                waits=[t_ng, t_lastArt.get(g - 1), t_aoev.get(g - 1),
                       t_aofl.get(g - 1)] +
                ([t_kst_final] if g == 0 else []),
                inc="sc")
            t_exp[g] = te_
            t_il[g] = P("vector", lambda e: e.reciprocal(
                invl[:, 0:1], ell[:, 0:1]), waits=[te_], inc="ve")

        t_il = {}
        t_vld = {}
        pend_s2 = []
        t_s2done = [None]
        x1st = {}
        prefetch_q(0)
        emit_k(0, [t_lastAmm, t_kst_s0])
        emit_v(0, [t_lastAmm, t_vst_s0])
        t_wp = []
        for (dst, l, eng, sem, gw) in (
                (wp_hi, 0, "sync", "s_lw", [t_lastAmm]),
                (wp_lo, 1, "gpsimd", "g_lw", [t_lastAmm, t_qst_final])):
            t_wp.append(P(eng, lambda e, dst=dst, l=l: e.dma_start(
                out=dst.rearrange("p (c d) -> p c d", c=DC),
                in_=wp_scr[l].rearrange("c p d -> p c d")),
                waits=gw + [("g_wps", plan.cnt["g_wps"])],
                inc=sem, dma=True))
        prefetch_x(0)
        emit_scores(0)
        emit_softmax(0)
        for g in range(NG):
            s, i = g // NQT, g % NQT
            te_ = t_exp[g]
            t_at = []
            tr = None
            W2 = min(512, S)
            for r in range(S // W2):
                pp = (r % 2) * 512
                for q4 in range(W2 // 128):
                    kt = r * (W2 // 128) + q4
                    tr = P("tensor", lambda e, kt=kt, q4=q4, pp=pp: e.transpose(
                        ps2[:, pp + q4 * 128:pp + (q4 + 1) * 128],
                        a32[:, kt * 128:(kt + 1) * 128], ident[:]),
                        waits=([te_] + t_stg2.get((r - 2) % 2, []))
                        if q4 == 0 else (),
                        inc="te" if q4 == W2 // 128 - 1 else None)
                th = P("scalar", lambda e, r=r, pp=pp, W2=W2: e.copy(
                    at_hi[:, r * W2:(r + 1) * W2], ps2[:, pp:pp + W2]),
                    waits=[tr] + ([t_kst_final] if g == 0 else []),
                    inc="sc")
                tl = P("vector", lambda e, r=r, pp=pp, W2=W2: e.tensor_tensor(
                    at_lo[:, r * W2:(r + 1) * W2], ps2[:, pp:pp + W2],
                    at_hi[:, r * W2:(r + 1) * W2], OP.subtract),
                    waits=[tr, th], inc="ve")
                t_stg2[r % 2] = [th, tl]
                t_at += [th, tl]
            t_lastArt[g] = tr
            while pend_s2:
                t_s2done[0] = pend_s2.pop(0)()
                t_x1rd[g - 1][0] = t_s2done[0]
            lastav = None
            cnt = {h: 0 for h in range(NH)}
            for kt in range(NQT):
                rd = (kt * 128) // W2
                for (ax, vx) in ((at_hi, v_hi), (at_hi, v_lo), (at_lo, v_hi)):
                    lhs = ax[:, kt * 128:(kt + 1) * 128]
                    for half in range(NH):
                        cnt[half] += 1
                        first = cnt[half] == 1
                        last = cnt[half] == 3 * NQT
                        w = ()
                        if first:
                            w = (t_at[2 * rd:2 * rd + 2] +
                                 [t_p1tok.get(g - 1)] + t_vld[s][kt // 4] +
                                 (t_evv if g == 0 else []))
                        elif (kt % 4 == 0 and half == 0 and ax is at_hi
                              and vx is v_hi):
                            w = (tuple(t_at[2 * rd:2 * rd + 2]) +
                                 tuple(t_vld[s][kt // 4]))
                        lastav = P("tensor", lambda e, lhs=lhs, vx=vx, kt=kt,
                                   half=half, st=first, sp=last: e.matmul(
                                       ps1[:, half * 512:(half + 1) * 512],
                                       lhs,
                                       vx[:, kt * D + half * 512:
                                          kt * D + (half + 1) * 512],
                                       start=st, stop=sp),
                                   waits=w, inc="te" if last else None)
            t_avmm[g] = lastav
            if g + 1 < NG:
                prefetch_q(g + 1)
                if (g + 1) % NQT == 0:
                    emit_v(s + 1, [lastav])
                emit_scores(g + 1)
            if g + 2 < NG and (g + 2) % NQT == 0:
                emit_k(s + 1, [t_scmm[g + 1]])
            t_ao = P("vector", lambda e: e.scalar_tensor_tensor(
                ao32[:, :], ps1[:, :], invl[:, 0:1], bv_b[:, :],
                OP.mult, OP.add),
                waits=[lastav, t_il[g], t_bvb, t_aofl.get(g - 1)], inc="ve")
            t_aoev[g] = t_ao
            tr = None
            for r in range(2):
                pp = (r % 2) * 512
                for q4 in range(4):
                    dc = r * 4 + q4
                    tr = P("tensor", lambda e, dc=dc, q4=q4, pp=pp: e.transpose(
                        ps2[:, pp + q4 * 128:pp + (q4 + 1) * 128],
                        ao32[:, dc * 128:(dc + 1) * 128], ident[:]),
                        waits=([t_ao] + t_stg2.get((r - 2) % 2, []))
                        if q4 == 0 else (),
                        inc="te" if q4 == 3 else None)
                th = P("scalar", lambda e, r=r, pp=pp: e.copy(
                    aof_hi[:, r * 512:(r + 1) * 512], ps2[:, pp:pp + 512]),
                    waits=[tr] + (endA if g == 0 else []),
                    inc="sc")
                tl = P("vector", lambda e, r=r, pp=pp: e.tensor_tensor(
                    aof_lo[:, r * 512:(r + 1) * 512], ps2[:, pp:pp + 512],
                    aof_hi[:, r * 512:(r + 1) * 512], OP.subtract),
                    waits=[tr, th], inc="ve")
                t_stg2[r % 2] = [th, tl]
                if r == 1:
                    t_aof_ev = [th, tl]
            t_aofl[g] = t_aof_ev[1]
            if g + 1 < NG:
                emit_softmax(g + 1)
            lastpj = None
            cnt = {h: 0 for h in range(NH)}
            for (ax, wx) in ((aof_hi, wp_hi), (aof_hi, wp_lo),
                             (aof_lo, wp_hi)):
                for dc in range(DC):
                    lhs = ax[:, dc * 128:(dc + 1) * 128]
                    for half in range(NH):
                        cnt[half] += 1
                        first = cnt[half] == 1
                        last = cnt[half] == 3 * DC
                        w = (t_aof_ev + t_stg2[0] + t_stg2[1] + [t_ao] + t_wp) \
                            if first else ()
                        lastpj = P("tensor", lambda e, lhs=lhs, wx=wx, dc=dc,
                                   half=half, st=first, sp=last: e.matmul(
                                       ps1[:, half * 512:(half + 1) * 512],
                                       lhs,
                                       wx[:, dc * D + half * 512:
                                          dc * D + (half + 1) * 512],
                                       start=st, stop=sp),
                                   waits=w, inc="te" if last else None)
            t_p1 = P("vector", lambda e: e.tensor_tensor(
                x1t[:, :], ps1[:, :], bp_b[:, :], OP.add),
                waits=[lastpj, t_bpb] + t_x1rd.get(g - 1, []), inc="ve")
            t_p1tok[g] = t_p1
            t_p2 = P("vector", lambda e: e.tensor_tensor(
                x1t[:, :], x1t[:, :], xres[:, :], OP.add),
                waits=[t_p1, t_xld[g]], inc="ve")
            t_p2tok[g] = t_p2
            def s2fn(g=g, t_p2=t_p2):
                return P("scalar", lambda e, g=g, t_p2=t_p2: e.activation(
                    sqA[:, 0:D], x1t[:, :], AF.Square,
                    accum_out=sum2_s[:, g:g + 1]),
                    waits=[t_p2, t_s2done[0]], inc="sc")
            pend_s2.append(s2fn)
            t_r2 = P("vector", lambda e, g=g: e.reduce_sum(
                sum2_v[:, g:g + 1], x1t[:, :], axis=AX.X),
                waits=[t_p2], inc="ve")
            t_st = P("gpsimd", lambda e, s=s, i=i: e.dma_start(
                out=x1_scr[(s * S + i * 128):(s * S + (i + 1) * 128), :],
                in_=x1t[:, :]),
                waits=[t_p2], inc="s_sx1", dma=True)
            x1st[g] = t_st
            t_x1rd[g] = [t_r2, t_r2, t_st]
            prefetch_x(g + 1)

        while pend_s2:
            t_s2done[0] = pend_s2.pop(0)()
            t_x1rd[NG - 1][0] = t_s2done[0]

        endB = plan.now("te", "ve", "sc", "s_sx1", "s_lq", "s_lxb")

        t_scmm_last = t_scmm[NG - 1]
        t_avmm_last = t_avmm[NG - 1]
        wd_in = wdT_h.rearrange("(hc p) d -> p hc d", p=128)
        wd_out = wd_sb.rearrange("p (hc d) -> p hc d", hc=HC)
        t_wd = P("gpsimd", lambda e: e.dma_start(out=wd_out, in_=wd_in),
                 waits=[t_scmm_last], inc="g_cv", dma=True)
        t2a = P("vector", lambda e: e.tensor_tensor(
            st2[:, 0:NG], sum2_v[:, :], sum2_v[:, :], OP.mult),
            waits=endB, inc="ve")
        t2b = P("vector", lambda e: e.scalar_tensor_tensor(
            st2[:, NG:2 * NG], st2[:, 0:NG], -1.0 / D, sum2_s[:, :],
            OP.mult, OP.add), waits=[t2a], inc="ve")
        t2c = P("vector", lambda e: e.tensor_scalar(
            st2[:, 2 * NG:3 * NG], st2[:, NG:2 * NG], 1.0 / D, eps,
            OP.mult, OP.add), waits=[t2b], inc="ve")
        t2d = P("scalar", lambda e: e.activation(
            st2_s[:, :], st2[:, 2 * NG:3 * NG], AF.Sqrt),
            waits=[t2c], inc="sc")
        t2e = P("vector", lambda e: e.reciprocal(inv2[:, :], st2_s[:, :]),
                waits=[t2d], inc="ve")
        t2f = P("vector", lambda e: e.scalar_tensor_tensor(
            ngm2[:, :], sum2_v[:, :], -1.0 / D, inv2[:, :], OP.mult, OP.mult),
            waits=[t2e], inc="ve")

        t_wld = {}
        t_wmm = {}
        t_oev = {}
        t_ylast = {}
        t_xn2 = {}
        t_ev2 = {}
        t_hev_pair = []
        t_gu_prev = []
        t_gu_hist = {}
        t_stgC = [t_stg2.get(0, []) + t_stg2.get(1, [])]
        tr2_prev = [None]
        x1_readers = {0: [], 1: []}
        t_x1ld = {}
        x2fh_c = xn2f_hi.rearrange("p (x c t) -> p x c t", x=2, c=DC)
        x2fl_c = xn2f_lo.rearrange("p (x c t) -> p x c t", x=2, c=DC)

        def emit_x1_loads(u):
            if u >= NST or u in t_x1ld:
                return
            parU = u % 2
            t_x1ld[u] = [P("sync", lambda e, u=u, j=j, parU=parU: e.dma_start(
                out=x1A[:, (parU * NSUB + j) * D:(parU * NSUB + j + 1) * D],
                in_=x1_scr[u * TA + j * 128:u * TA + (j + 1) * 128, :]),
                waits=([t_p1tok[NG - 1], x1st[j]] if u == 0 else
                       [t_xn2.get((u - 2, j)), t_oev.get((u - 2, j))]),
                inc="s_lx1", dma=True) for j in range(NSUB)]

        def emit_xn2(u, j):
            parU = u % 2
            col = (u * TA) // 128 + j
            t_xn2[(u, j)] = P("vector", lambda e, j=j, parU=parU, col=col:
                              e.tensor_scalar(
                                  xn2[:, :],
                                  x1A[:, (parU * NSUB + j) * D:
                                      (parU * NSUB + j + 1) * D],
                                  inv2[:, col:col + 1], ngm2[:, col:col + 1],
                                  OP.mult, OP.add),
                              waits=[t_x1ld[u][j], t2f, tr2_prev[0]],
                              inc="ve")

        def emit_xn2_tr(u, j):
            parU = u % 2
            tr = None
            for dc in range(DC):
                tr = P("tensor", lambda e, dc=dc: e.transpose(
                    ps2[:, dc * 128:(dc + 1) * 128],
                    xn2[:, dc * 128:(dc + 1) * 128], ident[:]),
                    waits=([t_xn2[(u, j)]] + t_stgC[0] +
                           ([t_wmm.get((u - 1, HC - 1))] if j == 0 else []))
                    if dc == 0 else (),
                    inc="te" if dc == DC - 1 else None)
            tr2_prev[0] = tr
            th = P("scalar", lambda e, j=j, parU=parU: e.copy(
                x2fh_c[:, parU, :, j * 128:(j + 1) * 128], ps2c[:, :, :]),
                waits=[tr], inc="sc")
            tl = P("vector", lambda e, j=j, parU=parU: e.tensor_tensor(
                x2fl_c[:, parU, :, j * 128:(j + 1) * 128], ps2c[:, :, :],
                x2fh_c[:, parU, :, j * 128:(j + 1) * 128], OP.subtract),
                waits=[tr, th], inc="ve")
            t_ev2[(u, j)] = [th, tl]
            t_stgC[0] = [th, tl]

        emit_x1_loads(0)
        for j in range(NSUB):
            emit_xn2(0, j)
            emit_xn2_tr(0, j)

        for u in range(NST):
            parU = u % 2
            xb2 = parU * DC * TA
            s, t0 = (u * TA) // S, (u * TA) % S
            t_xnf2 = [t_ev2[(u, j)][k] for j in range(NSUB) for k in (0, 1)]
            emit_x1_loads(u + 1)

            t_hev = []
            for hc in range(HC):
                parW = hc % 2
                for look in ([0, 1] if hc == 0 else [1]):
                    hh = hc + look
                    if hh >= HC or (u, hh) in t_wld:
                        continue
                    pw = hh % 2
                    gw = [t_wmm.get((u, hh - 2))]
                    if hh < 2:
                        gw = [t_wmm.get((u - 1, HC - 2 + hh))]
                        if u == 0:
                            gw = [("g_cv", n_cv), t_avmm_last]
                    tg = P("sync", lambda e, hh=hh, pw=pw: e.dma_start(
                        out=wgbuf[:, pw * D:(pw + 1) * D].rearrange(
                            "p (c h) -> p c h", c=DC),
                        in_=wg_scr[hh]),
                        waits=gw, inc="s_lwg", dma=True)
                    tu = P("sync", lambda e, hh=hh, pw=pw: e.dma_start(
                        out=wubuf[:, pw * D:(pw + 1) * D].rearrange(
                            "p (c h) -> p c h", c=DC),
                        in_=wu_scr[hh]),
                        waits=gw, inc="s_lwu", dma=True)
                    t_wld[(u, hh)] = [tg, tu]
                tw = t_wld[(u, hc)]
                gbank = ps0[:, parW * 1024:parW * 1024 + TA]
                ubank = ps0[:, parW * 1024 + 512:parW * 1024 + 512 + TA]
                pairs_g = [(wgbuf[:, parW * D + dc * 128:
                                  parW * D + (dc + 1) * 128],
                            xy[:, xb2 + dc * TA:xb2 + (dc + 1) * TA])
                           for xy in (xn2f_hi, xn2f_lo) for dc in range(DC)]
                pairs_u = [(wubuf[:, parW * D + dc * 128:
                                  parW * D + (dc + 1) * 128],
                            xy[:, xb2 + dc * TA:xb2 + (dc + 1) * TA])
                           for xy in (xn2f_hi, xn2f_lo) for dc in range(DC)]
                ghc = u * HC + hc
                tmm_g = mm_group(gbank, pairs_g,
                                 waits=t_xnf2 + tw + t_gu_hist.get(ghc - 2, []) +
                                 ([t_oev.get((u - 1, 1))] if hc == 0 else []) +
                                 ([t_oev.get((u - 1, 3))]
                                  if hc == 1 and NSUB == 4 else []))
                tmm_u = mm_group(ubank, pairs_u, waits=[])
                t_wmm[(u, hc)] = tmm_u
                t_sg = P("scalar", lambda e, gbank=gbank: e.activation(
                    sgb[:, :], gbank, AF.Silu, scale=1.0 / 127.0),
                    waits=[tmm_g] + ([t_gu_prev[1]] if t_gu_prev else []),
                    inc="sc")
                t_h = P("vector", lambda e, ubank=ubank: e.scalar_tensor_tensor(
                    h32[:, :], ubank, 1.0 / 127.0, sgb[:, :],
                    OP.mult, OP.mult),
                    waits=[tmm_u, t_sg] + t_hev_pair, inc="ve")
                t_hh = P("scalar", lambda e, hc=hc: e.copy(
                    hf_hi[:, hc * TA:(hc + 1) * TA], h32[:, :]),
                    waits=[t_h, t_oev.get((u - 1, NSUB - 1))], inc="sc")
                t_hl = P("vector", lambda e, hc=hc: e.tensor_tensor(
                    hf_lo[:, hc * TA:(hc + 1) * TA], h32[:, :],
                    hf_hi[:, hc * TA:(hc + 1) * TA], OP.subtract),
                    waits=[t_hh], inc="ve")
                t_hev_pair = [t_hh, t_hl]
                t_hev += [t_hh, t_hl]
                t_gu_prev = [t_sg, t_h]
                t_gu_hist[ghc] = [t_sg, t_h]
                if u + 1 < NST and HC >= 12:
                    jj = hc // (HC // NSUB)
                    ph_in = hc % (HC // NSUB)
                    if ph_in == 1:
                        emit_xn2(u + 1, jj)
                    elif ph_in == 4:
                        emit_xn2_tr(u + 1, jj)

            for j in range(NSUB):
                psd = ps1 if j % 2 == 0 else (
                    ps0[:, 0:1024] if j % 4 == 1 else ps0[:, 1024:2048])
                lastdn = None
                cnt = {h: 0 for h in range(NH)}
                for hx in (hf_hi, hf_lo):
                    for hc in range(HC):
                        lhs = hx[:, hc * TA + j * 128:hc * TA + (j + 1) * 128]
                        for half in range(NH):
                            cnt[half] += 1
                            first = cnt[half] == 1
                            last = cnt[half] == 2 * HC
                            w = ()
                            if first:
                                w = t_hev + [t_wd, t_oev.get((u, j - 2))]
                                if j % 2 == 1:
                                    w = w + t_gu_hist[
                                        u * HC + HC - (2 if j % 4 == 1 else 1)]
                            lastdn = P("tensor", lambda e, lhs=lhs, hc=hc,
                                       half=half, psd=psd, st=first, sp=last:
                                       e.matmul(
                                           psd[:, half * 512:(half + 1) * 512],
                                           lhs,
                                           wd_sb[:, hc * D + half * 512:
                                                 hc * D + (half + 1) * 512],
                                           start=st, stop=sp),
                                       waits=w, inc="te" if last else None)
                to = P("vector", lambda e, j=j, parU=parU, psd=psd:
                       e.scalar_tensor_tensor(
                           outA[:, (j % 2) * D:(j % 2 + 1) * D], psd[:, 0:D],
                           1.0 / 127.0,
                           x1A[:, (parU * NSUB + j) * D:
                               (parU * NSUB + j + 1) * D],
                           OP.mult, OP.add),
                       waits=[lastdn, t_ylast.get((u, j - 2)),
                              t_ylast.get((u - 1, j + 2))], inc="ve")
                t_oev[(u, j)] = to
                ty = P("scalar", lambda e, s=s, t0=t0, j=j: e.dma_start(
                    out=y_h[s, t0 + j * 128:t0 + j * 128 + 128, :],
                    in_=outA[:, (j % 2) * D:(j % 2 + 1) * D]),
                    waits=[to], inc="s_sy", dma=True)
                t_ylast[(u, j)] = ty
            for hh in ((0, 1) if u + 1 < NST else ()):
                if hh >= HC or (u + 1, hh) in t_wld:
                    continue
                pw = hh % 2
                gw = [t_wmm.get((u, HC - 2 + hh))]
                tg = P("sync", lambda e, hh=hh, pw=pw: e.dma_start(
                    out=wgbuf[:, pw * D:(pw + 1) * D].rearrange(
                        "p (c h) -> p c h", c=DC),
                    in_=wg_scr[hh]),
                    waits=gw, inc="s_lwg", dma=True)
                tu = P("sync", lambda e, hh=hh, pw=pw: e.dma_start(
                    out=wubuf[:, pw * D:(pw + 1) * D].rearrange(
                        "p (c h) -> p c h", c=DC),
                    in_=wu_scr[hh]),
                    waits=gw, inc="s_lwu", dma=True)
                t_wld[(u + 1, hh)] = [tg, tu]
            if u + 1 < NST and HC < 12:
                for j in range(NSUB):
                    emit_xn2(u + 1, j)
                    emit_xn2_tr(u + 1, j)

        n_y = plan.cnt["s_sy"]
        plan.ops["sync"].append(
            (lambda e: e.nop(), (("s_sy", n_y),), None, 1))

        block = es.enter_context(nc.Block())

        @block.sync
        def _(e):
            plan.play("sync", e)

        @block.gpsimd
        def _(e):
            plan.play("gpsimd", e)

        @block.scalar
        def _(e):
            plan.play("scalar", e)

        @block.vector
        def _(e):
            plan.play("vector", e)

        @block.tensor
        def _(e):
            plan.play("tensor", e)

    return nc


def kernel(L0_attn_x, L0_attn_Wqkv, L0_attn_bqkv, L0_attn_Wproj, L0_attn_bproj,
           L0_attn_gamma, L0_ffn_w_gate, L0_ffn_w_up, L0_ffn_w_down):
    x = np.ascontiguousarray(np.asarray(L0_attn_x, dtype=np.float32))
    Bfull, S, D = x.shape
    ncore = 8
    Bc = Bfull // ncore
    H = L0_ffn_w_gate.shape[0]

    wqkv = np.ascontiguousarray(np.asarray(L0_attn_Wqkv, dtype=np.float32))
    bqkv = np.ascontiguousarray(np.asarray(L0_attn_bqkv, dtype=np.float32))
    wproj = np.ascontiguousarray(np.asarray(L0_attn_Wproj, dtype=np.float32))
    bproj = np.ascontiguousarray(np.asarray(L0_attn_bproj, dtype=np.float32))
    gamma = np.asarray(L0_attn_gamma, dtype=np.float32).reshape(1, 1)
    wgT = np.ascontiguousarray(np.asarray(L0_ffn_w_gate, dtype=np.int32).T)
    wuT = np.ascontiguousarray(np.asarray(L0_ffn_w_up, dtype=np.int32).T)
    wdT = np.ascontiguousarray(np.asarray(L0_ffn_w_down, dtype=np.int32).T)

    nc = build_program(B=Bc, S=S, D=D, H=H)
    in_maps = []
    for c in range(ncore):
        in_maps.append({
            "x": np.ascontiguousarray(x[c * Bc:(c + 1) * Bc]),
            "wqkv": wqkv, "bqkv": bqkv, "wproj": wproj, "bproj": bproj,
            "gamma": gamma, "wgT": wgT, "wuT": wuT, "wdT": wdT,
        })
    res = run_bass_kernel_spmd(nc, in_maps, core_ids=list(range(ncore)))
    out = np.concatenate([r["y"] for r in res.results], axis=0)
    return out.astype(np.float32)
